# version 04
# baseline (speedup 1.0000x reference)
import math
from contextlib import ExitStack

import numpy as np

import concourse.bass as bass
import concourse.mybir as mybir
from concourse.bass_utils import run_bass_kernel_spmd

F32 = mybir.dt.float32
BF16 = mybir.dt.bfloat16
AF = mybir.ActivationFunctionType
ALU = mybir.AluOpType
AX = mybir.AxisListType

D = 1024
NTOK = 2048
DC = 8
NT = 4
T = 512
TT16 = 16
H = 16
DH = 64
PM = 64
DFF = 4096
NEXP = 8
EPS_LN_OUT = 1e-5
EPS_LN_IN = 1e-6
EPS_KERN = 1e-6
RATIO = 0.125
DN = 64.0 ** (-0.25)
SCALE_B = 64.0 ** (-0.5)


def ts(i, n):
    return slice(i * n, (i + 1) * n)


class Sem:
    def __init__(self, stack, nc, name):
        self.h = stack.enter_context(nc.semaphore(name))
        self.n = 0
        self.name = name

    def inc(self, ins, k=1):
        ins.then_inc(self.h, k)
        self.n += k
        return self.n


class Ctx:
    def __init__(self, nc, stack, block):
        self.nc = nc
        self.stack = stack
        self.block = block
        self.sems = {}
        self.last_wait = {}

    def sem(self, name):
        if name not in self.sems:
            self.sems[name] = Sem(self.stack, self.nc, name)
        return self.sems[name]

    def sbuf(self, name, shape, dtype, stack=None):
        st = stack if stack is not None else self.stack
        return st.enter_context(self.nc.sbuf_tensor(name, shape, dtype))

    def psum(self, name, shape, dtype):
        return self.stack.enter_context(self.nc.psum_tensor(name, shape, dtype))

    def wait(self, eng, ekey, sem, val):
        if val <= 0:
            return
        k = (ekey, sem.name)
        if self.last_wait.get(k, -1) >= val:
            return
        eng.wait_ge(sem.h, val)
        self.last_wait[k] = val


class Stream:

    def __init__(self, ctx, name, cols, nstage, nbf, mcols=None):
        self.ctx = ctx
        self.cols = cols
        self.mcols = mcols or cols
        self.nstage = nstage
        self.nbf = nbf
        self.stage = ctx.sbuf(name + "_st", [128, nstage * cols], F32)
        self.bf = ctx.sbuf(name + "_bf", [128, nbf * cols], BF16)
        self.s_ld = [ctx.sem(name + f"_ld{i}") for i in range(nstage)]
        self.s_ct = ctx.sem(name + "_ct")
        self.s_us = ctx.sem(name + "_us")
        self.n_ld = 0
        self.n_ct = 0

    def load(self, sync, ekey, src_ap):
        j = self.n_ld
        self.ctx.wait(sync, ekey, self.s_ct, j - self.nstage + 1)
        dst = self.stage[:, ts(j % self.nstage, self.cols)]
        ins = sync.dma_start(
            out=dst.rearrange("p (c m) -> p c m", m=self.mcols),
            in_=src_ap.rearrange("(c p) m -> p c m", p=128))
        self.s_ld[j % self.nstage].inc(ins, 16)
        self.n_ld += 1

    def cast(self, eng, ekey, scale=None, free_wait=None):
        j = self.n_ct
        self.ctx.wait(eng, ekey, self.s_ld[j % self.nstage],
                      16 * (j // self.nstage + 1))
        if free_wait is not None:
            self.ctx.wait(eng, ekey, free_wait[0], free_wait[1])
        dst = self.bf[:, ts(j % self.nbf, self.cols)]
        src = self.stage[:, ts(j % self.nstage, self.cols)]
        if scale is None:
            ins = eng.tensor_copy(out=dst, in_=src)
        else:
            ins = eng.tensor_scalar_mul(dst, src, float(scale))
        self.s_ct.inc(ins, 1)
        self.n_ct += 1
        return j

    def tile(self, j, col0, w):
        base = (j % self.nbf) * self.cols
        return self.bf[:, base + col0: base + col0 + w]

    def wait_ready(self, eng, ekey, j):
        self.ctx.wait(eng, ekey, self.s_ct, j + 1)

    def release(self, ins):
        self.s_us.inc(ins, 1)


MOE_ELIST = None
ATTNA_STOP = None
ATTNA_KSKIP = set()
SERIAL_A = False
N_PAIRS = 8
FLUSH_TR = False
DBG = False
NOCONST = False
NOLN = False
RAWSTORE = False
FFN_KEYS = ["L0_ffn_w1s", "L0_ffn_w2s", "L0_ffn_w3s", "L0_ffn_W1",
            "L0_ffn_W2", "L0_ffn_W3", "L0_ffn_Wg"]
ATTNB_KEYS = ["L0_attn_b_w_qkv", "L0_attn_b_w_out", "L0_attn_b_b_out"]
ATTNA_KEYS = ["L0_attn_a_Wq", "L0_attn_a_Wk", "L0_attn_a_Wv", "L0_attn_a_Wo",
              "L0_attn_a_proj", "L0_attn_a_W1", "L0_attn_a_b1", "L0_attn_a_W2",
              "L0_attn_a_b2", "L0_attn_a_ln1_g", "L0_attn_a_ln1_b",
              "L0_attn_a_ln2_g", "L0_attn_a_ln2_b"]
INPUTS_FOR = {
    "copy": [],
    "ln": [],
    "ffn1": FFN_KEYS,
    "attnb": FFN_KEYS + ATTNB_KEYS,
    "attna_only": ATTNA_KEYS,
    "full": FFN_KEYS + ATTNB_KEYS + ATTNA_KEYS,
}


def build_nc(upto="full"):
    nc = bass.Bass("TRN2", target_bir_lowering=False, debug=False)
    stack = ExitStack()
    block = stack.enter_context(nc.Block())
    ctx = Ctx(nc, stack, block)

    x_in = nc.dram_tensor("x", [NTOK, D], F32, kind="ExternalInput").ap()
    y_out = nc.dram_tensor("y", [NTOK, D], F32, kind="ExternalOutput").ap()

    def din(name, shape):
        return nc.dram_tensor(name, shape, F32, kind="ExternalInput").ap()

    need_ffn = upto in ("ffn1", "attnb", "full")
    need_attnb = upto in ("attnb", "full")
    need_attna = upto in ("full", "attna_only")
    W = {}
    if need_ffn:
        W["w1s"] = din("L0_ffn_w1s", [D, DFF])
        W["w2s"] = din("L0_ffn_w2s", [D, DFF])
        W["w3s"] = din("L0_ffn_w3s", [DFF, D])
        W["W1"] = din("L0_ffn_W1", [NEXP, D, DFF])
        W["W2"] = din("L0_ffn_W2", [NEXP, D, DFF])
        W["W3"] = din("L0_ffn_W3", [NEXP, DFF, D])
        W["Wg"] = din("L0_ffn_Wg", [D, NEXP])
    if need_attnb:
        W["w_qkv"] = din("L0_attn_b_w_qkv", [D, 3 * D])
        W["w_out"] = din("L0_attn_b_w_out", [D, D])
        W["b_out"] = din("L0_attn_b_b_out", [D])
    if need_attna:
        W["Wq"] = din("L0_attn_a_Wq", [D, H, DH])
        W["Wk"] = din("L0_attn_a_Wk", [D, H, DH])
        W["Wv"] = din("L0_attn_a_Wv", [D, H, DH])
        W["Wo"] = din("L0_attn_a_Wo", [H, DH, D])
        W["proj"] = din("L0_attn_a_proj", [PM, DH])
        W["aW1"] = din("L0_attn_a_W1", [D, DFF])
        W["ab1"] = din("L0_attn_a_b1", [DFF])
        W["aW2"] = din("L0_attn_a_W2", [DFF, D])
        W["ab2"] = din("L0_attn_a_b2", [D])
        W["ln1_g"] = din("L0_attn_a_ln1_g", [D])
        W["ln1_b"] = din("L0_attn_a_ln1_b", [D])
        W["ln2_g"] = din("L0_attn_a_ln2_g", [D])
        W["ln2_b"] = din("L0_attn_a_ln2_b", [D])
        cc_in = nc.dram_tensor("cc_in", [1, 1], F32).ap()
        cc_out = nc.dram_tensor("cc_out", [1, 1], F32).ap()

    x_sb = ctx.sbuf("x_sb", [128, DC * NTOK], F32)
    xh = ctx.sbuf("xh", [128, DC * NTOK], BF16)
    id_f32 = ctx.sbuf("id_f32", [128, 128], F32)
    id_bf = ctx.sbuf("id_bf", [128, 128], BF16)
    ones_f32_r = ctx.sbuf("ones_f32_r", [1, 128], F32)
    ones_bf_c = ctx.sbuf("ones_bf_c", [128, 1], BF16)
    row_a = ctx.sbuf("row_a", [1, T], F32)
    row_b = ctx.sbuf("row_b", [1, T], F32)
    row_c = ctx.sbuf("row_c", [1, T], F32)
    xb_t = ctx.sbuf("xb_t", [128, 2 * T], BF16)
    sq_t = ctx.sbuf("sq_t", [128, 2 * T], BF16)
    tf32a = ctx.sbuf("tf32a", [128, T], F32)
    tf32b = ctx.sbuf("tf32b", [128, T], F32)
    junk = ctx.sbuf("junk", [128, 8], F32)
    cst = ctx.sbuf("cst", [128, 4], F32)
    iotile = ctx.sbuf("iotile", [128, 2 * D], F32)

    pA = ctx.psum("pA", [128, T], F32)
    pB = ctx.psum("pB", [128, T], F32)
    pC = ctx.psum("pC", [128, T], F32)
    pD = ctx.psum("pD", [128, T], F32)
    pE = ctx.psum("pE", [128, T], F32)
    pF = ctx.psum("pF", [128, T], F32)
    pG = ctx.psum("pG", [128, T], F32)
    pI = ctx.psum("pI", [128, 512], F32)

    s_dma = ctx.sem("s_dma")
    s_in0 = ctx.sem("s_in0")
    s_in1 = ctx.sem("s_in1")
    s_st0 = ctx.sem("s_st0")
    s_st1 = ctx.sem("s_st1")
    s_pe = ctx.sem("s_pe")
    s_dve = ctx.sem("s_dve")
    s_act = ctx.sem("s_act")
    s_gp = ctx.sem("s_gp")
    s_out = ctx.sem("s_out")
    s_odv = ctx.sem("s_odv")
    s_oac = ctx.sem("s_oac")
    s_ogp = ctx.sem("s_ogp")

    def oedge(eng, sem, ins):
        sem.inc(ins)
        eng.wait_ge(sem.h, sem.n)
    s_amm = ctx.sem("s_amm")
    s_bmm = ctx.sem("s_bmm")
    s_omm = ctx.sem("s_omm")
    s_ptr = ctx.sem("s_ptr")
    s_gelu = ctx.sem("s_gelu")
    s_bev = ctx.sem("s_bev")
    s_hm = ctx.sem("s_hm")
    s_oad = ctx.sem("s_oad")
    s_mrow = ctx.sem("s_mrow")
    s_mb = ctx.sem("s_mb")
    s_rmm = ctx.sem("s_rmm")


    def tr_flush(tensor):
        if not FLUSH_TR:
            return None
        tensor.transpose(pI[0:128, 0:128], id_f32[:, :], id_f32[:, :])
        return tensor.transpose(pI[0:128, 0:128], id_f32[:, :], id_f32[:, :])

    def gp_consts(gp):
        ins = gp.memset(id_f32[:, :], 0.0)
        s_gp.inc(ins)
        ins = gp.memset(id_bf[:, :], 0.0)
        s_gp.inc(ins)
        gp.memset(ones_f32_r[:, :], 1.0)
        gp.memset(cst[:, 0:1], EPS_LN_OUT)
        gp.memset(cst[:, 1:2], EPS_LN_IN)
        gp.memset(cst[:, 2:3], math.pi)
        gp.memset(ones_bf_c[:, :], 1.0)
        gp.wait_ge(s_gp.h, s_gp.n - 1)
        gp.affine_select(out=id_f32[:, :], in_=id_f32[:, :],
                         compare_op=ALU.not_equal, fill=1.0,
                         base=0, pattern=[[-1, 128]], channel_multiplier=1)
        gp.wait_ge(s_gp.h, s_gp.n)
        ins = gp.affine_select(out=id_bf[:, :], in_=id_bf[:, :],
                               compare_op=ALU.not_equal, fill=1.0,
                               base=0, pattern=[[-1, 128]],
                               channel_multiplier=1)
        s_gp.inc(ins)

    block.gpsimd(gp_consts)
    GP_CONSTS = s_gp.n

    pe_in0 = s_pe.n
    dve_in0 = s_dve.n

    def in_dma(sync):
        for t in range(TT16):
            ctx.wait(sync, "sync", s_pe, pe_in0 + (t - 1) * 2)
            ins = sync.dma_start(out=iotile[:, ts(t % 2, D)],
                                 in_=x_in[ts(t, 128), :])
            (s_in0 if t % 2 == 0 else s_in1).inc(ins, 16)

    block.sync(in_dma)

    def in_tr(tensor):
        ctx.wait(tensor, "pe", s_gp, GP_CONSTS)
        for t in range(TT16):
            ctx.wait(tensor, "pe", s_in0 if t % 2 == 0 else s_in1,
                     16 * (t // 2 + 1))
            for c4 in range(2):
                g = t * 2 + c4
                bank = pG if g % 2 == 0 else pF
                ctx.wait(tensor, "pe", s_dve, dve_in0 + g - 1)
                for c in range(c4 * 4, c4 * 4 + 4):
                    ins = tensor.transpose(
                        bank[0:128, ts(c % 4, 128)],
                        iotile[:, (t % 2) * D + c * 128: (t % 2) * D + (c + 1) * 128],
                        id_f32[:, :])
                s_pe.inc(tr_flush(tensor) or ins)

    block.tensor(in_tr)

    def in_ev(vector):
        for t in range(TT16):
            for c4 in range(2):
                g = t * 2 + c4
                bank = pG if g % 2 == 0 else pF
                ctx.wait(vector, "dve", s_pe, pe_in0 + g + 1)
                for c in range(c4 * 4, c4 * 4 + 4):
                    ins = vector.tensor_copy(
                        out=x_sb[:, c * NTOK + t * 128: c * NTOK + (t + 1) * 128],
                        in_=bank[0:128, ts(c % 4, 128)])
                s_dve.inc(ins)

    block.vector(in_ev)

    X_READY_DVE = s_dve.n

    def emit_ln(src, dst, eps, gate_sem, gate_val, g_cols=None, b_cols=None,
                add_to=None):
        inv_d = 1.0 / D
        act_cast = {}
        dve_sq = {}
        pe_stats = {}
        dve_rows = {}
        act_rows = {}
        pe_bcast = {}
        dve_apply = {}
        pe0 = s_pe.n
        def slotfree(i):
            return pe0 + (i // DC) * (DC + 1) + (i % DC) + 1

        for nt in range(NT):
            def b_sc(scalar, nt=nt):
                ctx.wait(scalar, "act", gate_sem, gate_val)
                for c in range(DC):
                    i = nt * DC + c
                    sl = i % 2
                    if i >= 2:
                        ctx.wait(scalar, "act", s_pe, slotfree(i - 2))
                    ins = scalar.copy(
                        out=xb_t[:, ts(sl, T)],
                        in_=src[:, c * NTOK + nt * T: c * NTOK + (nt + 1) * T])
                    act_cast[i] = s_act.inc(ins)
            block.scalar(b_sc)

            def b_vsq(vector, nt=nt):
                ctx.wait(vector, "dve", gate_sem, gate_val)
                for c in range(DC):
                    i = nt * DC + c
                    sl = i % 2
                    ctx.wait(vector, "dve", s_act, act_cast[i])
                    if i >= 2:
                        ctx.wait(vector, "dve", s_pe, slotfree(i - 2))
                    ins = vector.tensor_mul(sq_t[:, ts(sl, T)],
                                            xb_t[:, ts(sl, T)], xb_t[:, ts(sl, T)])
                    dve_sq[i] = s_dve.inc(ins)
            block.vector(b_vsq)

            def b_pst(tensor, nt=nt):
                if nt >= 1:
                    ctx.wait(tensor, "pe", s_dve, dve_rows[nt - 1])
                for c in range(DC):
                    i = nt * DC + c
                    sl = i % 2
                    ctx.wait(tensor, "pe", s_dve, dve_sq[i])
                    tensor.matmul(pE[0:1, :], ones_bf_c[:, :], xb_t[:, ts(sl, T)],
                                  start=(c == 0), stop=(c == DC - 1))
                    ins = tensor.matmul(pF[0:1, :], ones_bf_c[:, :], sq_t[:, ts(sl, T)],
                                        start=(c == 0), stop=(c == DC - 1))
                    assert s_pe.inc(ins) == slotfree(i)
                pe_stats[nt] = s_pe.n
            block.tensor(b_pst)

            def b_vrow(vector, nt=nt):
                ctx.wait(vector, "dve", s_pe, pe_stats[nt])
                ins = vector.tensor_scalar_mul(row_a[:, :], pE[0:1, :], inv_d)
                vector.tensor_scalar_mul(row_c[:, :], pF[0:1, :], inv_d)
                oedge(vector, s_odv, ins)
                ins = vector.tensor_mul(row_b[:, :], row_a[:, :], row_a[:, :])
                oedge(vector, s_odv, ins)
                ins = vector.tensor_sub(row_c[:, :], row_c[:, :], row_b[:, :])
                dve_rows[nt] = s_dve.inc(ins)
            block.vector(b_vrow)

            def b_arow(scalar, nt=nt):
                ctx.wait(scalar, "act", s_dve, dve_rows[nt])
                ins = scalar.activation(out=row_c[:, :], in_=row_c[:, :],
                                  func=AF.Ln,
                                  bias=cst[0:1, 0:1] if eps == EPS_LN_OUT else cst[0:1, 1:2],
                                  scale=1.0)
                oedge(scalar, s_oac, ins)
                ins = scalar.activation(out=row_b[:, :], in_=row_c[:, :],
                                        func=AF.Exp, bias=0.0, scale=-0.5)
                act_rows[nt] = s_act.inc(ins)
            block.scalar(b_arow)

            def b_pbc(tensor, nt=nt):
                ctx.wait(tensor, "pe", s_act, act_rows[nt])
                if nt >= 1:
                    ctx.wait(tensor, "pe", s_dve, dve_apply[nt - 1])
                tensor.matmul(pC[:, :], ones_f32_r[:, :], row_a[:, :],
                              start=True, stop=True)
                ins = tensor.matmul(pD[:, :], ones_f32_r[:, :], row_b[:, :],
                                    start=True, stop=True)
                pe_bcast[nt] = s_pe.inc(ins)
            block.tensor(b_pbc)

            def b_vap(vector, nt=nt):
                ctx.wait(vector, "dve", s_pe, pe_bcast[nt])
                for k in range(DC // 2):
                    c0, c1 = 2 * k, 2 * k + 1
                    sl = {c0: tf32a, c1: tf32b}
                    dsl = {c: dst[:, c * NTOK + nt * T:
                                  c * NTOK + (nt + 1) * T] for c in (c0, c1)}
                    for c in (c0, c1):
                        src_sl = src[:, c * NTOK + nt * T:
                                     c * NTOK + (nt + 1) * T]
                        ins = vector.tensor_sub(sl[c][:, :], src_sl, pC[:, :])
                    oedge(vector, s_odv, ins)
                    if g_cols is None:
                        for c in (c0, c1):
                            ins = vector.tensor_mul(dsl[c], sl[c][:, :],
                                                    pD[:, :])
                        if k < DC // 2 - 1:
                            oedge(vector, s_odv, ins)
                    else:
                        for c in (c0, c1):
                            ins = vector.tensor_mul(sl[c][:, :], sl[c][:, :],
                                                    pD[:, :])
                        oedge(vector, s_odv, ins)
                        if add_to is None:
                            for c in (c0, c1):
                                ins = vector.tensor_scalar(
                                    out=dsl[c], in0=sl[c][:, :],
                                    scalar1=g_cols[:, c: c + 1],
                                    scalar2=b_cols[:, c: c + 1],
                                    op0=ALU.mult, op1=ALU.add)
                            if k < DC // 2 - 1:
                                oedge(vector, s_odv, ins)
                        else:
                            for c in (c0, c1):
                                ins = vector.tensor_scalar(
                                    out=sl[c][:, :], in0=sl[c][:, :],
                                    scalar1=g_cols[:, c: c + 1],
                                    scalar2=b_cols[:, c: c + 1],
                                    op0=ALU.mult, op1=ALU.add)
                            oedge(vector, s_odv, ins)
                            for c in (c0, c1):
                                at = add_to[:, c * NTOK + nt * T:
                                            c * NTOK + (nt + 1) * T]
                                ins = vector.tensor_add(at, at, sl[c][:, :])
                            if k < DC // 2 - 1:
                                oedge(vector, s_odv, ins)
                dve_apply[nt] = s_dve.inc(ins)
            block.vector(b_vap)

        return s_dve.n

    moe_res = {}

    moe_call_no = [0]

    def emit_moe(xh_gate_dve, elist=None, prev_w12=None, prev_w3=None):
        if elist is None:
            elist = list(range(NEXP + 1))
        nel = len(elist)
        ci = moe_call_no[0]
        moe_call_no[0] += 1
        lst = ExitStack()
        if "w12" not in moe_res:
            moe_res["w12"] = Stream(ctx, "stw12", 1024, 2, 2, mcols=128)
            moe_res["w3"] = Stream(ctx, "stw3", 1024, 2, 2, mcols=128)
            moe_res["hgrp"] = ctx.sbuf("hgrp", [128, 8 * NTOK], BF16)
        if "wg_bf" not in moe_res:
            moe_res["wg_bf"] = ctx.sbuf("wg_bf", [128, NEXP * DC], BF16)
        stw12 = moe_res["w12"]
        stw3 = moe_res["w3"]
        wg_bf = moe_res["wg_bf"]
        hgrp = moe_res["hgrp"]
        wg_st = ctx.sbuf(f"wg_st{ci}", [128, NEXP * DC], F32, stack=lst)
        mask_cols = ctx.sbuf(f"mask_cols{ci}", [128, TT16 * NEXP], F32, stack=lst)
        mcol = ctx.sbuf(f"mcol{ci}", [128, 64], F32, stack=lst)
        maskb = ctx.sbuf(f"maskb{ci}", [128, NT * T], F32, stack=lst)
        ga_ring = ctx.sbuf(f"ga{ci}", [128, 3 * T], BF16, stack=lst)
        bb_ring = ctx.sbuf(f"bb{ci}", [128, 3 * T], BF16, stack=lst)

        s_wg = ctx.sem("s_wg")

        def wg_dma(sync):
            ins = sync.dma_start(
                out=wg_st[:, :].rearrange("p (c e) -> p c e", e=NEXP),
                in_=W["Wg"].rearrange("(c p) e -> p c e", p=NEXP if False else 128))
            s_wg.inc(ins, 16)
        block.sync(wg_dma)
        wg_dma_n = s_wg.n

        def wg_cast(gp):
            ctx.wait(gp, "gp", s_wg, wg_dma_n)
            ins = gp.tensor_copy(out=wg_bf[:, :], in_=wg_st[:, :])
            s_gp.inc(ins)
        block.gpsimd(wg_cast)
        wg_ready = s_gp.n

        def r_pe(tensor):
            ctx.wait(tensor, "pe", s_dve, xh_gate_dve)
            ctx.wait(tensor, "pe", s_gp, wg_ready)
            for t in range(TT16):
                for c in range(DC):
                    ins = tensor.matmul(
                        pE[0:128, t * NEXP:(t + 1) * NEXP],
                        xh[:, c * NTOK + t * 128: c * NTOK + (t + 1) * 128],
                        wg_bf[:, c * NEXP:(c + 1) * NEXP],
                        start=(c == 0), stop=(c == DC - 1))
                s_rmm.inc(ins)
        block.tensor(r_pe)
        rmm_end = s_rmm.n

        def r_dve(vector):
            ctx.wait(vector, "dve", s_rmm, rmm_end)
            for t in range(TT16):
                vector.memset(mcol[:, 16 + t: 17 + t], 0.0)
            for t in range(TT16):
                ins = vector.reduce_max(
                    mcol[:, t: t + 1], pE[0:128, t * NEXP:(t + 1) * NEXP],
                    axis=AX.X)
            oedge(vector, s_odv, ins)
            for e in range(NEXP):
                for t in range(TT16):
                    ins = vector.tensor_tensor(
                        out=mcol[:, 32 + t: 33 + t],
                        in0=pE[0:128, t * NEXP + e: t * NEXP + e + 1],
                        in1=mcol[:, t: t + 1], op=ALU.is_equal)
                oedge(vector, s_odv, ins)
                for t in range(TT16):
                    ins = vector.tensor_mul(mcol[:, 48 + t: 49 + t],
                                            mcol[:, 32 + t: 33 + t],
                                            mcol[:, 16 + t: 17 + t])
                oedge(vector, s_odv, ins)
                for t in range(TT16):
                    ins = vector.tensor_sub(
                        mask_cols[:, t * NEXP + e: t * NEXP + e + 1],
                        mcol[:, 32 + t: 33 + t], mcol[:, 48 + t: 49 + t])
                oedge(vector, s_odv, ins)
                for t in range(TT16):
                    ins = vector.tensor_add(
                        mcol[:, 16 + t: 17 + t], mcol[:, 16 + t: 17 + t],
                        mask_cols[:, t * NEXP + e: t * NEXP + e + 1])
                if e < NEXP - 1:
                    oedge(vector, s_odv, ins)
                else:
                    s_dve.inc(ins)
        block.vector(r_dve)
        routing_dve_end = s_dve.n

        def srcs(e):
            if e == 0:
                return W["w1s"], W["w2s"], W["w3s"]
            return (W["W1"][e - 1], W["W2"][e - 1], W["W3"][e - 1])

        amm0 = s_amm.n
        bmm0 = s_bmm.n
        omm0 = s_omm.n
        gelu0 = s_gelu.n
        bev0 = s_bev.n
        hm0 = s_hm.n
        oad0 = s_oad.n
        w12_call0 = stw12.n_ld
        w3_call0 = stw3.n_ld
        w12_ct0 = stw12.n_ct
        w3_ct0 = stw3.n_ct

        def w12_free_wait(jj):
            if jj < 0:
                return prev_w12
            ei, r = divmod(jj, 64)
            ic, isw2 = divmod(r, 2)
            grp = ei * 128 + ic * 4 + 4
            return (s_bmm, bmm0 + grp) if isw2 else (s_amm, amm0 + grp)

        def w3_free_wait(jj):
            if jj < 0:
                return prev_w3
            ei, r = divmod(jj, 32)
            g, dc = divmod(r, 8)
            return (s_omm, omm0 + ei * 128 + g * 32 + dc * 4 + 4)

        def loads(sync):
            for e in elist:
                w1, w2, w3 = srcs(e)
                for g in range(4):
                    for ic in range(g * 8, g * 8 + 8):
                        stw12.load(sync, "sync", w1[:, ts(ic, 128)])
                        stw12.load(sync, "sync", w2[:, ts(ic, 128)])
                    for dc in range(DC):
                        stw3.load(sync, "sync", w3[ts(g, 1024), ts(dc, 128)])
        block.sync(loads)

        def casts(gp):
            for e in elist:
                for g in range(4):
                    for ic in range(8):
                        for _ in range(2):
                            jj = stw12.n_ct - w12_ct0
                            stw12.cast(gp, "gp",
                                       free_wait=w12_free_wait(jj - stw12.nbf))
                    for dc in range(DC):
                        jj = stw3.n_ct - w3_ct0
                        stw3.cast(gp, "gp",
                                  free_wait=w3_free_wait(jj - stw3.nbf))
        block.gpsimd(casts)

        for ei, e in enumerate(elist):
            w12_j0 = w12_call0 + ei * 64
            w3_j0 = w3_call0 + ei * 32
            ptr0 = s_ptr.n
            mrow0 = s_mrow.n
            mb0 = s_mb.n

            def pe_body(tensor, e=e, w12_j0=w12_j0, w3_j0=w3_j0,
                        ptr0=ptr0, mrow0=mrow0, mb0=mb0):
                ctx.wait(tensor, "pe", s_dve, xh_gate_dve)
                if e > 0:
                    ctx.wait(tensor, "pe", s_dve, routing_dve_end)
                    for nt in range(NT):
                        ctx.wait(tensor, "pe", s_mb, mb0 + nt)
                        for i, t in enumerate(range(nt * 4, nt * 4 + 4)):
                            col = t * NEXP + (e - 1)
                            ins = tensor.transpose(
                                pG[0:1, ts(i, 128)],
                                mask_cols[:, col: col + 1],
                                id_f32[:, :])
                        s_ptr.inc(tr_flush(tensor) or ins)
                        ctx.wait(tensor, "pe", s_mrow, mrow0 + nt + 1)
                        ins = tensor.matmul(pG[:, :], ones_f32_r[:, :],
                                            row_a[0:1, :], start=True, stop=True)
                        s_ptr.inc(ins)
                for g in range(4):
                    for icl in range(8):
                        ic = g * 8 + icl
                        jw1 = w12_j0 + ic * 2
                        jw2 = jw1 + 1
                        stw12.wait_ready(tensor, "pe", jw1)
                        stw12.wait_ready(tensor, "pe", jw2)
                        for nt in range(NT):
                            ga_idx = s_amm.n - amm0
                            bank = pA if ga_idx % 2 == 0 else pB
                            ctx.wait(tensor, "pe", s_gelu, gelu0 + ga_idx - 1)
                            for c in range(DC):
                                ins = tensor.matmul(
                                    bank[:, :],
                                    stw12.tile(jw1, c * 128, 128),
                                    xh[:, c * NTOK + nt * T: c * NTOK + (nt + 1) * T],
                                    start=(c == 0), stop=(c == DC - 1))
                            s_amm.inc(ins)
                            gb_idx = s_bmm.n - bmm0
                            bankb = pC if gb_idx % 2 == 0 else pD
                            ctx.wait(tensor, "pe", s_bev, bev0 + gb_idx - 1)
                            for c in range(DC):
                                ins = tensor.matmul(
                                    bankb[:, :],
                                    stw12.tile(jw2, c * 128, 128),
                                    xh[:, c * NTOK + nt * T: c * NTOK + (nt + 1) * T],
                                    start=(c == 0), stop=(c == DC - 1))
                            s_bmm.inc(ins)
                    ctx.wait(tensor, "pe", s_hm, hm0 + ei * 128 + (g + 1) * 32)
                    for dc in range(DC):
                        j3 = w3_j0 + g * 8 + dc
                        stw3.wait_ready(tensor, "pe", j3)
                        for nt in range(NT):
                            oidx = s_omm.n - omm0
                            bank = pE if oidx % 2 == 0 else pF
                            if oidx < 2:
                                ctx.wait(tensor, "pe", s_dve, routing_dve_end)
                            ctx.wait(tensor, "pe", s_oad, oad0 + oidx - 1)
                            for icl in range(8):
                                ins = tensor.matmul(
                                    bank[:, :],
                                    stw3.tile(j3, icl * 128, 128),
                                    hgrp[:, icl * NTOK + nt * T: icl * NTOK + (nt + 1) * T],
                                    start=(icl == 0), stop=(icl == 7))
                            s_omm.inc(ins)
            block.tensor(pe_body)

            def sc_body(scalar, e=e):
                for g in range(4):
                    for icl in range(8):
                        for nt in range(NT):
                            gi = s_gelu.n - gelu0
                            bank = pA if gi % 2 == 0 else pB
                            ctx.wait(scalar, "act", s_amm, amm0 + gi + 1)
                            ctx.wait(scalar, "act", s_hm, hm0 + gi - 2)
                            ins = scalar.activation(
                                out=ga_ring[:, ts(gi % 3, T)], in_=bank[:, :],
                                func=AF.Gelu, bias=0.0, scale=1.0)
                            s_gelu.inc(ins)
            block.scalar(sc_body)

            def ve_body(vector, e=e, ptr0=ptr0, mrow0=mrow0, mb0=mb0):
                if e > 0:
                    for nt in range(NT):
                        ctx.wait(vector, "dve", s_ptr, ptr0 + 2 * nt + 1)
                        ins = vector.tensor_copy(out=row_a[0:1, :],
                                                 in_=pG[0:1, 0:T])
                        s_mrow.inc(ins)
                        ctx.wait(vector, "dve", s_ptr, ptr0 + 2 * nt + 2)
                        ins = vector.tensor_copy(out=maskb[:, ts(nt, T)],
                                                 in_=pG[:, :])
                        s_mb.inc(ins)
                items = [(g, icl, nt) for g in range(4) for icl in range(8)
                         for nt in range(NT)]

                def emit_bev(i):
                    g, icl, nt = items[i]
                    bi = s_bev.n - bev0
                    bankb = pC if bi % 2 == 0 else pD
                    ctx.wait(vector, "dve", s_bmm, bmm0 + bi + 1)
                    ctx.wait(vector, "dve", s_hm, hm0 + bi - 2)
                    if e > 0:
                        ins = vector.tensor_mul(
                            bb_ring[:, ts(bi % 3, T)], bankb[:, :],
                            maskb[:, ts(nt, T)])
                    else:
                        ins = vector.tensor_copy(
                            out=bb_ring[:, ts(bi % 3, T)], in_=bankb[:, :])
                    s_bev.inc(ins)

                def emit_hm(i):
                    g, icl, nt = items[i]
                    hi = s_hm.n - hm0
                    if hi % 32 == 0:
                        ctx.wait(vector, "dve", s_omm, omm0 + (hi // 32) * 32)
                    ctx.wait(vector, "dve", s_gelu, gelu0 + hi + 1)
                    ctx.wait(vector, "dve", s_bev, bev0 + hi + 1)
                    ins = vector.tensor_mul(
                        hgrp[:, icl * NTOK + nt * T: icl * NTOK + (nt + 1) * T],
                        ga_ring[:, ts(hi % 3, T)], bb_ring[:, ts(hi % 3, T)])
                    s_hm.inc(ins)

                hm_next = [0]

                def emit_hm_upto(lim):
                    while hm_next[0] < lim:
                        emit_hm(hm_next[0])
                        hm_next[0] += 1

                for g in range(4):
                    for icl in range(8):
                        for nt in range(NT):
                            i = g * 32 + icl * 4 + nt
                            emit_bev(i)
                            emit_hm_upto(i)
                    emit_hm_upto((g + 1) * 32)
                    for dc in range(DC):
                        for nt in range(NT):
                            oi = s_oad.n - oad0
                            bank = pE if oi % 2 == 0 else pF
                            ctx.wait(vector, "dve", s_omm, omm0 + oi + 1)
                            xsl = x_sb[:, dc * NTOK + nt * T: dc * NTOK + (nt + 1) * T]
                            ins = vector.tensor_add(xsl, xsl, bank[:, :])
                            s_oad.inc(ins)
            block.vector(ve_body)

        lst.close()
        return s_oad.n

    def emit_attna(ln_gate_dve):
        if "w12" not in moe_res:
            moe_res["w12"] = Stream(ctx, "stw12", 1024, 2, 2, mcols=128)
            moe_res["w3"] = Stream(ctx, "stw3", 1024, 2, 2, mcols=128)
            moe_res["hgrp"] = ctx.sbuf("hgrp", [128, 8 * NTOK], BF16)
        stw12 = moe_res["w12"]
        stw3 = moe_res["w3"]
        hgrp = moe_res["hgrp"]

        lst = ExitStack()
        qa = ctx.sbuf("qa", [128, NTOK], BF16, stack=lst)
        ka = ctx.sbuf("ka", [128, NTOK], BF16, stack=lst)
        va_bf = ctx.sbuf("va_bf", [128, NTOK], BF16, stack=lst)
        ksq = ctx.sbuf("ksq", [128, NTOK], BF16, stack=lst)
        vtk = ctx.sbuf("vtk", [128, 16 * 129], BF16, stack=lst)
        eftile = ctx.sbuf("eftile", [128, 2 * 128], BF16, stack=lst)
        kvu = ctx.sbuf("kvu", [128, 8 * 129], F32, stack=lst)
        kveps = ctx.sbuf("kveps", [128, 8 * 129], F32, stack=lst)
        blkp = ctx.sbuf("blkp", [128, 128], BF16, stack=lst)
        hhalf = ctx.sbuf("hhalf", [128, 2], BF16, stack=lst)

        kvblk = ctx.sbuf("kvblk", [128, 128], BF16, stack=lst)
        sblk = ctx.sbuf("sblk", [128, 2], BF16, stack=lst)
        kvf = ctx.sbuf("kvf", [128, 129], F32, stack=lst)
        arb_s = ctx.sbuf("arb_s", [128, T], BF16, stack=lst)
        aav_s = ctx.sbuf("aav_s", [128, T], BF16, stack=lst)
        mxc = ctx.sbuf("mxc", [128, 4], F32, stack=lst)
        emxc = ctx.sbuf("emxc", [128, 1], F32, stack=lst)
        bcols = ctx.sbuf("bcols", [128, 4], F32, stack=lst)
        b1c = ctx.sbuf("b1c", [128, 32], F32, stack=lst)
        b2c = ctx.sbuf("b2c", [128, 8], F32, stack=lst)
        g1c = ctx.sbuf("g1c", [128, 8], F32, stack=lst)
        be1c = ctx.sbuf("be1c", [128, 8], F32, stack=lst)
        g2c = ctx.sbuf("g2c", [128, 8], F32, stack=lst)
        be2c = ctx.sbuf("be2c", [128, 8], F32, stack=lst)
        blkscr = nc.dram_tensor("blkscr", [128, 128], F32).ap()
        prst = tf32b[0:64, 0:64]
        prT = tf32b[0:64, 64:128]

        s_bp = ctx.sem("s_bp")
        s_bqe = ctx.sem("s_bqe")
        s_bvt = ctx.sem("s_bvt")
        s_bvk = ctx.sem("s_bvk")
        s_bsc = ctx.sem("s_bsc")
        s_adg = ctx.sem("s_adg")
        s_bxp = ctx.sem("s_bxp")
        s_bav = ctx.sem("s_bav")
        s_brc = ctx.sem("s_brc")
        s_amx = ctx.sem("s_amx")
        s_ptr = ctx.sem("s_ptr")
        s_brb = ctx.sem("s_brb")
        s_brbe = ctx.sem("s_brbe")
        s_bat = ctx.sem("s_bat")
        s_bas = ctx.sem("s_bas")
        s_akf = ctx.sem("s_akf")
        s_aep = ctx.sem("s_aep")
        s_arm = ctx.sem("s_arm")
        s_aqv = ctx.sem("s_aqv")
        s_blk = ctx.sem("s_blk")
        s_cc = ctx.sem("s_cc")
        s_ccd = ctx.sem("s_ccd")
        s_ccd2 = ctx.sem("s_ccd2")
        s_abc = ctx.sem("s_abc")
        s_acv = ctx.sem("s_acv")
        s_ace = ctx.sem("s_ace")
        s_osub = ctx.sem("s_osub")

        bp0 = s_bp.n
        bqe0 = s_bqe.n
        bvt0 = s_bvt.n
        bvk0 = s_bvk.n
        bsc0 = s_bsc.n
        adg0 = s_adg.n
        bxp0 = s_bxp.n
        bav0 = s_bav.n
        brc0 = s_brc.n
        amx0 = s_amx.n
        ptr0 = s_ptr.n
        brb0 = s_brb.n
        brbe0 = s_brbe.n
        bat0 = s_bat.n
        bas0 = s_bas.n
        akf0 = s_akf.n
        aep0 = s_aep.n
        arm0 = s_arm.n
        aqv0 = s_aqv.n
        LNR = math.log(RATIO)
        REPS = RATIO * EPS_KERN
        LVL = {"C": 0, "P": 1, "V": 2, "K": 3}.get(ATTNA_STOP, 9)


        def ca_gp(gp):
            ctx.wait(gp, "gp", s_dve, ln_gate_dve)
            if NOCONST:
                ins = gp.memset(hhalf[:, :], 0.0)
                s_gp.inc(ins)
                return
            ins = gp.memset(hhalf[:, :], 0.0)
            gp.memset(tf32a[:, 0:128], 0.0)
            oedge(gp, s_ogp, ins)
            gp.memset(hhalf[0:64, 0:1], 0.5)
            ins = gp.memset(hhalf[64:128, 1:2], 0.5)
            s_gp.inc(ins)
        block.gpsimd(ca_gp)
        gp_ca = s_gp.n

        def ca_dma1(sync):
            if NOCONST:
                return
            ctx.wait(sync, "sync", s_gp, gp_ca)
            sync.dma_start(out=blkscr, in_=tf32a[:, 0:128]).then_inc(s_blk.h, 16)
            s_blk.n += 16
            ins = sync.dma_start(out=prst, in_=W["proj"])
            s_blk.inc(ins, 16)
        block.sync(ca_dma1)

        def ca_pe1(tensor):
            if NOCONST:
                return
            ctx.wait(tensor, "pe", s_blk, s_blk.n)
            ctx.wait(tensor, "pe", s_dve, ln_gate_dve)
            ins = tensor.transpose(pC[0:64, 0:64], prst, id_f32[0:64, 0:64])
            s_ptr.inc(ins)
        block.tensor(ca_pe1)
        ptr_ca = s_ptr.n

        def ca_ve1(vector):
            if NOCONST:
                return
            ctx.wait(vector, "dve", s_ptr, ptr_ca)
            ins = vector.tensor_copy(out=prT, in_=pC[0:64, 0:64])
            s_dve.inc(ins)
        block.vector(ca_ve1)
        dve_ca = s_dve.n

        def ca_dma2(sync):
            if NOCONST:
                return
            ctx.wait(sync, "sync", s_dve, dve_ca)
            sync.dma_start(out=blkscr[0:64, 0:64],
                           in_=prT).then_inc(s_blk.h, 16)
            s_blk.n += 16
            sync.wait_ge(s_blk.h, s_blk.n)
            sync.dma_start(out=blkscr[64:128, 64:128],
                           in_=prT).then_inc(s_blk.h, 16)
            s_blk.n += 16
            sync.wait_ge(s_blk.h, s_blk.n)
            ins = sync.dma_start(out=tf32a[:, 0:128], in_=blkscr)
            s_blk.inc(ins, 16)
        block.sync(ca_dma2)

        def ca_gp2(gp):
            if NOCONST:
                ins = gp.memset(blkp[:, :], 0.0)
                s_gp.inc(ins)
                return
            ctx.wait(gp, "gp", s_blk, s_blk.n)
            ins = gp.tensor_copy(out=blkp[:, :], in_=tf32a[:, 0:128])
            s_gp.inc(ins)
        block.gpsimd(ca_gp2)
        gp_blk = s_gp.n

        qkv_j0 = stw12.n_ld
        wo_j0 = stw3.n_ld
        qkv_ct0 = stw12.n_ct
        wo_ct0 = stw3.n_ct
        omm_wo0 = [None]

        def a_loads(sync):
            if LVL < 1:
                return
            if SERIAL_A:
                ctx.wait(sync, "sync", s_dve, ln_gate_dve)
            for hp in range(N_PAIRS):
                for wn in ("Wq", "Wk", "Wv"):
                    ap2 = W[wn][:, 2 * hp: 2 * hp + 2, :].rearrange(
                        "d h k -> d (h k)")
                    stw12.load(sync, "sync", ap2)
        block.sync(a_loads)

        def a_casts(gp):
            if LVL < 1:
                return
            for hp in range(N_PAIRS):
                for w in range(3):
                    jj = stw12.n_ct - qkv_ct0
                    if jj - stw12.nbf < 0:
                        fw = None if stw12.n_ct < stw12.nbf else (s_bp, bp0)
                    else:
                        pj = jj - stw12.nbf
                        php, pw = divmod(pj, 3)
                        fw = (s_bp, bp0 + php * 12 + pw * 4 + 4)
                    stw12.cast(gp, "gp", scale=DN if w < 2 else None,
                               free_wait=fw)
        block.gpsimd(a_casts)

        ptr_pa = s_ptr.n - ptr0
        bev_a0 = s_bev.n
        ksq_dve = {}
        qsq_dve = {}
        for hp in range(N_PAIRS):
            per_pair_dd = 32 if LVL >= 4 else 16
            kb = bsc0 + hp * per_pair_dd
            qdb = kb + TT16
            kdg = adg0 + hp * per_pair_dd
            qdg = kdg + TT16
            kxp = bxp0 + hp * 64
            qxp = kxp + 32
            krc = brc0 + hp * 32
            qrc = krc + TT16

            def paA_pe(tensor, hp=hp, kb=kb, qdb=qdb, kdg=kdg, qdg=qdg,
                       kxp=kxp, qxp=qxp, krc=krc, qrc=qrc):
                if LVL < 1:
                    return
                if hp == 0:
                    ctx.wait(tensor, "pe", s_dve, ln_gate_dve)
                    ctx.wait(tensor, "pe", s_gp, gp_blk)
                if LVL >= 4:
                    ctx.wait(tensor, "pe", s_aqv, aqv0 + hp * 16)
                for w in range(3):
                    j = qkv_j0 + hp * 3 + w
                    stw12.wait_ready(tensor, "pe", j)
                    for nt in range(NT):
                        pr = s_bp.n - bp0
                        bank = pA if pr % 2 == 0 else pB
                        ctx.wait(tensor, "pe", s_bqe, bqe0 + pr - 1)
                        for c in range(DC):
                            ins = tensor.matmul(
                                bank[:, :], stw12.tile(j, c * 128, 128),
                                xh[:, c * NTOK + nt * T: c * NTOK + (nt + 1) * T],
                                start=(c == 0), stop=(c == DC - 1))
                        s_bp.inc(ins)
                if LVL < 2:
                    return
                if "vtr" not in ATTNA_KSKIP:
                    ctx.wait(tensor, "pe", s_bqe, bqe0 + hp * 12 + 12)
                    for ck in range(TT16):
                        ctx.wait(tensor, "pe", s_bvk,
                                 bvk0 + hp * 33 + 1 + 2 * ck + 1)
                        if ck >= 2:
                            ctx.wait(tensor, "pe", s_bvk,
                                     bvk0 + hp * 33 + 1 + 2 * (ck - 1))
                        bank = pC if ck % 2 == 0 else pD
                        ins = tensor.transpose(bank[0:128, 0:128],
                                               tf32b[:, ts(ck % 2, 128)],
                                               id_f32[:, :])
                        s_bvt.inc(ins)
                if LVL < 3:
                    return
                if "vtr" not in ATTNA_KSKIP:
                    ctx.wait(tensor, "pe", s_bvk, bvk0 + hp * 33 + 33)
                if hp >= 1 and "kv" not in ATTNA_KSKIP:
                    akf_pp = 3 if LVL >= 4 else 2
                    ctx.wait(tensor, "pe", s_akf, akf0 + (hp - 1) * akf_pp + 2)
                for tt in range(TT16):
                    d = kb + tt
                    dbank = pC if (d - bsc0) % 2 == 0 else pD
                    if "dd" not in ATTNA_KSKIP:
                        if "acts" not in ATTNA_KSKIP:
                            ctx.wait(tensor, "pe", s_bxp,
                                     2 * (d - bsc0) - 2 + bxp0)
                        if "diag" not in ATTNA_KSKIP:
                            ctx.wait(tensor, "pe", s_brc, brc0 + (d - bsc0) - 1)
                        if tt >= 2 and "rm" not in ATTNA_KSKIP and \
                                "justrm" not in ATTNA_KSKIP:
                            ctx.wait(tensor, "pe", s_amx,
                                     amx0 + hp * 16 + tt - 1)
                        if "ksq" not in ATTNA_KSKIP:
                            bev_pp = 8 if LVL >= 4 else 4
                            ctx.wait(tensor, "pe", s_bev,
                                     bev_a0 + hp * bev_pp + 4)
                        dd_mv = id_bf[:, :] if "blkid" in ATTNA_KSKIP \
                            else blkp[:, :]
                        ins = tensor.matmul(dbank[:, 0:128], ka[:, ts(tt, 128)],
                                            dd_mv, start=True, stop=True)
                        s_bsc.inc(ins)
                    if "diag" not in ATTNA_KSKIP:
                        ctx.wait(tensor, "pe", s_brc, brc0 + (d - bsc0))
                        ins = tensor.matmul(pG[0:128, 0:2], ksq[:, ts(tt, 128)],
                                            hhalf[:, :], start=True, stop=True)
                        s_adg.inc(ins)
                    if "kv" not in ATTNA_KSKIP and tt >= 1:
                        t1 = tt - 1
                        ctx.wait(tensor, "pe", s_bxp,
                                 bxp0 + 2 * (kb + t1 - bsc0) + 2)
                        tensor.matmul(pE[:, 0:129], eftile[:, ts(t1 % 2, 128)],
                                      vtk[:, t1 * 129:(t1 + 1) * 129],
                                      start=(t1 == 0), stop=False)
                        tensor.matmul(pF[0:1, 0:129], ones_bf_c[:, :],
                                      vtk[:, t1 * 129:(t1 + 1) * 129],
                                      start=(t1 == 0), stop=False)
                if "kv" not in ATTNA_KSKIP:
                    t1 = TT16 - 1
                    ctx.wait(tensor, "pe", s_bxp,
                             bxp0 + 2 * (kb + t1 - bsc0) + 2)
                    tensor.matmul(pE[:, 0:129], eftile[:, ts(t1 % 2, 128)],
                                  vtk[:, t1 * 129:(t1 + 1) * 129],
                                  start=False, stop=True)
                    ins = tensor.matmul(pF[0:1, 0:129], ones_bf_c[:, :],
                                        vtk[:, t1 * 129:(t1 + 1) * 129],
                                        start=False, stop=True)
                    s_bav.inc(ins)
                if LVL < 4:
                    return
                ctx.wait(tensor, "pe", s_akf, akf0 + hp * 3 + 2)
                ins = tensor.matmul(pF[:, 0:129], ones_f32_r[:, :],
                                    row_b[0:1, 0:129], start=True, stop=True)
                s_brb.inc(ins)
                ctx.wait(tensor, "pe", s_amx, amx0 + (hp + 1) * 16)
                for tt in range(TT16):
                    d = qdb + tt
                    dbank = pC if (d - bsc0) % 2 == 0 else pD
                    ctx.wait(tensor, "pe", s_bev, bev_a0 + hp * 8 + 8)
                    ctx.wait(tensor, "pe", s_bxp, bxp0 + 2 * (d - bsc0) - 2)
                    ctx.wait(tensor, "pe", s_brc, brc0 + (d - bsc0) - 1)
                    ins = tensor.matmul(dbank[:, 0:128], qa[:, ts(tt, 128)],
                                        blkp[:, :], start=True, stop=True)
                    s_bsc.inc(ins)
                    ctx.wait(tensor, "pe", s_brc, brc0 + (d - bsc0))
                    ins = tensor.matmul(pG[0:128, 0:2], ksq[:, ts(tt, 128)],
                                        hhalf[:, :], start=True, stop=True)
                    s_adg.inc(ins)
                    if tt >= 1:
                        qi = hp * 16 + tt - 1
                        ctx.wait(tensor, "pe", s_aep, aep0 + qi + 1)
                        ctx.wait(tensor, "pe", s_aqv, aqv0 + qi - 1)
                        bank = pA if (tt - 1) % 2 == 0 else pB
                        ins = tensor.transpose(bank[0:128, 0:128],
                                               tf32a[:, ts((tt - 1) % 2, 128)],
                                               id_f32[:, :])
                        s_ptr.inc(ins)
                qi = hp * 16 + 15
                ctx.wait(tensor, "pe", s_aep, aep0 + qi + 1)
                ctx.wait(tensor, "pe", s_aqv, aqv0 + qi - 1)
                ins = tensor.transpose(pB[0:128, 0:128],
                                       tf32a[:, ts(15 % 2, 128)], id_f32[:, :])
                s_ptr.inc(ins)
            block.tensor(paA_pe)

            def paA_sc(scalar, hp=hp, kb=kb, qdb=qdb, kxp=kxp, qxp=qxp,
                       krc=krc, qrc=qrc):
                if LVL < 1:
                    return
                for w in range(3):
                    for nt in range(NT):
                        pr = s_bqe.n - bqe0
                        bank = pA if pr % 2 == 0 else pB
                        ctx.wait(scalar, "act", s_bp, bp0 + pr + 1)
                        if w == 0:
                            if "dd" in ATTNA_KSKIP:
                                pass
                            elif LVL >= 4:
                                ctx.wait(scalar, "act", s_bsc, bsc0 + hp * 32)
                            elif LVL == 3:
                                ctx.wait(scalar, "act", s_bsc, bsc0 + hp * 16)
                            if hp >= 1 and LVL >= 4:
                                ctx.wait(scalar, "act", s_bev, qsq_dve[hp - 1])
                            ins = scalar.copy(out=qa[:, ts(nt, T)],
                                              in_=bank[:, :])
                        elif w == 1:
                            if hp >= 1 and LVL >= 3 and \
                                    "ksq" not in ATTNA_KSKIP:
                                ctx.wait(scalar, "act", s_bev, ksq_dve[hp - 1])
                            ins = scalar.copy(out=ka[:, ts(nt, T)],
                                              in_=bank[:, :])
                        else:
                            if LVL >= 2 and hp >= 1 and \
                                    "vtr" not in ATTNA_KSKIP:
                                ctx.wait(scalar, "act", s_bvk,
                                         bvk0 + hp * 33)
                            ins = scalar.copy(out=va_bf[:, ts(nt, T)],
                                              in_=bank[:, :])
                        s_bqe.inc(ins)
                if LVL < 3:
                    return
                for tt in range(TT16):
                    if "acts" in ATTNA_KSKIP:
                        break
                    d = kb + tt - bsc0
                    dbank = pC if d % 2 == 0 else pD
                    if "dd" not in ATTNA_KSKIP:
                        ctx.wait(scalar, "act", s_bsc, bsc0 + d + 1)
                    if "rm" not in ATTNA_KSKIP and \
                            "justrm" not in ATTNA_KSKIP and \
                            "cmb" not in ATTNA_KSKIP:
                        ctx.wait(scalar, "act", s_arm, arm0 + hp * 16 + tt + 1)
                    if "diag" not in ATTNA_KSKIP:
                        ctx.wait(scalar, "act", s_brc, brc0 + d + 1)
                        b0, b1 = bcols[:, 0:1], bcols[:, 1:2]
                    else:
                        b0, b1 = cst[:, 0:1], cst[:, 0:1]
                    kaf = AF.Copy if "actcp" in ATTNA_KSKIP else AF.Exp
                    if "actb0" in ATTNA_KSKIP or "actcp" in ATTNA_KSKIP:
                        b0, b1 = 0.0, 0.0
                    scalar.activation(out=eftile[:, (tt % 2) * 128:(tt % 2) * 128 + 64],
                                      in_=dbank[:, 0:64], func=kaf,
                                      bias=b0, scale=1.0)
                    ins = scalar.activation(
                        out=eftile[:, (tt % 2) * 128 + 64:(tt % 2) * 128 + 128],
                        in_=dbank[:, 64:128], func=kaf,
                        bias=b1, scale=1.0)
                    s_bxp.inc(ins, 2)
                if LVL < 4:
                    return
                for tt in range(TT16):
                    d = qdb + tt - bsc0
                    dbank = pC if d % 2 == 0 else pD
                    ctx.wait(scalar, "act", s_bsc, bsc0 + d + 1)
                    ctx.wait(scalar, "act", s_brc, brc0 + d + 1)
                    if hp * 16 + tt >= 2:
                        ctx.wait(scalar, "act", s_ptr,
                                 ptr0 + ptr_pa + hp * 16 + tt - 1)
                    scalar.activation(out=eftile[:, (tt % 2) * 128:(tt % 2) * 128 + 64],
                                      in_=dbank[:, 0:64], func=AF.Exp,
                                      bias=bcols[:, 0:1], scale=1.0)
                    ins = scalar.activation(
                        out=eftile[:, (tt % 2) * 128 + 64:(tt % 2) * 128 + 128],
                        in_=dbank[:, 64:128], func=AF.Exp,
                        bias=bcols[:, 1:2], scale=1.0)
                    s_bxp.inc(ins, 2)
            block.scalar(paA_sc)

            def paA_ve(vector, hp=hp, kb=kb, qdb=qdb):
                if LVL < 2:
                    return
                if "vtr" not in ATTNA_KSKIP:
                    if LVL >= 3 and "kv" not in ATTNA_KSKIP:
                        ctx.wait(vector, "dve", s_bav, bav0 + hp)
                    ctx.wait(vector, "dve", s_bqe, bqe0 + hp * 12 + 12)
                    ins = vector.memset(vtk[:, :], 1.0)
                    s_bvk.inc(ins)
                    for ck in range(TT16):
                        if ck >= 2:
                            ctx.wait(vector, "dve", s_bvt,
                                     bvt0 + hp * 16 + ck - 1)
                        ins = vector.tensor_copy(out=tf32b[:, ts(ck % 2, 128)],
                                                 in_=va_bf[:, ts(ck, 128)])
                        s_bvk.inc(ins)
                        ctx.wait(vector, "dve", s_bvt, bvt0 + hp * 16 + ck + 1)
                        bank = pC if ck % 2 == 0 else pD
                        ins = vector.tensor_copy(
                            out=vtk[:, ck * 129: ck * 129 + 128],
                            in_=bank[0:128, 0:128])
                        s_bvk.inc(ins)
                if LVL < 3:
                    return
                ctx.wait(vector, "dve", s_bqe, bqe0 + hp * 12 + 8)
                if "diag" not in ATTNA_KSKIP:
                    if LVL >= 4:
                        ctx.wait(vector, "dve", s_adg, adg0 + hp * 32)
                    else:
                        ctx.wait(vector, "dve", s_adg, adg0 + hp * 16)
                for nt in range(NT):
                    if "ksq" in ATTNA_KSKIP:
                        break
                    ins = vector.tensor_mul(ksq[:, ts(nt, T)], ka[:, ts(nt, T)],
                                            ka[:, ts(nt, T)])
                    s_bev.inc(ins)
                ksq_dve[hp] = s_bev.n
                def combine(tt2, hp=hp):
                    if tt2 < 0:
                        return None
                    sl = mxc[:, 1 + tt2 % 2: 2 + tt2 % 2]
                    ctx.wait(vector, "dve", s_arm, arm0 + hp * 16 + tt2 + 1)
                    ctx.wait(vector, "dve", s_amx, amx0 + hp * 16 + tt2)
                    if tt2 == 0 and hp == 0:
                        return vector.tensor_copy(out=mxc[:, 0:1], in_=sl)
                    return vector.tensor_max(mxc[:, 0:1], mxc[:, 0:1], sl)

                for tt in range(TT16):
                    d = kb + tt - bsc0
                    dbank = pC if d % 2 == 0 else pD
                    if "rm" not in ATTNA_KSKIP and "cmb" not in ATTNA_KSKIP:
                        ins = combine(tt - 2)
                        if ins is not None:
                            s_amx.inc(ins)
                    if "diag" not in ATTNA_KSKIP:
                        ctx.wait(vector, "dve", s_adg, adg0 + d + 1)
                        ctx.wait(vector, "dve", s_bxp, bxp0 + 2 * d)
                        vector.tensor_scalar(out=bcols[:, 0:1], in0=pG[0:128, 0:1],
                                             scalar1=-1.0, scalar2=LNR,
                                             op0=ALU.mult, op1=ALU.add)
                        ins = vector.tensor_scalar(out=bcols[:, 1:2],
                                                   in0=pG[0:128, 1:2],
                                                   scalar1=-1.0, scalar2=LNR,
                                                   op0=ALU.mult, op1=ALU.add)
                        s_brc.inc(ins)
                    if "rm" not in ATTNA_KSKIP and \
                            "justrm" not in ATTNA_KSKIP:
                        ctx.wait(vector, "dve", s_bsc, bsc0 + d + 1)
                        ctx.wait(vector, "dve", s_amx, amx0 + hp * 16 + tt - 1)
                        ins = vector.reduce_max(mxc[:, 1 + tt % 2: 2 + tt % 2],
                                                dbank[:, 0:128], axis=AX.X)
                        if "cmb" in ATTNA_KSKIP:
                            s_amx.inc(ins)
                        else:
                            s_arm.inc(ins)
                if "rm" not in ATTNA_KSKIP and "cmb" not in ATTNA_KSKIP:
                    s_amx.inc(combine(TT16 - 2))
                    s_amx.inc(combine(TT16 - 1))
                if "kv" not in ATTNA_KSKIP:
                    ctx.wait(vector, "dve", s_bav, bav0 + hp + 1)
                    ins = vector.tensor_copy(out=kvu[:, hp * 129:(hp + 1) * 129],
                                             in_=pE[:, 0:129])
                    s_akf.inc(ins)
                    ins = vector.tensor_scalar_mul(row_b[0:1, 0:129],
                                                   pF[0:1, 0:129], REPS)
                    s_akf.inc(ins)
                if LVL < 4:
                    return
                ctx.wait(vector, "dve", s_bqe, bqe0 + hp * 12 + 4)
                ctx.wait(vector, "dve", s_adg, adg0 + hp * 32 + 16)
                for nt in range(NT):
                    ins = vector.tensor_mul(ksq[:, ts(nt, T)], qa[:, ts(nt, T)],
                                            qa[:, ts(nt, T)])
                    s_bev.inc(ins)
                qsq_dve[hp] = s_bev.n
                for tt in range(TT16):
                    d = qdb + tt - bsc0
                    dbank = pC if d % 2 == 0 else pD
                    ctx.wait(vector, "dve", s_adg, adg0 + d + 1)
                    ctx.wait(vector, "dve", s_bsc, bsc0 + d + 1)
                    ctx.wait(vector, "dve", s_brc, brc0 + d)
                    vector.reduce_max(mxc[:, 1:2], dbank[:, 0:64], axis=AX.X)
                    ins = vector.reduce_max(mxc[:, 2:3], dbank[:, 64:128],
                                            axis=AX.X)
                    oedge(vector, s_odv, ins)
                    vector.tensor_add(mxc[:, 3:4], pG[0:128, 0:1], mxc[:, 1:2])
                    ins = vector.tensor_add(bcols[:, 2:3], pG[0:128, 1:2],
                                            mxc[:, 2:3])
                    oedge(vector, s_odv, ins)
                    ctx.wait(vector, "dve", s_bxp, bxp0 + 2 * d)
                    vector.tensor_scalar(out=bcols[:, 0:1], in0=mxc[:, 3:4],
                                         scalar1=-1.0, scalar2=LNR,
                                         op0=ALU.mult, op1=ALU.add)
                    ins = vector.tensor_scalar(out=bcols[:, 1:2],
                                               in0=bcols[:, 2:3],
                                               scalar1=-1.0, scalar2=LNR,
                                               op0=ALU.mult, op1=ALU.add)
                    s_brc.inc(ins)
                    ctx.wait(vector, "dve", s_bxp, bxp0 + 2 * (d + 1))
                    if tt >= 2:
                        ctx.wait(vector, "dve", s_ptr,
                                 ptr0 + ptr_pa + hp * 16 + tt - 1)
                    ins = vector.tensor_scalar_add(tf32a[:, ts(tt % 2, 128)],
                                                   eftile[:, ts(tt % 2, 128)],
                                                   REPS)
                    s_aep.inc(ins)
                    if tt >= 1:
                        qi = hp * 16 + tt - 1
                        ctx.wait(vector, "dve", s_ptr, ptr0 + ptr_pa + qi + 1)
                        bank = pA if (tt - 1) % 2 == 0 else pB
                        ins = vector.tensor_copy(
                            out=hgrp[:, hp * NTOK + (tt - 1) * 128:
                                     hp * NTOK + tt * 128],
                            in_=bank[0:128, 0:128])
                        s_aqv.inc(ins)
                qi = hp * 16 + 15
                ctx.wait(vector, "dve", s_ptr, ptr0 + ptr_pa + qi + 1)
                ins = vector.tensor_copy(
                    out=hgrp[:, hp * NTOK + 15 * 128: hp * NTOK + 16 * 128],
                    in_=pB[0:128, 0:128])
                s_aqv.inc(ins)
                ctx.wait(vector, "dve", s_brb, brb0 + hp + 1)
                ins = vector.tensor_copy(out=kveps[:, hp * 129:(hp + 1) * 129],
                                         in_=pF[:, 0:129])
                s_akf.inc(ins)
            block.vector(paA_ve)

        COLVECS = [("ab1", b1c, 32), ("ab2", b2c, 8), ("ln1_g", g1c, 8),
                   ("ln1_b", be1c, 8), ("ln2_g", g2c, 8), ("ln2_b", be2c, 8)]
        if ATTNA_STOP in ("A0", "C", "P", "V", "K"):
            lst.close()
            return s_dve.n
        amx_end = s_amx.n

        def cc_pe(tensor):
            ctx.wait(tensor, "pe", s_amx, amx_end)
            ctx.wait(tensor, "pe", s_bxp, bxp0 + 512)
            ctx.wait(tensor, "pe", s_brc, brc0 + 256)
            ctx.wait(tensor, "pe", s_aep, aep0 + 128)
            ins = tensor.transpose(pC[0:1, 0:128], mxc[:, 0:1], id_f32[:, :])
            s_ptr.inc(ins)
        block.tensor(cc_pe)
        ptr_cc = s_ptr.n

        def cc_ve(vector):
            ctx.wait(vector, "dve", s_ptr, ptr_cc)
            ins = vector.reduce_max(mxc[0:1, 3:4], pC[0:1, 0:128], axis=AX.X)
            s_dve.inc(ins)
        block.vector(cc_ve)
        dve_cc = s_dve.n

        def cc_dma1(sync):
            ctx.wait(sync, "sync", s_dve, dve_cc)
            ins = sync.dma_start(out=cc_in, in_=mxc[0:1, 3:4])
            s_ccd.inc(ins, 16)
            gi = 0
            for name, buf, nb in COLVECS:
                for g4 in range(nb // 4):
                    ctx.wait(sync, "sync", s_acv, 4 * gi)
                    ins = sync.dma_start(
                        out=iotile[0:1, 0:512],
                        in_=W[name][ts(g4, 512)].rearrange(
                            "(one m) -> one m", one=1))
                    s_abc.inc(ins, 16)
                    gi += 1
            if ATTNA_STOP != "A":
                wo2 = W["Wo"].rearrange("h k d -> (h k) d")
                for dc in range(DC):
                    stw3.load(sync, "sync", wo2[:, ts(dc, 128)])
        block.sync(cc_dma1)

        def col_pe(tensor):
            gi = 0
            bi = 0
            for name, buf, nb in COLVECS:
                for g4 in range(nb // 4):
                    ctx.wait(tensor, "pe", s_abc, 16 * (gi + 1))
                    ctx.wait(tensor, "pe", s_brc, brc0 + 256)
                    if bi >= 4:
                        ctx.wait(tensor, "pe", s_ace, bi - 3)
                    for c in range(4):
                        ctx.wait(tensor, "pe", s_ace, bi - 3)
                        ins = tensor.transpose(
                            pG[0:128, (bi % 4):(bi % 4) + 1],
                            iotile[0:1, c * 128:(c + 1) * 128],
                            id_f32[0:1, 0:1])
                        s_acv.inc(ins)
                        bi += 1
                    gi += 1
        block.tensor(col_pe)

        def col_ve(vector):
            bi = 0
            for name, buf, nb in COLVECS:
                for g4 in range(nb // 4):
                    for c in range(4):
                        ctx.wait(vector, "dve", s_acv, bi + 1)
                        col = g4 * 4 + c
                        ins = vector.tensor_copy(
                            out=buf[:, col:col + 1],
                            in_=pG[0:128, (bi % 4):(bi % 4) + 1])
                        s_ace.inc(ins)
                        bi += 1
        block.vector(col_ve)
        ace_end = s_ace.n

        def cc_gp(gp):
            ctx.wait(gp, "gp", s_ccd, s_ccd.n)
            ins = gp.collective_compute(
                "AllReduce", ALU.max, replica_groups=[list(range(8))],
                ins=[cc_in], outs=[cc_out])
            s_cc.inc(ins)
            gp.wait_ge(s_cc.h, s_cc.n)
            ins = gp.dma_start(out=mxc[0:1, 3:4], in_=cc_out)
            s_ccd2.inc(ins, 16)
        block.gpsimd(cc_gp)

        def cc_pe2(tensor):
            ctx.wait(tensor, "pe", s_ccd2, s_ccd2.n)
            ins = tensor.matmul(pD[:, 0:1], ones_f32_r[:, :], mxc[0:1, 3:4],
                                start=True, stop=True)
            s_brb.inc(ins)
        block.tensor(cc_pe2)
        brb_cc = s_brb.n

        def cc_sc(scalar):
            ctx.wait(scalar, "act", s_brb, brb_cc)
            ins = scalar.activation(out=emxc[:, :], in_=pD[:, 0:1],
                                    func=AF.Exp, bias=0.0, scale=-1.0)
            s_act.inc(ins)
        block.scalar(cc_sc)
        act_cc = s_act.n

        COLVECS_DONE = True
        if ATTNA_STOP == "A":
            lst.close()
            return s_aqv.n if False else s_dve.n
        for hp in range(8):
            def paB_ve1(vector, hp=hp):
                if hp == 0:
                    ctx.wait(vector, "dve", s_act, act_cc)
                else:
                    ctx.wait(vector, "dve", s_bsc,
                             bsc0 + 256 + hp * NT)
                    ctx.wait(vector, "dve", s_adg,
                             adg0 + 256 + hp * 2 * NT)
                ins = vector.scalar_tensor_tensor(
                    out=kvf[:, :], in0=kvu[:, hp * 129:(hp + 1) * 129],
                    scalar=emxc[:, 0:1], in1=kveps[:, hp * 129:(hp + 1) * 129],
                    op0=ALU.mult, op1=ALU.add)
                s_dve.inc(ins)
                vector.memset(kvblk[:, :], 0.0)
                ins = vector.memset(sblk[:, :], 0.0)
                oedge(vector, s_odv, ins)
                vector.tensor_copy(out=kvblk[0:64, 0:64], in_=kvf[0:64, 0:64])
                vector.tensor_copy(out=kvblk[64:128, 64:128],
                                   in_=kvf[64:128, 64:128])
                vector.tensor_copy(out=sblk[0:64, 0:1], in_=kvf[0:64, 128:129])
                ins = vector.tensor_copy(out=sblk[64:128, 1:2],
                                         in_=kvf[64:128, 128:129])
                s_akf.inc(ins)
            block.vector(paB_ve1)
            akf_b = s_akf.n

            def paB_pe(tensor, hp=hp, akf_b=akf_b):
                ctx.wait(tensor, "pe", s_akf, akf_b)
                for nt in range(NT):
                    i = hp * NT + nt
                    ctx.wait(tensor, "pe", s_brc, brc0 + 256 + 2 * i)
                    ins = tensor.matmul(pF[0:1, :], sblk[:, 0:1],
                                        hgrp[:, hp * NTOK + nt * T:
                                             hp * NTOK + (nt + 1) * T],
                                        start=True, stop=True)
                    s_adg.inc(ins)
                    ins = tensor.matmul(pG[0:1, :], sblk[:, 1:2],
                                        hgrp[:, hp * NTOK + nt * T:
                                             hp * NTOK + (nt + 1) * T],
                                        start=True, stop=True)
                    s_adg.inc(ins)
                    nbank = pC if i % 2 == 0 else pD
                    ctx.wait(tensor, "pe", s_bas, bas0 + i - 1)
                    ins = tensor.matmul(nbank[:, :], kvblk[:, :],
                                        hgrp[:, hp * NTOK + nt * T:
                                             hp * NTOK + (nt + 1) * T],
                                        start=True, stop=True)
                    s_bsc.inc(ins)
                    ctx.wait(tensor, "pe", s_brc, brc0 + 256 + 2 * i + 2)
                    ctx.wait(tensor, "pe", s_brbe, brbe0 + 2 * i - 2 + 2)
                    tensor.matmul(pE[0:64, :], ones_f32_r[0:1, 0:64],
                                  row_b[0:1, 0:T], start=True, stop=True)
                    ins = tensor.matmul(pA[64:128, :], ones_f32_r[0:1, 0:64],
                                        row_c[0:1, 0:T], start=True, stop=True,
                                        tile_position=(0, 64))
                    s_brb.inc(ins)
            block.tensor(paB_pe)

            def paB_sc(scalar, hp=hp):
                for nt in range(NT):
                    i = hp * NT + nt
                    nbank = pC if i % 2 == 0 else pD
                    ctx.wait(scalar, "act", s_bsc, bsc0 + 256 + i + 1)
                    ctx.wait(scalar, "act", s_bat, bat0 + i)
                    ins = scalar.copy(out=aav_s[:, :], in_=nbank[:, :])
                    s_bas.inc(ins)
            block.scalar(paB_sc)

            def paB_ve2(vector, hp=hp):
                for nt in range(NT):
                    i = hp * NT + nt
                    ctx.wait(vector, "dve", s_adg, adg0 + 256 + 2 * i + 2)
                    vector.reciprocal(row_b[0:1, 0:T], pF[0:1, :])
                    ins = vector.reciprocal(row_c[0:1, 0:T], pG[0:1, :])
                    s_brc.inc(ins, 2)
                    ctx.wait(vector, "dve", s_brb, brb0 + 8 + 1 + i + 1)
                    ctx.wait(vector, "dve", s_bat, bat0 + i)
                    vector.tensor_copy(out=arb_s[0:64, :], in_=pE[0:64, :])
                    ins = vector.tensor_copy(out=arb_s[64:128, :],
                                             in_=pA[64:128, :])
                    s_brbe.inc(ins, 2)
                    vector.wait_ge(s_brbe.h, s_brbe.n)
                    ctx.wait(vector, "dve", s_bas, bas0 + i + 1)
                    ins = vector.tensor_mul(
                        hgrp[:, hp * NTOK + nt * T: hp * NTOK + (nt + 1) * T],
                        aav_s[:, :], arb_s[:, :])
                    s_bat.inc(ins)
            block.vector(paB_ve2)

        bat_end = s_bat.n
        bqe_end = s_bqe.n
        omm0 = s_omm.n
        oad0 = s_oad.n

        def wo_casts(gp):
            for dc in range(DC):
                jj = stw3.n_ct - wo_ct0
                pj = jj - stw3.nbf
                fw = None if pj < 0 else (s_omm, omm0 + pj * 4 + 4)
                stw3.cast(gp, "gp", free_wait=fw)
        block.gpsimd(wo_casts)

        def wo_pe(tensor):
            ctx.wait(tensor, "pe", s_bat, bat_end)
            for dc in range(DC):
                j = wo_j0 + dc
                stw3.wait_ready(tensor, "pe", j)
                for nt in range(NT):
                    oidx = s_omm.n - omm0
                    bank = pA if oidx % 2 == 0 else pB
                    ctx.wait(tensor, "pe", s_oad, oad0 + oidx - 1)
                    for c in range(DC):
                        ins = tensor.matmul(
                            bank[:, :], stw3.tile(j, c * 128, 128),
                            hgrp[:, c * NTOK + nt * T: c * NTOK + (nt + 1) * T],
                            start=(c == 0), stop=(c == DC - 1))
                    s_omm.inc(ins)
        block.tensor(wo_pe)

        def wo_ve(vector):
            for dc in range(DC):
                for nt in range(NT):
                    oidx = s_oad.n - oad0
                    bank = pA if oidx % 2 == 0 else pB
                    ctx.wait(vector, "dve", s_omm, omm0 + oidx + 1)
                    xsl = xh[:, dc * NTOK + nt * T: dc * NTOK + (nt + 1) * T]
                    ins = vector.tensor_add(xsl, xsl, bank[:, :])
                    s_oad.inc(ins)
        block.vector(wo_ve)

        def colgate(vector):
            ctx.wait(vector, "dve", s_ace, ace_end)
        block.vector(colgate)
        ln1_end = emit_ln(xh, xh, EPS_LN_IN, s_oad, s_oad.n,
                          g_cols=g1c, b_cols=be1c)

        iw1_j0 = stw12.n_ld
        iw2_j0 = stw3.n_ld
        iw1_ct0 = stw12.n_ct
        iw2_ct0 = stw3.n_ct
        amm_i0 = s_amm.n
        omm_i0 = s_omm.n
        osub_i0 = s_osub.n
        brbe_end_a = s_brbe.n
        gelu_i0 = s_gelu.n
        bev_i0 = s_bev.n
        hm_i0 = s_hm.n
        oad_i0 = s_oad.n

        def if_loads(sync):
            for nt in range(NT):
                for ic in range(32):
                    stw12.load(sync, "sync", W["aW1"][:, ts(ic, 128)])
                for dc in range(DC):
                    for g in range(4):
                        stw3.load(sync, "sync",
                                  W["aW2"][ts(g, 1024), ts(dc, 128)])
        block.sync(if_loads)

        def if_casts(gp):
            for nt in range(NT):
                for ic in range(32):
                    jj = stw12.n_ct - iw1_ct0
                    pj = jj - stw12.nbf
                    if pj < 0:
                        fw = (s_bp, bp0 + 96)
                    else:
                        fw = (s_amm, amm_i0 + pj + 1)
                    stw12.cast(gp, "gp", free_wait=fw)
                for k in range(32):
                    jj = stw3.n_ct - iw2_ct0
                    pj = jj - stw3.nbf
                    if pj < 0:
                        fw = (s_omm, omm_i0)
                    elif pj % 4 == 3:
                        fw = (s_omm, omm_i0 + pj // 4 + 1)
                    else:
                        fw = (s_osub, osub_i0 + (pj // 4) * 3 + pj % 4 + 1)
                    stw3.cast(gp, "gp", free_wait=fw)
        block.gpsimd(if_casts)

        for nt in range(NT):
            def if_pe(tensor, nt=nt):
                if nt == 0:
                    ctx.wait(tensor, "pe", s_dve, ln1_end)
                    ctx.wait(tensor, "pe", s_oad, oad_i0)
                    ctx.wait(tensor, "pe", s_brbe, brbe_end_a)
                for ic in range(32):
                    gi = s_amm.n - amm_i0
                    j = iw1_j0 + nt * 32 + ic
                    stw12.wait_ready(tensor, "pe", j)
                    bank = pA if gi % 2 == 0 else pB
                    ctx.wait(tensor, "pe", s_bev, bev_i0 + gi - 1)
                    for c in range(DC):
                        ins = tensor.matmul(
                            bank[:, :], stw12.tile(j, c * 128, 128),
                            xh[:, c * NTOK + nt * T: c * NTOK + (nt + 1) * T],
                            start=(c == 0), stop=(c == DC - 1))
                    s_amm.inc(ins)
                ctx.wait(tensor, "pe", s_hm, hm_i0 + (nt + 1) * 32)
                for dc in range(DC):
                    oi = s_omm.n - omm_i0
                    bank = pE if oi % 2 == 0 else pF
                    ctx.wait(tensor, "pe", s_oad, oad_i0 + oi - 1)
                    for ic in range(32):
                        j = iw2_j0 + nt * 32 + dc * 4 + ic // 8
                        stw3.wait_ready(tensor, "pe", j)
                        ins = tensor.matmul(
                            bank[:, :], stw3.tile(j, (ic % 8) * 128, 128),
                            hgrp[:, ic * T: (ic + 1) * T],
                            start=(ic == 0), stop=(ic == 31))
                        if ic % 8 == 7 and ic < 31:
                            s_osub.inc(ins)
                    s_omm.inc(ins)
            block.tensor(if_pe)

            def if_ve(vector, nt=nt):
                if nt == 0:
                    ctx.wait(vector, "dve", s_omm, omm_i0)
                for ic in range(32):
                    bi = s_bev.n - bev_i0
                    bank = pA if bi % 2 == 0 else pB
                    ctx.wait(vector, "dve", s_amm, amm_i0 + bi + 1)
                    if nt >= 1 and ic == 0:
                        ctx.wait(vector, "dve", s_omm, omm_i0 + nt * 8)
                    ctx.wait(vector, "dve", s_hm, hm_i0 + bi)
                    vector.tensor_scalar(out=tf32a[:, :], in0=bank[:, :],
                                         scalar1=b1c[:, ic:ic + 1], scalar2=0.0,
                                         op0=ALU.add, op1=ALU.max)
                    ins = vector.tensor_scalar(out=tf32b[:, :],
                                               in0=bank[:, :],
                                               scalar1=b1c[:, ic:ic + 1],
                                               scalar2=0.0,
                                               op0=ALU.add, op1=ALU.min)
                    s_bev.inc(ins)
                    vector.wait_ge(s_bev.h, s_bev.n)
                    hi = s_hm.n - hm_i0
                    ctx.wait(vector, "dve", s_gelu, gelu_i0 + hi + 1)
                    ins = vector.scalar_tensor_tensor(
                        out=hgrp[:, ic * T:(ic + 1) * T],
                        in0=va_bf[:, :].bitcast(F32)[:, 0:512], scalar=-1.0,
                        in1=tf32a[:, :], op0=ALU.add, op1=ALU.add)
                    s_hm.inc(ins)
                for dc in range(DC):
                    oi = s_oad.n - oad_i0
                    bank = pE if oi % 2 == 0 else pF
                    ctx.wait(vector, "dve", s_omm, omm_i0 + oi + 1)
                    xsl = xh[:, dc * NTOK + nt * T: dc * NTOK + (nt + 1) * T]
                    ins = vector.scalar_tensor_tensor(
                        out=xsl, in0=bank[:, :], scalar=b2c[:, dc:dc + 1],
                        in1=xsl, op0=ALU.add, op1=ALU.add)
                    s_oad.inc(ins)
            block.vector(if_ve)

            def if_sc(scalar, nt=nt):
                for ic in range(32):
                    gi = s_gelu.n - gelu_i0
                    ctx.wait(scalar, "act", s_bev, bev_i0 + gi + 1)
                    ctx.wait(scalar, "act", s_hm, hm_i0 + gi - 1 + 1)
                    ins = scalar.activation(
                        out=va_bf[:, :].bitcast(F32)[:, 0:512],
                        in_=tf32b[:, :], func=AF.Exp, bias=0.0, scale=1.0)
                    s_gelu.inc(ins)
            block.scalar(if_sc)

        ln2_end = emit_ln(xh, xh, EPS_LN_IN, s_oad, s_oad.n,
                          g_cols=g2c, b_cols=be2c, add_to=x_sb)
        lst.close()
        return ln2_end

    def emit_attnb(ln_gate_dve, moe_ends):
        stw12 = moe_res["w12"]
        stw3 = moe_res["w3"]
        attn_fm = moe_res["hgrp"]

        lst = ExitStack()
        C2b = ctx.sbuf("C2b", [128, NTOK], BF16, stack=lst)
        S2b = ctx.sbuf("S2b", [128, NTOK], BF16, stack=lst)
        qb = ctx.sbuf("qb", [128, NTOK], BF16, stack=lst)
        kb = ctx.sbuf("kb", [128, NTOK], BF16, stack=lst)
        qsw = ctx.sbuf("qsw", [128, NTOK // 2], BF16, stack=lst)
        vb32 = ctx.sbuf("vb32", [128, NTOK], F32, stack=lst)
        v_tok = ctx.sbuf("v_tok", [128, 16 * 65], BF16, stack=lst)
        probsT = iotile[:, :].bitcast(BF16)
        rb_s = ctx.sbuf("rb_s", [128, T], BF16, stack=lst)
        av_s = ctx.sbuf("av_s", [128, T], BF16, stack=lst)
        bout_c = ctx.sbuf("bout_c", [128, DC], F32, stack=lst)
        colsc = ctx.sbuf("colsc", [128, 4], F32, stack=lst)
        colsi = ctx.sbuf("colsi", [128, 1], mybir.dt.int32, stack=lst)
        tb32 = iotile
        ii32 = vb32[:, :].bitcast(mybir.dt.int32)

        s_bp = ctx.sem("s_bp")
        s_bqe = ctx.sem("s_bqe")
        s_bsc = ctx.sem("s_bsc")
        s_bxp = ctx.sem("s_bxp")
        s_bav = ctx.sem("s_bav")
        s_brb = ctx.sem("s_brb")
        s_bvt = ctx.sem("s_bvt")
        s_bvk = ctx.sem("s_bvk")
        s_brc = ctx.sem("s_brc")
        s_brbe = ctx.sem("s_brbe")
        s_bat = ctx.sem("s_bat")
        s_bas = ctx.sem("s_bas")
        s_rope = ctx.sem("s_rope")
        s_swap = ctx.sem("s_swap")
        s_bo = ctx.sem("s_bo")

        bp0 = s_bp.n
        bqe0 = s_bqe.n
        bsc0 = s_bsc.n
        bxp0 = s_bxp.n
        bav0 = s_bav.n
        brb0 = s_brb.n
        bvt0 = s_bvt.n
        bvk0 = s_bvk.n
        brc0 = s_brc.n
        brbe0 = s_brbe.n
        bat0 = s_bat.n
        bas0 = s_bas.n
        rope0 = s_rope.n
        TWO_PI = 2.0 * math.pi

        INV2PI = 1.0 / (2.0 * math.pi)

        def tbl_gp(gp):
            for q4 in range(4):
                gp.iota(colsi[q4 * 32:(q4 + 1) * 32, :], pattern=[[0, 1]],
                        base=0, channel_multiplier=1)
            ins = gp.iota(vb32[:, :].bitcast(mybir.dt.int32),
                          pattern=[[1, NTOK]], base=0, channel_multiplier=0)
            s_gp.inc(ins)
        block.gpsimd(tbl_gp)
        gp_tbl = s_gp.n

        def tbl_ve1(vector):
            ctx.wait(vector, "dve", s_gp, gp_tbl)
            vector.tensor_copy(out=vb32[:, :], in_=vb32[:, :].bitcast(mybir.dt.int32))
            ins = vector.tensor_copy(out=colsc[:, 0:1], in_=colsi[:, :])
            s_dve.inc(ins)
        block.vector(tbl_ve1)
        dve_tbl1 = s_dve.n

        def tbl_sc1(scalar):
            ctx.wait(scalar, "act", s_dve, dve_tbl1)
            ins = scalar.activation(out=colsc[:, 2:3], in_=colsc[:, 0:1],
                                    func=AF.Exp, bias=0.0,
                                    scale=-math.log(10000.0) / 32.0)
            s_act.inc(ins)
        block.scalar(tbl_sc1)
        act_tbl1 = s_act.n

        HS = NTOK // 2
        qbi = qb[:, :].bitcast(mybir.dt.int32)
        kbf = kb[:, :].bitcast(F32)

        def tbl_ve2(vector):
            ctx.wait(vector, "dve", s_act, act_tbl1)
            ins = vector.tensor_scalar_mul(vb32[:, :], vb32[:, :],
                                           colsc[:, 2:3])
            oedge(vector, s_odv, ins)
            for ch in range(2):
                c0 = ch * HS
                vsl = vb32[:, c0:c0 + HS]
                tsl = tb32[:, c0:c0 + HS]
                ins = vector.tensor_scalar(out=tsl, in0=vsl, scalar1=INV2PI,
                                           scalar2=0.25, op0=ALU.mult,
                                           op1=ALU.add)
                oedge(vector, s_odv, ins)
                ins = vector.tensor_scalar_mul(vsl, vsl, INV2PI)
                oedge(vector, s_odv, ins)
                ins = vector.tensor_copy(out=qbi[:, 0:HS], in_=tsl)
                oedge(vector, s_odv, ins)
                ins = vector.tensor_copy(out=kbf[:, 0:HS], in_=qbi[:, 0:HS])
                oedge(vector, s_odv, ins)
                ins = vector.tensor_sub(tsl, tsl, kbf[:, 0:HS])
                oedge(vector, s_odv, ins)
                ins = vector.tensor_copy(out=qbi[:, 0:HS], in_=vsl)
                oedge(vector, s_odv, ins)
                ins = vector.tensor_scalar_add(tsl, tsl, -0.5)
                oedge(vector, s_odv, ins)
                ins = vector.tensor_copy(out=kbf[:, 0:HS], in_=qbi[:, 0:HS])
                oedge(vector, s_odv, ins)
                ins = vector.tensor_sub(vsl, vsl, kbf[:, 0:HS])
                oedge(vector, s_odv, ins)
                ins = vector.tensor_scalar_add(vsl, vsl, -0.5)
                if ch == 0:
                    oedge(vector, s_odv, ins)
                else:
                    s_dve.inc(ins)
        block.vector(tbl_ve2)
        dve_tbl2 = s_dve.n

        def tbl_sc2(scalar):
            ctx.wait(scalar, "act", s_dve, dve_tbl2)
            scalar.activation(out=C2b[:, :], in_=tb32[:, 0:NTOK], func=AF.Sin,
                              bias=0.0, scale=-2.0 * math.pi)
            ins = scalar.activation(out=S2b[:, :], in_=vb32[:, :], func=AF.Sin,
                                    bias=0.0, scale=-2.0 * math.pi)
            s_act.inc(ins)
        block.scalar(tbl_sc2)
        act_tbl2 = s_act.n

        def tbl_ve3(vector):
            ctx.wait(vector, "dve", s_act, act_tbl2)
            vector.tensor_scalar_mul(S2b[0:32, :], S2b[0:32, :], -1.0)
            ins = vector.tensor_scalar_mul(S2b[64:96, :], S2b[64:96, :], -1.0)
            s_dve.inc(ins)
        block.vector(tbl_ve3)
        dve_tbl3 = s_dve.n

        qkv_j0 = stw12.n_ld
        wo_j0 = stw3.n_ld
        qkv_ct0 = stw12.n_ct
        wo_ct0 = stw3.n_ct

        def b_loads(sync):
            for c in range(DC):
                ins = sync.dma_start(
                    out=bout_c[:, c:c + 1],
                    in_=W["b_out"][ts(c, 128)].rearrange("(p one) -> p one", one=1))
                s_bo.inc(ins, 16)
            for hp in range(8):
                for w in range(3):
                    cols = w * D + hp * 128
                    stw12.load(sync, "sync", W["w_qkv"][:, cols: cols + 128])
            for dc in range(DC):
                stw3.load(sync, "sync", W["w_out"][:, ts(dc, 128)])
        block.sync(b_loads)

        def qkv_cast(gp):
            jj = stw12.n_ct - qkv_ct0
            if jj - stw12.nbf < 0:
                fw = (s_bmm, moe_ends["bmm"])
            else:
                pj = jj - stw12.nbf
                php, pw = divmod(pj, 3)
                fw = (s_bp, bp0 + php * 12 + pw * 4 + 4)
            stw12.cast(gp, "gp", free_wait=fw)

        def b_casts0(gp):
            qkv_cast(gp)
            qkv_cast(gp)
        block.gpsimd(b_casts0)

        for hp in range(8):
            def pe_pair(tensor, hp=hp):
                if hp == 0:
                    ctx.wait(tensor, "pe", s_dve, ln_gate_dve)
                    ctx.wait(tensor, "pe", s_gelu, moe_ends["gelu"])
                    ctx.wait(tensor, "pe", s_bev, moe_ends["bev"])
                    ctx.wait(tensor, "pe", s_oad, moe_ends["oad"])
                    ctx.wait(tensor, "pe", s_mb, moe_ends["mb"])
                for w in range(3):
                    j = qkv_j0 + hp * 3 + w
                    stw12.wait_ready(tensor, "pe", j)
                    for nt in range(NT):
                        pr = s_bp.n - bp0
                        bank = pA if pr % 2 == 0 else pB
                        ctx.wait(tensor, "pe", s_bqe, bqe0 + pr - 1)
                        for c in range(DC):
                            ins = tensor.matmul(
                                bank[:, :], stw12.tile(j, c * 128, 128),
                                xh[:, c * NTOK + nt * T: c * NTOK + (nt + 1) * T],
                                start=(c == 0), stop=(c == DC - 1))
                        s_bp.inc(ins)
                ctx.wait(tensor, "pe", s_rope, rope0 + hp * 4 + 4)
                ctx.wait(tensor, "pe", s_bqe, bqe0 + hp * 12 + 12)
                for h in range(2):
                    r0 = h * 64
                    dr0 = 64 - r0
                    segb = bvk0 + (hp * 2 + h) * 5
                    seg_brc = brc0 + hp * 8 + h * 4
                    ctx.wait(tensor, "pe", s_brbe, brbe0 + hp * 8 + h * 4)
                    ctx.wait(tensor, "pe", s_brc, seg_brc)
                    ctx.wait(tensor, "pe", s_bas, bas0 + hp * 8 + h * 4)
                    for g4 in range(4):
                        bank = pG if g4 % 2 == 0 else pF
                        if g4 >= 2:
                            ctx.wait(tensor, "pe", s_bvk, segb + g4)
                        for c in range(4):
                            ck = g4 * 4 + c
                            ins = tensor.transpose(
                                bank[0:128, ts(c, 128)],
                                vb32[:, ts(ck, 128)], id_f32[:, :])
                        s_bvt.inc(tr_flush(tensor) or ins)
                    ctx.wait(tensor, "pe", s_bvk, segb + 5)
                    for qt in range(NT):
                        hq = s_bav.n - bav0
                        avbank = pE if hq % 2 == 0 else pF
                        ctx.wait(tensor, "pe", s_brc, brc0 + hq - 1)
                        ctx.wait(tensor, "pe", s_bas, bas0 + hq - 1)
                        ctx.wait(tensor, "pe", s_brbe, brbe0 + hq)
                        sc_base = s_bsc.n - bsc0

                        def av_mm(tensor, ck, first, last):
                            tensor.matmul(
                                pG[0:1, :],
                                v_tok[:, ck * 65 + 64: ck * 65 + 65],
                                probsT[:, ts((sc_base + ck) % 3, T)],
                                start=first, stop=last)
                            return tensor.matmul(
                                avbank[r0:r0 + 64, :],
                                v_tok[:, ck * 65: ck * 65 + 64],
                                probsT[:, ts((sc_base + ck) % 3, T)],
                                start=first, stop=last,
                                tile_position=(0, r0))

                        for ck in range(16):
                            sbank = (pC, pD, pI)[(sc_base + ck) % 3]
                            ctx.wait(tensor, "pe", s_bxp,
                                     bxp0 + sc_base + ck - 2)
                            ins = tensor.matmul(
                                sbank[:, :],
                                kb[r0:r0 + 64, ts(ck, 128)],
                                qb[r0:r0 + 64, ts(qt, T)],
                                start=True, stop=True)
                            s_bsc.inc(ins)
                            if ck >= 2:
                                ctx.wait(tensor, "pe", s_bxp,
                                         bxp0 + sc_base + ck - 1)
                                av_mm(tensor, ck - 2, ck - 2 == 0, False)
                        ctx.wait(tensor, "pe", s_bxp, bxp0 + sc_base + 15)
                        av_mm(tensor, 14, False, False)
                        ctx.wait(tensor, "pe", s_bxp, bxp0 + sc_base + 16)
                        ins = av_mm(tensor, 15, False, True)
                        s_bav.inc(ins)
                        ctx.wait(tensor, "pe", s_brc, brc0 + hq + 1)
                        ctx.wait(tensor, "pe", s_brbe, brbe0 + hq)
                        if h == 0:
                            ins = tensor.matmul(
                                pG[0:64, :], ones_f32_r[0:1, 0:64],
                                row_c[0:1, :], start=True, stop=True)
                        else:
                            ins = tensor.matmul(
                                pG[64:128, :], ones_f32_r[0:1, 0:64],
                                row_c[0:1, :], start=True, stop=True,
                                tile_position=(0, 64))
                        s_brb.inc(ins)
            block.tensor(pe_pair)

            def sc_pair(scalar, hp=hp):
                for w in range(3):
                    for nt in range(NT):
                        pr = s_bqe.n - bqe0
                        bank = pA if pr % 2 == 0 else pB
                        ctx.wait(scalar, "act", s_bp, bp0 + pr + 1)
                        if w == 0:
                            if hp == 0:
                                ctx.wait(scalar, "act", s_dve, dve_tbl2)
                            ctx.wait(scalar, "act", s_bsc, bsc0 + hp * 128)
                            ctx.wait(scalar, "act", s_rope, rope0 + hp * 4)
                            ins = scalar.activation(
                                out=qb[:, ts(nt, T)], in_=bank[:, :],
                                func=AF.Copy, bias=0.0, scale=SCALE_B)
                        elif w == 1:
                            ins = scalar.copy(out=kb[:, ts(nt, T)],
                                              in_=bank[:, :])
                        else:
                            ctx.wait(scalar, "act", s_bvt, bvt0 + hp * 8)
                            ins = scalar.copy(out=vb32[:, ts(nt, T)],
                                              in_=bank[:, :])
                        s_bqe.inc(ins)
                for h in range(2):
                    r0 = h * 64
                    for qt in range(NT):
                        hq = s_bas.n - bas0
                        for ck in range(16):
                            xp = s_bxp.n - bxp0
                            sbank = (pC, pD, pI)[xp % 3]
                            ctx.wait(scalar, "act", s_bsc, bsc0 + xp + 1)
                            ins = scalar.activation(
                                out=probsT[:, ts(xp % 3, T)], in_=sbank[:, :],
                                func=AF.Exp, bias=0.0, scale=1.0)
                            s_bxp.inc(ins)
                        avbank = pE if hq % 2 == 0 else pF
                        ctx.wait(scalar, "act", s_bav, bav0 + hq + 1)
                        ctx.wait(scalar, "act", s_bat, bat0 + hq)
                        ins = scalar.copy(out=av_s[r0:r0 + 64, :],
                                          in_=avbank[r0:r0 + 64, :])
                        s_bas.inc(ins)
            block.scalar(sc_pair)

            def ve_pair(vector, hp=hp):
                if hp == 0:
                    ctx.wait(vector, "dve", s_omm, moe_ends["omm"])
                for h in range(2):
                    r0 = h * 64
                    dr0 = 64 - r0
                    ctx.wait(vector, "dve", s_bav, bav0 + (hp * 2 + h) * 4)
                    ins = vector.memset(v_tok[:, :], 1.0)
                    s_bvk.inc(ins)
                    vector.wait_ge(s_bvk.h, s_bvk.n)
                    for g4 in range(4):
                        bank = pG if g4 % 2 == 0 else pF
                        ctx.wait(vector, "dve", s_bvt,
                                 bvt0 + (hp * 2 + h) * 4 + g4 + 1)
                        for c in range(4):
                            ck = g4 * 4 + c
                            ins = vector.tensor_copy(
                                out=v_tok[:, ck * 65: ck * 65 + 64],
                                in_=bank[0:128, c * 128 + r0:
                                         c * 128 + r0 + 64])
                        s_bvk.inc(ins)
                    for qt in range(NT):
                        hq = s_brc.n - brc0
                        avbank = pE if hq % 2 == 0 else pF
                        ctx.wait(vector, "dve", s_bav, bav0 + hq + 1)
                        ins = vector.reciprocal(row_c[0:1, :],
                                                pG[0:1, :])
                        s_brc.inc(ins)
                        vector.tensor_copy(out=junk[:, 0:4], in_=id_f32[:, 0:4])
                        ctx.wait(vector, "dve", s_brb, brb0 + hq + 1)
                        ins = vector.tensor_copy(out=rb_s[r0:r0 + 64, :],
                                                 in_=pG[r0:r0 + 64, :])
                        s_brbe.inc(ins)
                        vector.wait_ge(s_brbe.h, s_brbe.n)
                        ctx.wait(vector, "dve", s_bas, bas0 + hq + 1)
                        ins = vector.tensor_mul(
                            attn_fm[r0:r0 + 64, hp * NTOK + qt * T:
                                    hp * NTOK + (qt + 1) * T],
                            av_s[r0:r0 + 64, :], rb_s[r0:r0 + 64, :])
                        s_bat.inc(ins)
            block.vector(ve_pair)

            def gp_pair(gp, hp=hp):
                if hp == 0:
                    ctx.wait(gp, "gp", s_dve, dve_tbl3)
                    ctx.wait(gp, "gp", s_act, act_tbl2)
                for which, buf in (("q", qb), ("k", kb)):
                    ctx.wait(gp, "gp", s_bqe,
                             bqe0 + hp * 12 + (4 if which == "q" else 8))
                    for ch in range(2):
                        c0 = ch * HS
                        ri = s_rope.n - rope0
                        if ri >= 1:
                            ctx.wait(gp, "gp", s_rope, rope0 + ri)
                        for (dst0, src0) in ((0, 32), (32, 0), (64, 96), (96, 64)):
                            ins = gp.dma_start(
                                out=qsw[dst0:dst0 + 32, :],
                                in_=buf[src0:src0 + 32, c0:c0 + HS])
                            s_swap.inc(ins, 16)
                        gp.wait_ge(s_swap.h, s_swap.n)
                        for half in range(2):
                            r = half * 64
                            gp.tensor_mul(qsw[r:r + 64, :], qsw[r:r + 64, :],
                                          S2b[r:r + 64, c0:c0 + HS])
                        for half in range(2):
                            r = half * 64
                            ins = gp.tensor_mul(buf[r:r + 64, c0:c0 + HS],
                                                buf[r:r + 64, c0:c0 + HS],
                                                C2b[r:r + 64, c0:c0 + HS])
                        oedge(gp, s_ogp, ins)
                        for half in range(2):
                            r = half * 64
                            ins = gp.tensor_add(buf[r:r + 64, c0:c0 + HS],
                                                buf[r:r + 64, c0:c0 + HS],
                                                qsw[r:r + 64, :])
                        s_rope.inc(ins)
                qkv_cast(gp)
                if hp < 7:
                    qkv_cast(gp)
                    qkv_cast(gp)
                if hp == 7:
                    for dc in range(DC):
                        jj = stw3.n_ct - wo_ct0
                        if jj - stw3.nbf < 0:
                            fw = (s_omm, moe_ends["omm"])
                        else:
                            fw = (s_omm,
                                  moe_ends["omm"] + (jj - stw3.nbf) * 4 + 4)
                        stw3.cast(gp, "gp", free_wait=fw)
            block.gpsimd(gp_pair)

        omm_p4_0 = s_omm.n
        bat_end = s_bat.n
        oad_p4_0 = s_oad.n
        bqe_end = s_bqe.n

        def p4_pe(tensor):
            ctx.wait(tensor, "pe", s_bat, bat_end)
            ctx.wait(tensor, "pe", s_bqe, bqe_end)
            for dc in range(DC):
                j = wo_j0 + dc
                stw3.wait_ready(tensor, "pe", j)
                for nt in range(NT):
                    oidx = s_omm.n - omm_p4_0
                    bank = pA if oidx % 2 == 0 else pB
                    ctx.wait(tensor, "pe", s_oad, oad_p4_0 + oidx - 1)
                    for c in range(DC):
                        ins = tensor.matmul(
                            bank[:, :], stw3.tile(j, c * 128, 128),
                            attn_fm[:, c * NTOK + nt * T:
                                    c * NTOK + (nt + 1) * T],
                            start=(c == 0), stop=(c == DC - 1))
                    s_omm.inc(ins)
        block.tensor(p4_pe)

        def p4_ve(vector):
            ctx.wait(vector, "dve", s_bo, s_bo.n)
            for dc in range(DC):
                for nt in range(NT):
                    oidx = s_oad.n - oad_p4_0
                    bank = pA if oidx % 2 == 0 else pB
                    ctx.wait(vector, "dve", s_omm, omm_p4_0 + oidx + 1)
                    xsl = x_sb[:, dc * NTOK + nt * T: dc * NTOK + (nt + 1) * T]
                    ins = vector.scalar_tensor_tensor(
                        out=xsl, in0=bank[:, :], scalar=bout_c[:, dc: dc + 1],
                        in1=xsl, op0=ALU.add, op1=ALU.add)
                    s_oad.inc(ins)
        block.vector(p4_ve)

        lst.close()
        return s_oad.n

    def emit_store(gate_sem, gate_val):
        pe_st0 = s_pe.n
        dve_st0 = s_dve.n

        def out_tr(tensor):
            ctx.wait(tensor, "pe", gate_sem, gate_val)
            for t in range(TT16):
                for c4 in range(2):
                    g = t * 2 + c4
                    bank = pG if g % 2 == 0 else pF
                    ctx.wait(tensor, "pe", s_dve, dve_st0 + g - 1)
                    for c in range(c4 * 4, c4 * 4 + 4):
                        ins = tensor.transpose(
                            bank[0:128, ts(c % 4, 128)],
                            x_sb[:, c * NTOK + t * 128: c * NTOK + (t + 1) * 128],
                            id_f32[:, :])
                    s_pe.inc(tr_flush(tensor) or ins)
        block.tensor(out_tr)

        def out_ev(vector):
            for t in range(TT16):
                for c4 in range(2):
                    g = t * 2 + c4
                    bank = pG if g % 2 == 0 else pF
                    ctx.wait(vector, "dve", s_pe, pe_st0 + g + 1)
                    if t >= 2 and c4 == 0:
                        ctx.wait(vector, "dve", s_st0 if t % 2 == 0 else s_st1,
                                 16 * ((t - 2) // 2 + 1))
                    for c in range(c4 * 4, c4 * 4 + 4):
                        ins = vector.tensor_copy(
                            out=iotile[:, (t % 2) * D + c * 128: (t % 2) * D + (c + 1) * 128],
                            in_=bank[0:128, ts(c % 4, 128)])
                    s_dve.inc(ins)
        block.vector(out_ev)

        def out_dma(sync):
            for t in range(TT16):
                ctx.wait(sync, "sync", s_dve, dve_st0 + (t + 1) * 2)
                ins = sync.dma_start(out=y_out[ts(t, 128), :],
                                     in_=iotile[:, ts(t % 2, D)])
                (s_st0 if t % 2 == 0 else s_st1).inc(ins, 16)
            sync.wait_ge(s_st0.h, s_st0.n)
            sync.wait_ge(s_st1.h, s_st1.n)
        block.sync(out_dma)

    def emit_store_raw(gate_sem, gate_val):
        def raw_dma(sync):
            ctx.wait(sync, "sync", gate_sem, gate_val)
            ins = sync.dma_start(
                out=y_out.rearrange("(a p) d -> p a d", p=128),
                in_=x_sb[:, :].rearrange("p (a d) -> p a d", d=1024))
            s_out.inc(ins, 16)
            sync.wait_ge(s_out.h, s_out.n)
        block.sync(raw_dma)

    if upto == "copy":
        emit_store(s_dve, X_READY_DVE)
        return nc, stack

    if upto == "ln":
        ln_end = emit_ln(x_sb, xh, EPS_LN_OUT, s_dve, X_READY_DVE)

        def lncp(vector):
            ctx.wait(vector, "dve", s_dve, ln_end)
            for c in range(DC):
                ins = vector.tensor_copy(out=x_sb[:, ts(c, NTOK)],
                                         in_=xh[:, ts(c, NTOK)])
                s_dve.inc(ins)
        block.vector(lncp)
        emit_store(s_dve, s_dve.n)
        return nc, stack

    if upto == "attnb":
        ln_end = emit_ln(x_sb, xh, EPS_LN_OUT, s_dve, X_READY_DVE)
        oad_end = emit_moe(ln_end, elist=MOE_ELIST)
        moe_ends = {"gelu": s_gelu.n, "bev": s_bev.n, "oad": s_oad.n,
                    "mb": s_mb.n, "omm": s_omm.n, "bmm": s_bmm.n,
                    "amm": s_amm.n}
        ln2_end = emit_ln(x_sb, xh, EPS_LN_OUT, s_oad, oad_end)
        oad_end2 = emit_attnb(ln2_end, moe_ends)
        emit_store(s_oad, oad_end2)
        return nc, stack

    if upto == "attna_only":
        if NOLN:
            def rawln(vector):
                ctx.wait(vector, "dve", s_dve, X_READY_DVE)
                for c in range(DC):
                    ins = vector.tensor_copy(out=xh[:, ts(c, NTOK)],
                                             in_=x_sb[:, ts(c, NTOK)])
                    s_dve.inc(ins)
            block.vector(rawln)
            ln_end = s_dve.n
        else:
            ln_end = emit_ln(x_sb, xh, EPS_LN_OUT, s_dve, X_READY_DVE)
        a_end = emit_attna(ln_end)
        if RAWSTORE:
            emit_store_raw(s_dve, a_end)
        else:
            emit_store(s_dve, a_end)
        return nc, stack

    if upto == "full":
        ln_end = emit_ln(x_sb, xh, EPS_LN_OUT, s_dve, X_READY_DVE)
        a_end = emit_attna(ln_end)
        pw12 = (s_amm, s_amm.n)
        pw3 = (s_omm, s_omm.n)
        ln2_end = emit_ln(x_sb, xh, EPS_LN_OUT, s_dve, a_end)
        oad1 = emit_moe(ln2_end, elist=MOE_ELIST, prev_w12=pw12, prev_w3=pw3)
        moe_ends = {"gelu": s_gelu.n, "bev": s_bev.n, "oad": s_oad.n,
                    "mb": s_mb.n, "omm": s_omm.n, "bmm": s_bmm.n,
                    "amm": s_amm.n}
        ln3_end = emit_ln(x_sb, xh, EPS_LN_OUT, s_oad, oad1)
        oad2 = emit_attnb(ln3_end, moe_ends)
        s_bp_h = ctx.sem("s_bp")
        pw12b = (s_bp_h, s_bp_h.n)
        pw3b = (s_omm, s_omm.n)
        ln4_end = emit_ln(x_sb, xh, EPS_LN_OUT, s_oad, oad2)
        oad3 = emit_moe(ln4_end, elist=MOE_ELIST, prev_w12=pw12b, prev_w3=pw3b)
        emit_store(s_oad, oad3)
        return nc, stack

    if upto == "ffn1":
        ln_end = emit_ln(x_sb, xh, EPS_LN_OUT, s_dve, X_READY_DVE)
        oad_end = emit_moe(ln_end, elist=MOE_ELIST)
        if DBG:
            dbgx = nc.dram_tensor("dbgx", [128, DC * NTOK], F32,
                                  kind="ExternalOutput").ap()
            dbgh = nc.dram_tensor("dbgh", [128, DC * NTOK], BF16,
                                  kind="ExternalOutput").ap()
            dbgm = nc.dram_tensor("dbgm", [128, TT16 * NEXP], F32,
                                  kind="ExternalOutput").ap()

            s_dbg = ctx.sem("s_dbg")

            def dbg_dma(sync):
                ctx.wait(sync, "sync", s_oad, oad_end)
                sync.dma_start(out=dbgx, in_=x_sb[:, :]).then_inc(s_dbg.h, 16)
                sync.dma_start(out=dbgh, in_=xh[:, :]).then_inc(s_dbg.h, 16)
                ins = sync.dma_start(out=dbgm, in_=xh[:, 0:TT16 * NEXP])
                s_dbg.inc(ins, 32)
                s_dbg.n += 16
                sync.wait_ge(s_dbg.h, 48)
            block.sync(dbg_dma)
        emit_store(s_oad, oad_end)
        return nc, stack

    raise NotImplementedError(upto)


def kernel(**inputs):
    x = np.ascontiguousarray(inputs["L0_attn_a_x"])
    nc, stack = build_nc("full")
    weights = {k: np.ascontiguousarray(inputs[k])
               for k in INPUTS_FOR["full"]}
    in_maps = [dict(weights, x=x[b]) for b in range(8)]
    res = run_bass_kernel_spmd(nc, in_maps, core_ids=list(range(8)))
    stack.close()
    return np.stack([res.results[b]["y"] for b in range(8)], axis=0)


# version 05 (final)
# speedup vs baseline: 1.0079x; 1.0079x over previous
import math
from contextlib import ExitStack

import numpy as np

import concourse.bass as bass
import concourse.mybir as mybir
from concourse.bass_utils import run_bass_kernel_spmd

F32 = mybir.dt.float32
BF16 = mybir.dt.bfloat16
AF = mybir.ActivationFunctionType
ALU = mybir.AluOpType
AX = mybir.AxisListType

D = 1024
NTOK = 2048
DC = 8
NT = 4
T = 512
TT16 = 16
H = 16
DH = 64
PM = 64
DFF = 4096
NEXP = 8
EPS_LN_OUT = 1e-5
EPS_LN_IN = 1e-6
EPS_KERN = 1e-6
RATIO = 0.125
DN = 64.0 ** (-0.25)
SCALE_B = 64.0 ** (-0.5)


def ts(i, n):
    return slice(i * n, (i + 1) * n)


class Sem:
    def __init__(self, stack, nc, name):
        self.h = stack.enter_context(nc.semaphore(name))
        self.n = 0
        self.name = name

    def inc(self, ins, k=1):
        ins.then_inc(self.h, k)
        self.n += k
        return self.n


class Ctx:
    def __init__(self, nc, stack, block):
        self.nc = nc
        self.stack = stack
        self.block = block
        self.sems = {}
        self.last_wait = {}

    def sem(self, name):
        if name not in self.sems:
            self.sems[name] = Sem(self.stack, self.nc, name)
        return self.sems[name]

    def sbuf(self, name, shape, dtype, stack=None):
        st = stack if stack is not None else self.stack
        return st.enter_context(self.nc.sbuf_tensor(name, shape, dtype))

    def psum(self, name, shape, dtype):
        return self.stack.enter_context(self.nc.psum_tensor(name, shape, dtype))

    def wait(self, eng, ekey, sem, val):
        if val <= 0:
            return
        k = (ekey, sem.name)
        if self.last_wait.get(k, -1) >= val:
            return
        eng.wait_ge(sem.h, val)
        self.last_wait[k] = val


class Stream:

    def __init__(self, ctx, name, cols, nstage, nbf, mcols=None):
        self.ctx = ctx
        self.cols = cols
        self.mcols = mcols or cols
        self.nstage = nstage
        self.nbf = nbf
        self.stage = ctx.sbuf(name + "_st", [128, nstage * cols], F32)
        self.bf = ctx.sbuf(name + "_bf", [128, nbf * cols], BF16)
        self.s_ld = [ctx.sem(name + f"_ld{i}") for i in range(nstage)]
        self.s_ct = ctx.sem(name + "_ct")
        self.s_us = ctx.sem(name + "_us")
        self.n_ld = 0
        self.n_ct = 0

    def load(self, sync, ekey, src_ap):
        j = self.n_ld
        self.ctx.wait(sync, ekey, self.s_ct, j - self.nstage + 1)
        dst = self.stage[:, ts(j % self.nstage, self.cols)]
        ins = sync.dma_start(
            out=dst.rearrange("p (c m) -> p c m", m=self.mcols),
            in_=src_ap.rearrange("(c p) m -> p c m", p=128))
        self.s_ld[j % self.nstage].inc(ins, 16)
        self.n_ld += 1

    def cast(self, eng, ekey, scale=None, free_wait=None):
        j = self.n_ct
        self.ctx.wait(eng, ekey, self.s_ld[j % self.nstage],
                      16 * (j // self.nstage + 1))
        if free_wait is not None:
            self.ctx.wait(eng, ekey, free_wait[0], free_wait[1])
        dst = self.bf[:, ts(j % self.nbf, self.cols)]
        src = self.stage[:, ts(j % self.nstage, self.cols)]
        if scale is None:
            ins = eng.tensor_copy(out=dst, in_=src)
        else:
            ins = eng.tensor_scalar_mul(dst, src, float(scale))
        self.s_ct.inc(ins, 1)
        self.n_ct += 1
        return j

    def tile(self, j, col0, w):
        base = (j % self.nbf) * self.cols
        return self.bf[:, base + col0: base + col0 + w]

    def wait_ready(self, eng, ekey, j):
        self.ctx.wait(eng, ekey, self.s_ct, j + 1)

    def release(self, ins):
        self.s_us.inc(ins, 1)


MOE_ELIST = None
ATTNA_STOP = None
ATTNA_KSKIP = set()
SERIAL_A = False
N_PAIRS = 8
FLUSH_TR = False
DBG = False
NOCONST = False
NOLN = False
RAWSTORE = False
FFN_KEYS = ["L0_ffn_w1s", "L0_ffn_w2s", "L0_ffn_w3s", "L0_ffn_W1",
            "L0_ffn_W2", "L0_ffn_W3", "L0_ffn_Wg"]
ATTNB_KEYS = ["L0_attn_b_w_qkv", "L0_attn_b_w_out", "L0_attn_b_b_out"]
ATTNA_KEYS = ["L0_attn_a_Wq", "L0_attn_a_Wk", "L0_attn_a_Wv", "L0_attn_a_Wo",
              "L0_attn_a_proj", "L0_attn_a_W1", "L0_attn_a_b1", "L0_attn_a_W2",
              "L0_attn_a_b2", "L0_attn_a_ln1_g", "L0_attn_a_ln1_b",
              "L0_attn_a_ln2_g", "L0_attn_a_ln2_b"]
INPUTS_FOR = {
    "copy": [],
    "ln": [],
    "ffn1": FFN_KEYS,
    "attnb": FFN_KEYS + ATTNB_KEYS,
    "attna_only": ATTNA_KEYS,
    "full": FFN_KEYS + ATTNB_KEYS + ATTNA_KEYS,
}


def build_nc(upto="full"):
    nc = bass.Bass("TRN2", target_bir_lowering=False, debug=False)
    stack = ExitStack()
    block = stack.enter_context(nc.Block())
    ctx = Ctx(nc, stack, block)

    x_in = nc.dram_tensor("x", [NTOK, D], F32, kind="ExternalInput").ap()
    y_out = nc.dram_tensor("y", [NTOK, D], F32, kind="ExternalOutput").ap()

    def din(name, shape):
        return nc.dram_tensor(name, shape, F32, kind="ExternalInput").ap()

    need_ffn = upto in ("ffn1", "attnb", "full")
    need_attnb = upto in ("attnb", "full")
    need_attna = upto in ("full", "attna_only")
    W = {}
    if need_ffn:
        W["w1s"] = din("L0_ffn_w1s", [D, DFF])
        W["w2s"] = din("L0_ffn_w2s", [D, DFF])
        W["w3s"] = din("L0_ffn_w3s", [DFF, D])
        W["W1"] = din("L0_ffn_W1", [NEXP, D, DFF])
        W["W2"] = din("L0_ffn_W2", [NEXP, D, DFF])
        W["W3"] = din("L0_ffn_W3", [NEXP, DFF, D])
        W["Wg"] = din("L0_ffn_Wg", [D, NEXP])
    if need_attnb:
        W["w_qkv"] = din("L0_attn_b_w_qkv", [D, 3 * D])
        W["w_out"] = din("L0_attn_b_w_out", [D, D])
        W["b_out"] = din("L0_attn_b_b_out", [D])
    if need_attna:
        W["Wq"] = din("L0_attn_a_Wq", [D, H, DH])
        W["Wk"] = din("L0_attn_a_Wk", [D, H, DH])
        W["Wv"] = din("L0_attn_a_Wv", [D, H, DH])
        W["Wo"] = din("L0_attn_a_Wo", [H, DH, D])
        W["proj"] = din("L0_attn_a_proj", [PM, DH])
        W["aW1"] = din("L0_attn_a_W1", [D, DFF])
        W["ab1"] = din("L0_attn_a_b1", [DFF])
        W["aW2"] = din("L0_attn_a_W2", [DFF, D])
        W["ab2"] = din("L0_attn_a_b2", [D])
        W["ln1_g"] = din("L0_attn_a_ln1_g", [D])
        W["ln1_b"] = din("L0_attn_a_ln1_b", [D])
        W["ln2_g"] = din("L0_attn_a_ln2_g", [D])
        W["ln2_b"] = din("L0_attn_a_ln2_b", [D])
        cc_in = nc.dram_tensor("cc_in", [1, 1], F32).ap()
        cc_out = nc.dram_tensor("cc_out", [1, 1], F32).ap()

    x_sb = ctx.sbuf("x_sb", [128, DC * NTOK], F32)
    xh = ctx.sbuf("xh", [128, DC * NTOK], BF16)
    id_f32 = ctx.sbuf("id_f32", [128, 128], F32)
    id_bf = ctx.sbuf("id_bf", [128, 128], BF16)
    ones_f32_r = ctx.sbuf("ones_f32_r", [1, 128], F32)
    ones_bf_c = ctx.sbuf("ones_bf_c", [128, 1], BF16)
    row_a = ctx.sbuf("row_a", [1, T], F32)
    row_b = ctx.sbuf("row_b", [1, T], F32)
    row_c = ctx.sbuf("row_c", [1, T], F32)
    xb_t = ctx.sbuf("xb_t", [128, 2 * T], BF16)
    sq_t = ctx.sbuf("sq_t", [128, 2 * T], BF16)
    tf32a = ctx.sbuf("tf32a", [128, T], F32)
    tf32b = ctx.sbuf("tf32b", [128, T], F32)
    junk = ctx.sbuf("junk", [128, 8], F32)
    cst = ctx.sbuf("cst", [128, 4], F32)
    iotile = ctx.sbuf("iotile", [128, 2 * D], F32)

    pA = ctx.psum("pA", [128, T], F32)
    pB = ctx.psum("pB", [128, T], F32)
    pC = ctx.psum("pC", [128, T], F32)
    pD = ctx.psum("pD", [128, T], F32)
    pE = ctx.psum("pE", [128, T], F32)
    pF = ctx.psum("pF", [128, T], F32)
    pG = ctx.psum("pG", [128, T], F32)
    pI = ctx.psum("pI", [128, 512], F32)

    s_dma = ctx.sem("s_dma")
    s_in0 = ctx.sem("s_in0")
    s_in1 = ctx.sem("s_in1")
    s_st0 = ctx.sem("s_st0")
    s_st1 = ctx.sem("s_st1")
    s_pe = ctx.sem("s_pe")
    s_dve = ctx.sem("s_dve")
    s_act = ctx.sem("s_act")
    s_gp = ctx.sem("s_gp")
    s_out = ctx.sem("s_out")
    s_odv = ctx.sem("s_odv")
    s_oac = ctx.sem("s_oac")
    s_ogp = ctx.sem("s_ogp")

    def oedge(eng, sem, ins):
        sem.inc(ins)
        eng.wait_ge(sem.h, sem.n)
    s_amm = ctx.sem("s_amm")
    s_bmm = ctx.sem("s_bmm")
    s_omm = ctx.sem("s_omm")
    s_ptr = ctx.sem("s_ptr")
    s_gelu = ctx.sem("s_gelu")
    s_bev = ctx.sem("s_bev")
    s_hm = ctx.sem("s_hm")
    s_oad = ctx.sem("s_oad")
    s_mrow = ctx.sem("s_mrow")
    s_mb = ctx.sem("s_mb")
    s_rmm = ctx.sem("s_rmm")


    def tr_flush(tensor):
        if not FLUSH_TR:
            return None
        tensor.transpose(pI[0:128, 0:128], id_f32[:, :], id_f32[:, :])
        return tensor.transpose(pI[0:128, 0:128], id_f32[:, :], id_f32[:, :])

    def gp_consts(gp):
        ins = gp.memset(id_f32[:, :], 0.0)
        s_gp.inc(ins)
        ins = gp.memset(id_bf[:, :], 0.0)
        s_gp.inc(ins)
        gp.memset(ones_f32_r[:, :], 1.0)
        gp.memset(cst[:, 0:1], EPS_LN_OUT)
        gp.memset(cst[:, 1:2], EPS_LN_IN)
        gp.memset(cst[:, 2:3], math.pi)
        gp.memset(ones_bf_c[:, :], 1.0)
        gp.wait_ge(s_gp.h, s_gp.n - 1)
        gp.affine_select(out=id_f32[:, :], in_=id_f32[:, :],
                         compare_op=ALU.not_equal, fill=1.0,
                         base=0, pattern=[[-1, 128]], channel_multiplier=1)
        gp.wait_ge(s_gp.h, s_gp.n)
        ins = gp.affine_select(out=id_bf[:, :], in_=id_bf[:, :],
                               compare_op=ALU.not_equal, fill=1.0,
                               base=0, pattern=[[-1, 128]],
                               channel_multiplier=1)
        s_gp.inc(ins)

    block.gpsimd(gp_consts)
    GP_CONSTS = s_gp.n

    pe_in0 = s_pe.n
    dve_in0 = s_dve.n

    def in_dma(sync):
        for t in range(TT16):
            ctx.wait(sync, "sync", s_pe, pe_in0 + (t - 1) * 2)
            ins = sync.dma_start(out=iotile[:, ts(t % 2, D)],
                                 in_=x_in[ts(t, 128), :])
            (s_in0 if t % 2 == 0 else s_in1).inc(ins, 16)

    block.sync(in_dma)

    def in_tr(tensor):
        ctx.wait(tensor, "pe", s_gp, GP_CONSTS)
        for t in range(TT16):
            ctx.wait(tensor, "pe", s_in0 if t % 2 == 0 else s_in1,
                     16 * (t // 2 + 1))
            for c4 in range(2):
                g = t * 2 + c4
                bank = pG if g % 2 == 0 else pF
                ctx.wait(tensor, "pe", s_dve, dve_in0 + g - 1)
                for c in range(c4 * 4, c4 * 4 + 4):
                    ins = tensor.transpose(
                        bank[0:128, ts(c % 4, 128)],
                        iotile[:, (t % 2) * D + c * 128: (t % 2) * D + (c + 1) * 128],
                        id_f32[:, :])
                s_pe.inc(tr_flush(tensor) or ins)

    block.tensor(in_tr)

    def in_ev(vector):
        for t in range(TT16):
            for c4 in range(2):
                g = t * 2 + c4
                bank = pG if g % 2 == 0 else pF
                ctx.wait(vector, "dve", s_pe, pe_in0 + g + 1)
                for c in range(c4 * 4, c4 * 4 + 4):
                    ins = vector.tensor_copy(
                        out=x_sb[:, c * NTOK + t * 128: c * NTOK + (t + 1) * 128],
                        in_=bank[0:128, ts(c % 4, 128)])
                s_dve.inc(ins)

    block.vector(in_ev)

    X_READY_DVE = s_dve.n

    def emit_ln(src, dst, eps, gate_sem, gate_val, g_cols=None, b_cols=None,
                add_to=None):
        inv_d = 1.0 / D
        act_cast = {}
        dve_sq = {}
        pe_stats = {}
        dve_rows = {}
        act_rows = {}
        pe_bcast = {}
        dve_apply = {}
        pe0 = s_pe.n
        def slotfree(i):
            return pe0 + (i // DC) * (DC + 1) + (i % DC) + 1

        for nt in range(NT):
            def b_sc(scalar, nt=nt):
                ctx.wait(scalar, "act", gate_sem, gate_val)
                for c in range(DC):
                    i = nt * DC + c
                    sl = i % 2
                    if i >= 2:
                        ctx.wait(scalar, "act", s_pe, slotfree(i - 2))
                    ins = scalar.copy(
                        out=xb_t[:, ts(sl, T)],
                        in_=src[:, c * NTOK + nt * T: c * NTOK + (nt + 1) * T])
                    act_cast[i] = s_act.inc(ins)
            block.scalar(b_sc)

            def b_vsq(vector, nt=nt):
                ctx.wait(vector, "dve", gate_sem, gate_val)
                for c in range(DC):
                    i = nt * DC + c
                    sl = i % 2
                    ctx.wait(vector, "dve", s_act, act_cast[i])
                    if i >= 2:
                        ctx.wait(vector, "dve", s_pe, slotfree(i - 2))
                    ins = vector.tensor_mul(sq_t[:, ts(sl, T)],
                                            xb_t[:, ts(sl, T)], xb_t[:, ts(sl, T)])
                    dve_sq[i] = s_dve.inc(ins)
            block.vector(b_vsq)

            def b_pst(tensor, nt=nt):
                if nt >= 1:
                    ctx.wait(tensor, "pe", s_dve, dve_rows[nt - 1])
                for c in range(DC):
                    i = nt * DC + c
                    sl = i % 2
                    ctx.wait(tensor, "pe", s_dve, dve_sq[i])
                    tensor.matmul(pE[0:1, :], ones_bf_c[:, :], xb_t[:, ts(sl, T)],
                                  start=(c == 0), stop=(c == DC - 1))
                    ins = tensor.matmul(pF[0:1, :], ones_bf_c[:, :], sq_t[:, ts(sl, T)],
                                        start=(c == 0), stop=(c == DC - 1))
                    assert s_pe.inc(ins) == slotfree(i)
                pe_stats[nt] = s_pe.n
            block.tensor(b_pst)

            def b_vrow(vector, nt=nt):
                ctx.wait(vector, "dve", s_pe, pe_stats[nt])
                ins = vector.tensor_scalar_mul(row_a[:, :], pE[0:1, :], inv_d)
                vector.tensor_scalar_mul(row_c[:, :], pF[0:1, :], inv_d)
                oedge(vector, s_odv, ins)
                ins = vector.tensor_mul(row_b[:, :], row_a[:, :], row_a[:, :])
                oedge(vector, s_odv, ins)
                ins = vector.tensor_sub(row_c[:, :], row_c[:, :], row_b[:, :])
                dve_rows[nt] = s_dve.inc(ins)
            block.vector(b_vrow)

            def b_arow(scalar, nt=nt):
                ctx.wait(scalar, "act", s_dve, dve_rows[nt])
                ins = scalar.activation(out=row_c[:, :], in_=row_c[:, :],
                                  func=AF.Ln,
                                  bias=cst[0:1, 0:1] if eps == EPS_LN_OUT else cst[0:1, 1:2],
                                  scale=1.0)
                oedge(scalar, s_oac, ins)
                ins = scalar.activation(out=row_b[:, :], in_=row_c[:, :],
                                        func=AF.Exp, bias=0.0, scale=-0.5)
                act_rows[nt] = s_act.inc(ins)
            block.scalar(b_arow)

            def b_pbc(tensor, nt=nt):
                ctx.wait(tensor, "pe", s_act, act_rows[nt])
                if nt >= 1:
                    ctx.wait(tensor, "pe", s_dve, dve_apply[nt - 1])
                tensor.matmul(pC[:, :], ones_f32_r[:, :], row_a[:, :],
                              start=True, stop=True)
                ins = tensor.matmul(pD[:, :], ones_f32_r[:, :], row_b[:, :],
                                    start=True, stop=True)
                pe_bcast[nt] = s_pe.inc(ins)
            block.tensor(b_pbc)

            def b_vap(vector, nt=nt):
                ctx.wait(vector, "dve", s_pe, pe_bcast[nt])
                for k in range(DC // 2):
                    c0, c1 = 2 * k, 2 * k + 1
                    sl = {c0: tf32a, c1: tf32b}
                    dsl = {c: dst[:, c * NTOK + nt * T:
                                  c * NTOK + (nt + 1) * T] for c in (c0, c1)}
                    for c in (c0, c1):
                        src_sl = src[:, c * NTOK + nt * T:
                                     c * NTOK + (nt + 1) * T]
                        ins = vector.tensor_sub(sl[c][:, :], src_sl, pC[:, :])
                    oedge(vector, s_odv, ins)
                    if g_cols is None:
                        for c in (c0, c1):
                            ins = vector.tensor_mul(dsl[c], sl[c][:, :],
                                                    pD[:, :])
                        if k < DC // 2 - 1:
                            oedge(vector, s_odv, ins)
                    else:
                        for c in (c0, c1):
                            ins = vector.tensor_mul(sl[c][:, :], sl[c][:, :],
                                                    pD[:, :])
                        oedge(vector, s_odv, ins)
                        if add_to is None:
                            for c in (c0, c1):
                                ins = vector.tensor_scalar(
                                    out=dsl[c], in0=sl[c][:, :],
                                    scalar1=g_cols[:, c: c + 1],
                                    scalar2=b_cols[:, c: c + 1],
                                    op0=ALU.mult, op1=ALU.add)
                            if k < DC // 2 - 1:
                                oedge(vector, s_odv, ins)
                        else:
                            for c in (c0, c1):
                                ins = vector.tensor_scalar(
                                    out=sl[c][:, :], in0=sl[c][:, :],
                                    scalar1=g_cols[:, c: c + 1],
                                    scalar2=b_cols[:, c: c + 1],
                                    op0=ALU.mult, op1=ALU.add)
                            oedge(vector, s_odv, ins)
                            for c in (c0, c1):
                                at = add_to[:, c * NTOK + nt * T:
                                            c * NTOK + (nt + 1) * T]
                                ins = vector.tensor_add(at, at, sl[c][:, :])
                            if k < DC // 2 - 1:
                                oedge(vector, s_odv, ins)
                dve_apply[nt] = s_dve.inc(ins)
            block.vector(b_vap)

        return s_dve.n

    moe_res = {}

    moe_call_no = [0]

    def emit_moe(xh_gate_dve, elist=None, prev_w12=None, prev_w3=None):
        if elist is None:
            elist = list(range(NEXP + 1))
        nel = len(elist)
        ci = moe_call_no[0]
        moe_call_no[0] += 1
        lst = ExitStack()
        if "w12" not in moe_res:
            moe_res["w12"] = Stream(ctx, "stw12", 1024, 2, 2, mcols=128)
            moe_res["w3"] = Stream(ctx, "stw3", 1024, 2, 2, mcols=128)
            moe_res["hgrp"] = ctx.sbuf("hgrp", [128, 8 * NTOK], BF16)
        if "wg_bf" not in moe_res:
            moe_res["wg_bf"] = ctx.sbuf("wg_bf", [128, NEXP * DC], BF16)
        stw12 = moe_res["w12"]
        stw3 = moe_res["w3"]
        wg_bf = moe_res["wg_bf"]
        hgrp = moe_res["hgrp"]
        wg_st = ctx.sbuf(f"wg_st{ci}", [128, NEXP * DC], F32, stack=lst)
        mask_cols = ctx.sbuf(f"mask_cols{ci}", [128, TT16 * NEXP], F32, stack=lst)
        mcol = ctx.sbuf(f"mcol{ci}", [128, 64], F32, stack=lst)
        maskb = ctx.sbuf(f"maskb{ci}", [128, NT * T], F32, stack=lst)
        ga_ring = ctx.sbuf(f"ga{ci}", [128, 3 * T], BF16, stack=lst)
        bb_ring = ctx.sbuf(f"bb{ci}", [128, 3 * T], BF16, stack=lst)

        s_wg = ctx.sem("s_wg")

        def wg_dma(sync):
            ins = sync.dma_start(
                out=wg_st[:, :].rearrange("p (c e) -> p c e", e=NEXP),
                in_=W["Wg"].rearrange("(c p) e -> p c e", p=NEXP if False else 128))
            s_wg.inc(ins, 16)
        block.sync(wg_dma)
        wg_dma_n = s_wg.n

        def wg_cast(gp):
            ctx.wait(gp, "gp", s_wg, wg_dma_n)
            ins = gp.tensor_copy(out=wg_bf[:, :], in_=wg_st[:, :])
            s_gp.inc(ins)
        block.gpsimd(wg_cast)
        wg_ready = s_gp.n

        def r_pe(tensor):
            ctx.wait(tensor, "pe", s_dve, xh_gate_dve)
            ctx.wait(tensor, "pe", s_gp, wg_ready)
            for t in range(TT16):
                for c in range(DC):
                    ins = tensor.matmul(
                        pE[0:128, t * NEXP:(t + 1) * NEXP],
                        xh[:, c * NTOK + t * 128: c * NTOK + (t + 1) * 128],
                        wg_bf[:, c * NEXP:(c + 1) * NEXP],
                        start=(c == 0), stop=(c == DC - 1))
                s_rmm.inc(ins)
        block.tensor(r_pe)
        rmm_end = s_rmm.n

        def r_dve(vector):
            ctx.wait(vector, "dve", s_rmm, rmm_end)
            for t in range(TT16):
                vector.memset(mcol[:, 16 + t: 17 + t], 0.0)
            for t in range(TT16):
                ins = vector.reduce_max(
                    mcol[:, t: t + 1], pE[0:128, t * NEXP:(t + 1) * NEXP],
                    axis=AX.X)
            oedge(vector, s_odv, ins)
            for e in range(NEXP):
                for t in range(TT16):
                    ins = vector.tensor_tensor(
                        out=mcol[:, 32 + t: 33 + t],
                        in0=pE[0:128, t * NEXP + e: t * NEXP + e + 1],
                        in1=mcol[:, t: t + 1], op=ALU.is_equal)
                oedge(vector, s_odv, ins)
                for t in range(TT16):
                    ins = vector.tensor_mul(mcol[:, 48 + t: 49 + t],
                                            mcol[:, 32 + t: 33 + t],
                                            mcol[:, 16 + t: 17 + t])
                oedge(vector, s_odv, ins)
                for t in range(TT16):
                    ins = vector.tensor_sub(
                        mask_cols[:, t * NEXP + e: t * NEXP + e + 1],
                        mcol[:, 32 + t: 33 + t], mcol[:, 48 + t: 49 + t])
                oedge(vector, s_odv, ins)
                for t in range(TT16):
                    ins = vector.tensor_add(
                        mcol[:, 16 + t: 17 + t], mcol[:, 16 + t: 17 + t],
                        mask_cols[:, t * NEXP + e: t * NEXP + e + 1])
                if e < NEXP - 1:
                    oedge(vector, s_odv, ins)
                else:
                    s_dve.inc(ins)
        block.vector(r_dve)
        routing_dve_end = s_dve.n

        def srcs(e):
            if e == 0:
                return W["w1s"], W["w2s"], W["w3s"]
            return (W["W1"][e - 1], W["W2"][e - 1], W["W3"][e - 1])

        amm0 = s_amm.n
        bmm0 = s_bmm.n
        omm0 = s_omm.n
        gelu0 = s_gelu.n
        bev0 = s_bev.n
        hm0 = s_hm.n
        oad0 = s_oad.n
        w12_call0 = stw12.n_ld
        w3_call0 = stw3.n_ld
        w12_ct0 = stw12.n_ct
        w3_ct0 = stw3.n_ct

        def w12_free_wait(jj):
            if jj < 0:
                return prev_w12
            ei, r = divmod(jj, 64)
            ic, isw2 = divmod(r, 2)
            grp = ei * 128 + ic * 4 + 4
            return (s_bmm, bmm0 + grp) if isw2 else (s_amm, amm0 + grp)

        def w3_free_wait(jj):
            if jj < 0:
                return prev_w3
            ei, r = divmod(jj, 32)
            g, dc = divmod(r, 8)
            return (s_omm, omm0 + ei * 128 + g * 32 + dc * 4 + 4)

        def loads(sync):
            for e in elist:
                w1, w2, w3 = srcs(e)
                for g in range(4):
                    for ic in range(g * 8, g * 8 + 8):
                        stw12.load(sync, "sync", w1[:, ts(ic, 128)])
                        stw12.load(sync, "sync", w2[:, ts(ic, 128)])
                    for dc in range(DC):
                        stw3.load(sync, "sync", w3[ts(g, 1024), ts(dc, 128)])
        block.sync(loads)

        def casts(gp):
            for e in elist:
                for g in range(4):
                    for ic in range(8):
                        for _ in range(2):
                            jj = stw12.n_ct - w12_ct0
                            stw12.cast(gp, "gp",
                                       free_wait=w12_free_wait(jj - stw12.nbf))
                    for dc in range(DC):
                        jj = stw3.n_ct - w3_ct0
                        stw3.cast(gp, "gp",
                                  free_wait=w3_free_wait(jj - stw3.nbf))
        block.gpsimd(casts)

        for ei, e in enumerate(elist):
            w12_j0 = w12_call0 + ei * 64
            w3_j0 = w3_call0 + ei * 32
            ptr0 = s_ptr.n
            mrow0 = s_mrow.n
            mb0 = s_mb.n

            def pe_body(tensor, e=e, w12_j0=w12_j0, w3_j0=w3_j0,
                        ptr0=ptr0, mrow0=mrow0, mb0=mb0):
                ctx.wait(tensor, "pe", s_dve, xh_gate_dve)
                if e > 0:
                    ctx.wait(tensor, "pe", s_dve, routing_dve_end)
                    for nt in range(NT):
                        ctx.wait(tensor, "pe", s_mb, mb0 + nt)
                        for i, t in enumerate(range(nt * 4, nt * 4 + 4)):
                            col = t * NEXP + (e - 1)
                            ins = tensor.transpose(
                                pG[0:1, ts(i, 128)],
                                mask_cols[:, col: col + 1],
                                id_f32[:, :])
                        s_ptr.inc(tr_flush(tensor) or ins)
                        ctx.wait(tensor, "pe", s_mrow, mrow0 + nt + 1)
                        ins = tensor.matmul(pG[:, :], ones_f32_r[:, :],
                                            row_a[0:1, :], start=True, stop=True)
                        s_ptr.inc(ins)
                for g in range(4):
                    for icl in range(8):
                        ic = g * 8 + icl
                        jw1 = w12_j0 + ic * 2
                        jw2 = jw1 + 1
                        stw12.wait_ready(tensor, "pe", jw1)
                        stw12.wait_ready(tensor, "pe", jw2)
                        for nt in range(NT):
                            ga_idx = s_amm.n - amm0
                            bank = pA if ga_idx % 2 == 0 else pB
                            ctx.wait(tensor, "pe", s_gelu, gelu0 + ga_idx - 1)
                            for c in range(DC):
                                ins = tensor.matmul(
                                    bank[:, :],
                                    stw12.tile(jw1, c * 128, 128),
                                    xh[:, c * NTOK + nt * T: c * NTOK + (nt + 1) * T],
                                    start=(c == 0), stop=(c == DC - 1))
                            s_amm.inc(ins)
                            gb_idx = s_bmm.n - bmm0
                            bankb = pC if gb_idx % 2 == 0 else pD
                            ctx.wait(tensor, "pe", s_bev, bev0 + gb_idx - 1)
                            for c in range(DC):
                                ins = tensor.matmul(
                                    bankb[:, :],
                                    stw12.tile(jw2, c * 128, 128),
                                    xh[:, c * NTOK + nt * T: c * NTOK + (nt + 1) * T],
                                    start=(c == 0), stop=(c == DC - 1))
                            s_bmm.inc(ins)
                    ctx.wait(tensor, "pe", s_hm, hm0 + ei * 128 + (g + 1) * 32)
                    for dc in range(DC):
                        j3 = w3_j0 + g * 8 + dc
                        stw3.wait_ready(tensor, "pe", j3)
                        for nt in range(NT):
                            oidx = s_omm.n - omm0
                            bank = pE if oidx % 2 == 0 else pF
                            if oidx < 2:
                                ctx.wait(tensor, "pe", s_dve, routing_dve_end)
                            ctx.wait(tensor, "pe", s_oad, oad0 + oidx - 1)
                            for icl in range(8):
                                ins = tensor.matmul(
                                    bank[:, :],
                                    stw3.tile(j3, icl * 128, 128),
                                    hgrp[:, icl * NTOK + nt * T: icl * NTOK + (nt + 1) * T],
                                    start=(icl == 0), stop=(icl == 7))
                            s_omm.inc(ins)
            block.tensor(pe_body)

            def sc_body(scalar, e=e):
                for g in range(4):
                    for icl in range(8):
                        for nt in range(NT):
                            gi = s_gelu.n - gelu0
                            bank = pA if gi % 2 == 0 else pB
                            ctx.wait(scalar, "act", s_amm, amm0 + gi + 1)
                            ctx.wait(scalar, "act", s_hm, hm0 + gi - 2)
                            ins = scalar.activation(
                                out=ga_ring[:, ts(gi % 3, T)], in_=bank[:, :],
                                func=AF.Gelu, bias=0.0, scale=1.0)
                            s_gelu.inc(ins)
            block.scalar(sc_body)

            def ve_body(vector, e=e, ptr0=ptr0, mrow0=mrow0, mb0=mb0):
                if e > 0:
                    for nt in range(NT):
                        ctx.wait(vector, "dve", s_ptr, ptr0 + 2 * nt + 1)
                        ins = vector.tensor_copy(out=row_a[0:1, :],
                                                 in_=pG[0:1, 0:T])
                        s_mrow.inc(ins)
                        ctx.wait(vector, "dve", s_ptr, ptr0 + 2 * nt + 2)
                        ins = vector.tensor_copy(out=maskb[:, ts(nt, T)],
                                                 in_=pG[:, :])
                        s_mb.inc(ins)
                items = [(g, icl, nt) for g in range(4) for icl in range(8)
                         for nt in range(NT)]

                def emit_bev(i):
                    g, icl, nt = items[i]
                    bi = s_bev.n - bev0
                    bankb = pC if bi % 2 == 0 else pD
                    ctx.wait(vector, "dve", s_bmm, bmm0 + bi + 1)
                    ctx.wait(vector, "dve", s_hm, hm0 + bi - 2)
                    if e > 0:
                        ins = vector.tensor_mul(
                            bb_ring[:, ts(bi % 3, T)], bankb[:, :],
                            maskb[:, ts(nt, T)])
                    else:
                        ins = vector.tensor_copy(
                            out=bb_ring[:, ts(bi % 3, T)], in_=bankb[:, :])
                    s_bev.inc(ins)

                def emit_hm(i):
                    g, icl, nt = items[i]
                    hi = s_hm.n - hm0
                    if hi % 32 == 0:
                        ctx.wait(vector, "dve", s_omm, omm0 + (hi // 32) * 32)
                    ctx.wait(vector, "dve", s_gelu, gelu0 + hi + 1)
                    ctx.wait(vector, "dve", s_bev, bev0 + hi + 1)
                    ins = vector.tensor_mul(
                        hgrp[:, icl * NTOK + nt * T: icl * NTOK + (nt + 1) * T],
                        ga_ring[:, ts(hi % 3, T)], bb_ring[:, ts(hi % 3, T)])
                    s_hm.inc(ins)

                hm_next = [0]

                def emit_hm_upto(lim):
                    while hm_next[0] < lim:
                        emit_hm(hm_next[0])
                        hm_next[0] += 1

                for g in range(4):
                    for icl in range(8):
                        for nt in range(NT):
                            i = g * 32 + icl * 4 + nt
                            emit_bev(i)
                            emit_hm_upto(i)
                    emit_hm_upto((g + 1) * 32)
                    for dc in range(DC):
                        for nt in range(NT):
                            oi = s_oad.n - oad0
                            bank = pE if oi % 2 == 0 else pF
                            ctx.wait(vector, "dve", s_omm, omm0 + oi + 1)
                            xsl = x_sb[:, dc * NTOK + nt * T: dc * NTOK + (nt + 1) * T]
                            ins = vector.tensor_add(xsl, xsl, bank[:, :])
                            s_oad.inc(ins)
            block.vector(ve_body)

        lst.close()
        return s_oad.n

    def emit_attna(ln_gate_dve):
        if "w12" not in moe_res:
            moe_res["w12"] = Stream(ctx, "stw12", 1024, 2, 2, mcols=128)
            moe_res["w3"] = Stream(ctx, "stw3", 1024, 2, 2, mcols=128)
            moe_res["hgrp"] = ctx.sbuf("hgrp", [128, 8 * NTOK], BF16)
        stw12 = moe_res["w12"]
        stw3 = moe_res["w3"]
        hgrp = moe_res["hgrp"]

        lst = ExitStack()
        qa = ctx.sbuf("qa", [128, NTOK], BF16, stack=lst)
        ka = ctx.sbuf("ka", [128, NTOK], BF16, stack=lst)
        va_bf = ctx.sbuf("va_bf", [128, NTOK], BF16, stack=lst)
        ksq = ctx.sbuf("ksq", [128, NTOK], BF16, stack=lst)
        vtk = ctx.sbuf("vtk", [128, 16 * 129], BF16, stack=lst)
        eftile = ctx.sbuf("eftile", [128, 2 * 128], BF16, stack=lst)
        kvu = ctx.sbuf("kvu", [128, 8 * 129], F32, stack=lst)
        kveps = ctx.sbuf("kveps", [128, 8 * 129], F32, stack=lst)
        blkp = ctx.sbuf("blkp", [128, 128], BF16, stack=lst)
        hhalf = ctx.sbuf("hhalf", [128, 2], BF16, stack=lst)

        kvblk = ctx.sbuf("kvblk", [128, 128], BF16, stack=lst)
        sblk = ctx.sbuf("sblk", [128, 2], BF16, stack=lst)
        kvf = ctx.sbuf("kvf", [128, 129], F32, stack=lst)
        arb_s = ctx.sbuf("arb_s", [128, T], BF16, stack=lst)
        aav_s = ctx.sbuf("aav_s", [128, T], BF16, stack=lst)
        mxc = ctx.sbuf("mxc", [128, 4], F32, stack=lst)
        emxc = ctx.sbuf("emxc", [128, 1], F32, stack=lst)
        bcols = ctx.sbuf("bcols", [128, 4], F32, stack=lst)
        b1c = ctx.sbuf("b1c", [128, 32], F32, stack=lst)
        b2c = ctx.sbuf("b2c", [128, 8], F32, stack=lst)
        g1c = ctx.sbuf("g1c", [128, 8], F32, stack=lst)
        be1c = ctx.sbuf("be1c", [128, 8], F32, stack=lst)
        g2c = ctx.sbuf("g2c", [128, 8], F32, stack=lst)
        be2c = ctx.sbuf("be2c", [128, 8], F32, stack=lst)
        blkscr = nc.dram_tensor("blkscr", [128, 128], F32).ap()
        prst = tf32b[0:64, 0:64]
        prT = tf32b[0:64, 64:128]

        s_bp = ctx.sem("s_bp")
        s_bqe = ctx.sem("s_bqe")
        s_bvt = ctx.sem("s_bvt")
        s_bvk = ctx.sem("s_bvk")
        s_bsc = ctx.sem("s_bsc")
        s_adg = ctx.sem("s_adg")
        s_bxp = ctx.sem("s_bxp")
        s_bav = ctx.sem("s_bav")
        s_brc = ctx.sem("s_brc")
        s_amx = ctx.sem("s_amx")
        s_ptr = ctx.sem("s_ptr")
        s_brb = ctx.sem("s_brb")
        s_brbe = ctx.sem("s_brbe")
        s_bat = ctx.sem("s_bat")
        s_bas = ctx.sem("s_bas")
        s_akf = ctx.sem("s_akf")
        s_aep = ctx.sem("s_aep")
        s_arm = ctx.sem("s_arm")
        s_aqv = ctx.sem("s_aqv")
        s_blk = ctx.sem("s_blk")
        s_cc = ctx.sem("s_cc")
        s_ccd = ctx.sem("s_ccd")
        s_ccd2 = ctx.sem("s_ccd2")
        s_abc = ctx.sem("s_abc")
        s_acv = ctx.sem("s_acv")
        s_ace = ctx.sem("s_ace")
        s_osub = ctx.sem("s_osub")

        bp0 = s_bp.n
        bqe0 = s_bqe.n
        bvt0 = s_bvt.n
        bvk0 = s_bvk.n
        bsc0 = s_bsc.n
        adg0 = s_adg.n
        bxp0 = s_bxp.n
        bav0 = s_bav.n
        brc0 = s_brc.n
        amx0 = s_amx.n
        ptr0 = s_ptr.n
        brb0 = s_brb.n
        brbe0 = s_brbe.n
        bat0 = s_bat.n
        bas0 = s_bas.n
        akf0 = s_akf.n
        aep0 = s_aep.n
        arm0 = s_arm.n
        aqv0 = s_aqv.n
        LNR = math.log(RATIO)
        REPS = RATIO * EPS_KERN
        LVL = {"C": 0, "P": 1, "V": 2, "K": 3}.get(ATTNA_STOP, 9)


        def ca_gp(gp):
            ctx.wait(gp, "gp", s_dve, ln_gate_dve)
            if NOCONST:
                ins = gp.memset(hhalf[:, :], 0.0)
                s_gp.inc(ins)
                return
            ins = gp.memset(hhalf[:, :], 0.0)
            gp.memset(tf32a[:, 0:128], 0.0)
            oedge(gp, s_ogp, ins)
            gp.memset(hhalf[0:64, 0:1], 0.5)
            ins = gp.memset(hhalf[64:128, 1:2], 0.5)
            s_gp.inc(ins)
        block.gpsimd(ca_gp)
        gp_ca = s_gp.n

        def ca_dma1(sync):
            if NOCONST:
                return
            ctx.wait(sync, "sync", s_gp, gp_ca)
            sync.dma_start(out=blkscr, in_=tf32a[:, 0:128]).then_inc(s_blk.h, 16)
            s_blk.n += 16
            ins = sync.dma_start(out=prst, in_=W["proj"])
            s_blk.inc(ins, 16)
        block.sync(ca_dma1)

        def ca_pe1(tensor):
            if NOCONST:
                return
            ctx.wait(tensor, "pe", s_blk, s_blk.n)
            ctx.wait(tensor, "pe", s_dve, ln_gate_dve)
            ins = tensor.transpose(pC[0:64, 0:64], prst, id_f32[0:64, 0:64])
            s_ptr.inc(ins)
        block.tensor(ca_pe1)
        ptr_ca = s_ptr.n

        def ca_ve1(vector):
            if NOCONST:
                return
            ctx.wait(vector, "dve", s_ptr, ptr_ca)
            ins = vector.tensor_copy(out=prT, in_=pC[0:64, 0:64])
            s_dve.inc(ins)
        block.vector(ca_ve1)
        dve_ca = s_dve.n

        def ca_dma2(sync):
            if NOCONST:
                return
            ctx.wait(sync, "sync", s_dve, dve_ca)
            sync.dma_start(out=blkscr[0:64, 0:64],
                           in_=prT).then_inc(s_blk.h, 16)
            s_blk.n += 16
            sync.wait_ge(s_blk.h, s_blk.n)
            sync.dma_start(out=blkscr[64:128, 64:128],
                           in_=prT).then_inc(s_blk.h, 16)
            s_blk.n += 16
            sync.wait_ge(s_blk.h, s_blk.n)
            ins = sync.dma_start(out=tf32a[:, 0:128], in_=blkscr)
            s_blk.inc(ins, 16)
        block.sync(ca_dma2)

        def ca_gp2(gp):
            if NOCONST:
                ins = gp.memset(blkp[:, :], 0.0)
                s_gp.inc(ins)
                return
            ctx.wait(gp, "gp", s_blk, s_blk.n)
            ins = gp.tensor_copy(out=blkp[:, :], in_=tf32a[:, 0:128])
            s_gp.inc(ins)
        block.gpsimd(ca_gp2)
        gp_blk = s_gp.n

        qkv_j0 = stw12.n_ld
        wo_j0 = stw3.n_ld
        qkv_ct0 = stw12.n_ct
        wo_ct0 = stw3.n_ct
        omm_wo0 = [None]

        def a_loads(sync):
            if LVL < 1:
                return
            if SERIAL_A:
                ctx.wait(sync, "sync", s_dve, ln_gate_dve)
            for hp in range(N_PAIRS):
                for wn in ("Wq", "Wk", "Wv"):
                    ap2 = W[wn][:, 2 * hp: 2 * hp + 2, :].rearrange(
                        "d h k -> d (h k)")
                    stw12.load(sync, "sync", ap2)
        block.sync(a_loads)

        def a_casts(gp):
            if LVL < 1:
                return
            for hp in range(N_PAIRS):
                for w in range(3):
                    jj = stw12.n_ct - qkv_ct0
                    if jj - stw12.nbf < 0:
                        fw = None if stw12.n_ct < stw12.nbf else (s_bp, bp0)
                    else:
                        pj = jj - stw12.nbf
                        php, pw = divmod(pj, 3)
                        fw = (s_bp, bp0 + php * 12 + pw * 4 + 4)
                    stw12.cast(gp, "gp", scale=DN if w < 2 else None,
                               free_wait=fw)
        block.gpsimd(a_casts)

        ptr_pa = s_ptr.n - ptr0
        bev_a0 = s_bev.n
        ksq_dve = {}
        qsq_dve = {}
        for hp in range(N_PAIRS):
            per_pair_dd = 32 if LVL >= 4 else 16
            kb = bsc0 + hp * per_pair_dd
            qdb = kb + TT16
            kdg = adg0 + hp * per_pair_dd
            qdg = kdg + TT16
            kxp = bxp0 + hp * 64
            qxp = kxp + 32
            krc = brc0 + hp * 32
            qrc = krc + TT16

            def paA_pe(tensor, hp=hp, kb=kb, qdb=qdb, kdg=kdg, qdg=qdg,
                       kxp=kxp, qxp=qxp, krc=krc, qrc=qrc):
                if LVL < 1:
                    return
                if hp == 0:
                    ctx.wait(tensor, "pe", s_dve, ln_gate_dve)
                    ctx.wait(tensor, "pe", s_gp, gp_blk)
                if LVL >= 4:
                    ctx.wait(tensor, "pe", s_aqv, aqv0 + hp * 16)
                for w in range(3):
                    j = qkv_j0 + hp * 3 + w
                    stw12.wait_ready(tensor, "pe", j)
                    for nt in range(NT):
                        pr = s_bp.n - bp0
                        bank = pA if pr % 2 == 0 else pB
                        ctx.wait(tensor, "pe", s_bqe, bqe0 + pr - 1)
                        for c in range(DC):
                            ins = tensor.matmul(
                                bank[:, :], stw12.tile(j, c * 128, 128),
                                xh[:, c * NTOK + nt * T: c * NTOK + (nt + 1) * T],
                                start=(c == 0), stop=(c == DC - 1))
                        s_bp.inc(ins)
                if LVL < 2:
                    return
                if "vtr" not in ATTNA_KSKIP:
                    ctx.wait(tensor, "pe", s_bqe, bqe0 + hp * 12 + 12)
                    for ck in range(TT16):
                        ctx.wait(tensor, "pe", s_bvk,
                                 bvk0 + hp * 33 + 1 + 2 * ck + 1)
                        if ck >= 2:
                            ctx.wait(tensor, "pe", s_bvk,
                                     bvk0 + hp * 33 + 1 + 2 * (ck - 1))
                        bank = pC if ck % 2 == 0 else pD
                        ins = tensor.transpose(bank[0:128, 0:128],
                                               tf32b[:, ts(ck % 2, 128)],
                                               id_f32[:, :])
                        s_bvt.inc(ins)
                if LVL < 3:
                    return
                if "vtr" not in ATTNA_KSKIP:
                    ctx.wait(tensor, "pe", s_bvk, bvk0 + hp * 33 + 33)
                if hp >= 1 and "kv" not in ATTNA_KSKIP:
                    akf_pp = 3 if LVL >= 4 else 2
                    ctx.wait(tensor, "pe", s_akf, akf0 + (hp - 1) * akf_pp + 2)
                for tt in range(TT16):
                    d = kb + tt
                    dbank = (pC, pD, pI)[(d - bsc0) % 3]
                    if "dd" not in ATTNA_KSKIP:
                        if "acts" not in ATTNA_KSKIP:
                            ctx.wait(tensor, "pe", s_bxp,
                                     2 * (d - bsc0) - 4 + bxp0)
                        if "diag" not in ATTNA_KSKIP:
                            ctx.wait(tensor, "pe", s_brc, brc0 + (d - bsc0) - 2)
                        if tt >= 3 and "rm" not in ATTNA_KSKIP and \
                                "justrm" not in ATTNA_KSKIP:
                            ctx.wait(tensor, "pe", s_amx,
                                     amx0 + hp * 16 + tt - 2)
                        if "ksq" not in ATTNA_KSKIP:
                            bev_pp = 8 if LVL >= 4 else 4
                            ctx.wait(tensor, "pe", s_bev,
                                     bev_a0 + hp * bev_pp + 4)
                        dd_mv = id_bf[:, :] if "blkid" in ATTNA_KSKIP \
                            else blkp[:, :]
                        ins = tensor.matmul(dbank[:, 0:128], ka[:, ts(tt, 128)],
                                            dd_mv, start=True, stop=True)
                        s_bsc.inc(ins)
                    if "diag" not in ATTNA_KSKIP:
                        ctx.wait(tensor, "pe", s_brc, brc0 + (d - bsc0))
                        ins = tensor.matmul(pG[0:128, 0:2], ksq[:, ts(tt, 128)],
                                            hhalf[:, :], start=True, stop=True)
                        s_adg.inc(ins)
                    if "kv" not in ATTNA_KSKIP and tt >= 1:
                        t1 = tt - 1
                        ctx.wait(tensor, "pe", s_bxp,
                                 bxp0 + 2 * (kb + t1 - bsc0) + 2)
                        tensor.matmul(pE[:, 0:129], eftile[:, ts(t1 % 2, 128)],
                                      vtk[:, t1 * 129:(t1 + 1) * 129],
                                      start=(t1 == 0), stop=False)
                        tensor.matmul(pF[0:1, 0:129], ones_bf_c[:, :],
                                      vtk[:, t1 * 129:(t1 + 1) * 129],
                                      start=(t1 == 0), stop=False)
                if "kv" not in ATTNA_KSKIP:
                    t1 = TT16 - 1
                    ctx.wait(tensor, "pe", s_bxp,
                             bxp0 + 2 * (kb + t1 - bsc0) + 2)
                    tensor.matmul(pE[:, 0:129], eftile[:, ts(t1 % 2, 128)],
                                  vtk[:, t1 * 129:(t1 + 1) * 129],
                                  start=False, stop=True)
                    ins = tensor.matmul(pF[0:1, 0:129], ones_bf_c[:, :],
                                        vtk[:, t1 * 129:(t1 + 1) * 129],
                                        start=False, stop=True)
                    s_bav.inc(ins)
                if LVL < 4:
                    return
                ctx.wait(tensor, "pe", s_akf, akf0 + hp * 3 + 2)
                ins = tensor.matmul(pF[:, 0:129], ones_f32_r[:, :],
                                    row_b[0:1, 0:129], start=True, stop=True)
                s_brb.inc(ins)
                ctx.wait(tensor, "pe", s_amx, amx0 + (hp + 1) * 16)
                for tt in range(TT16):
                    d = qdb + tt
                    dbank = (pC, pD, pI)[(d - bsc0) % 3]
                    ctx.wait(tensor, "pe", s_bev, bev_a0 + hp * 8 + 8)
                    ctx.wait(tensor, "pe", s_bxp, bxp0 + 2 * (d - bsc0) - 4)
                    ctx.wait(tensor, "pe", s_brc, brc0 + (d - bsc0) - 2)
                    ins = tensor.matmul(dbank[:, 0:128], qa[:, ts(tt, 128)],
                                        blkp[:, :], start=True, stop=True)
                    s_bsc.inc(ins)
                    ctx.wait(tensor, "pe", s_brc, brc0 + (d - bsc0))
                    ins = tensor.matmul(pG[0:128, 0:2], ksq[:, ts(tt, 128)],
                                        hhalf[:, :], start=True, stop=True)
                    s_adg.inc(ins)
                    if tt >= 1:
                        qi = hp * 16 + tt - 1
                        ctx.wait(tensor, "pe", s_aep, aep0 + qi + 1)
                        ctx.wait(tensor, "pe", s_aqv, aqv0 + qi - 1)
                        bank = pA if (tt - 1) % 2 == 0 else pB
                        ins = tensor.transpose(bank[0:128, 0:128],
                                               tf32a[:, ts((tt - 1) % 2, 128)],
                                               id_f32[:, :])
                        s_ptr.inc(ins)
                qi = hp * 16 + 15
                ctx.wait(tensor, "pe", s_aep, aep0 + qi + 1)
                ctx.wait(tensor, "pe", s_aqv, aqv0 + qi - 1)
                ins = tensor.transpose(pB[0:128, 0:128],
                                       tf32a[:, ts(15 % 2, 128)], id_f32[:, :])
                s_ptr.inc(ins)
            block.tensor(paA_pe)

            def paA_sc(scalar, hp=hp, kb=kb, qdb=qdb, kxp=kxp, qxp=qxp,
                       krc=krc, qrc=qrc):
                if LVL < 1:
                    return
                for w in range(3):
                    for nt in range(NT):
                        pr = s_bqe.n - bqe0
                        bank = pA if pr % 2 == 0 else pB
                        ctx.wait(scalar, "act", s_bp, bp0 + pr + 1)
                        if w == 0:
                            if "dd" in ATTNA_KSKIP:
                                pass
                            elif LVL >= 4:
                                ctx.wait(scalar, "act", s_bsc, bsc0 + hp * 32)
                            elif LVL == 3:
                                ctx.wait(scalar, "act", s_bsc, bsc0 + hp * 16)
                            if hp >= 1 and LVL >= 4:
                                ctx.wait(scalar, "act", s_bev, qsq_dve[hp - 1])
                            ins = scalar.copy(out=qa[:, ts(nt, T)],
                                              in_=bank[:, :])
                        elif w == 1:
                            if hp >= 1 and LVL >= 3 and \
                                    "ksq" not in ATTNA_KSKIP:
                                ctx.wait(scalar, "act", s_bev, ksq_dve[hp - 1])
                            ins = scalar.copy(out=ka[:, ts(nt, T)],
                                              in_=bank[:, :])
                        else:
                            if LVL >= 2 and hp >= 1 and \
                                    "vtr" not in ATTNA_KSKIP:
                                ctx.wait(scalar, "act", s_bvk,
                                         bvk0 + hp * 33)
                            ins = scalar.copy(out=va_bf[:, ts(nt, T)],
                                              in_=bank[:, :])
                        s_bqe.inc(ins)
                if LVL < 3:
                    return
                for tt in range(TT16):
                    if "acts" in ATTNA_KSKIP:
                        break
                    d = kb + tt - bsc0
                    dbank = (pC, pD, pI)[d % 3]
                    if "dd" not in ATTNA_KSKIP:
                        ctx.wait(scalar, "act", s_bsc, bsc0 + d + 1)
                    if "rm" not in ATTNA_KSKIP and \
                            "justrm" not in ATTNA_KSKIP and \
                            "cmb" not in ATTNA_KSKIP:
                        ctx.wait(scalar, "act", s_arm, arm0 + hp * 16 + tt + 1)
                    if "diag" not in ATTNA_KSKIP:
                        ctx.wait(scalar, "act", s_brc, brc0 + d + 1)
                        b0, b1 = bcols[:, 0:1], bcols[:, 1:2]
                    else:
                        b0, b1 = cst[:, 0:1], cst[:, 0:1]
                    kaf = AF.Copy if "actcp" in ATTNA_KSKIP else AF.Exp
                    if "actb0" in ATTNA_KSKIP or "actcp" in ATTNA_KSKIP:
                        b0, b1 = 0.0, 0.0
                    scalar.activation(out=eftile[:, (tt % 2) * 128:(tt % 2) * 128 + 64],
                                      in_=dbank[:, 0:64], func=kaf,
                                      bias=b0, scale=1.0)
                    ins = scalar.activation(
                        out=eftile[:, (tt % 2) * 128 + 64:(tt % 2) * 128 + 128],
                        in_=dbank[:, 64:128], func=kaf,
                        bias=b1, scale=1.0)
                    s_bxp.inc(ins, 2)
                if LVL < 4:
                    return
                for tt in range(TT16):
                    d = qdb + tt - bsc0
                    dbank = (pC, pD, pI)[d % 3]
                    ctx.wait(scalar, "act", s_bsc, bsc0 + d + 1)
                    ctx.wait(scalar, "act", s_brc, brc0 + d + 1)
                    if hp * 16 + tt >= 2:
                        ctx.wait(scalar, "act", s_ptr,
                                 ptr0 + ptr_pa + hp * 16 + tt - 1)
                    scalar.activation(out=eftile[:, (tt % 2) * 128:(tt % 2) * 128 + 64],
                                      in_=dbank[:, 0:64], func=AF.Exp,
                                      bias=bcols[:, 0:1], scale=1.0)
                    ins = scalar.activation(
                        out=eftile[:, (tt % 2) * 128 + 64:(tt % 2) * 128 + 128],
                        in_=dbank[:, 64:128], func=AF.Exp,
                        bias=bcols[:, 1:2], scale=1.0)
                    s_bxp.inc(ins, 2)
            block.scalar(paA_sc)

            def paA_ve(vector, hp=hp, kb=kb, qdb=qdb):
                if LVL < 2:
                    return
                if "vtr" not in ATTNA_KSKIP:
                    if LVL >= 3 and "kv" not in ATTNA_KSKIP:
                        ctx.wait(vector, "dve", s_bav, bav0 + hp)
                    ctx.wait(vector, "dve", s_bqe, bqe0 + hp * 12 + 12)
                    ins = vector.memset(vtk[:, :], 1.0)
                    s_bvk.inc(ins)
                    for ck in range(TT16):
                        if ck >= 2:
                            ctx.wait(vector, "dve", s_bvt,
                                     bvt0 + hp * 16 + ck - 1)
                        ins = vector.tensor_copy(out=tf32b[:, ts(ck % 2, 128)],
                                                 in_=va_bf[:, ts(ck, 128)])
                        s_bvk.inc(ins)
                        ctx.wait(vector, "dve", s_bvt, bvt0 + hp * 16 + ck + 1)
                        bank = pC if ck % 2 == 0 else pD
                        ins = vector.tensor_copy(
                            out=vtk[:, ck * 129: ck * 129 + 128],
                            in_=bank[0:128, 0:128])
                        s_bvk.inc(ins)
                if LVL < 3:
                    return
                ctx.wait(vector, "dve", s_bqe, bqe0 + hp * 12 + 8)
                if "diag" not in ATTNA_KSKIP:
                    if LVL >= 4:
                        ctx.wait(vector, "dve", s_adg, adg0 + hp * 32)
                    else:
                        ctx.wait(vector, "dve", s_adg, adg0 + hp * 16)
                for nt in range(NT):
                    if "ksq" in ATTNA_KSKIP:
                        break
                    ins = vector.tensor_mul(ksq[:, ts(nt, T)], ka[:, ts(nt, T)],
                                            ka[:, ts(nt, T)])
                    s_bev.inc(ins)
                ksq_dve[hp] = s_bev.n
                def combine(tt2, hp=hp):
                    if tt2 < 0:
                        return None
                    sl = mxc[:, 1 + tt2 % 2: 2 + tt2 % 2]
                    ctx.wait(vector, "dve", s_arm, arm0 + hp * 16 + tt2 + 1)
                    ctx.wait(vector, "dve", s_amx, amx0 + hp * 16 + tt2)
                    if tt2 == 0 and hp == 0:
                        return vector.tensor_copy(out=mxc[:, 0:1], in_=sl)
                    return vector.tensor_max(mxc[:, 0:1], mxc[:, 0:1], sl)

                for tt in range(TT16):
                    d = kb + tt - bsc0
                    dbank = (pC, pD, pI)[d % 3]
                    if "rm" not in ATTNA_KSKIP and "cmb" not in ATTNA_KSKIP:
                        ins = combine(tt - 2)
                        if ins is not None:
                            s_amx.inc(ins)
                    if "diag" not in ATTNA_KSKIP:
                        ctx.wait(vector, "dve", s_adg, adg0 + d + 1)
                        ctx.wait(vector, "dve", s_bxp, bxp0 + 2 * d)
                        vector.tensor_scalar(out=bcols[:, 0:1], in0=pG[0:128, 0:1],
                                             scalar1=-1.0, scalar2=LNR,
                                             op0=ALU.mult, op1=ALU.add)
                        ins = vector.tensor_scalar(out=bcols[:, 1:2],
                                                   in0=pG[0:128, 1:2],
                                                   scalar1=-1.0, scalar2=LNR,
                                                   op0=ALU.mult, op1=ALU.add)
                        s_brc.inc(ins)
                    if "rm" not in ATTNA_KSKIP and \
                            "justrm" not in ATTNA_KSKIP:
                        ctx.wait(vector, "dve", s_bsc, bsc0 + d + 1)
                        ctx.wait(vector, "dve", s_amx, amx0 + hp * 16 + tt - 1)
                        ins = vector.reduce_max(mxc[:, 1 + tt % 2: 2 + tt % 2],
                                                dbank[:, 0:128], axis=AX.X)
                        if "cmb" in ATTNA_KSKIP:
                            s_amx.inc(ins)
                        else:
                            s_arm.inc(ins)
                if "rm" not in ATTNA_KSKIP and "cmb" not in ATTNA_KSKIP:
                    s_amx.inc(combine(TT16 - 2))
                    s_amx.inc(combine(TT16 - 1))
                if "kv" not in ATTNA_KSKIP:
                    ctx.wait(vector, "dve", s_bav, bav0 + hp + 1)
                    ins = vector.tensor_copy(out=kvu[:, hp * 129:(hp + 1) * 129],
                                             in_=pE[:, 0:129])
                    s_akf.inc(ins)
                    ins = vector.tensor_scalar_mul(row_b[0:1, 0:129],
                                                   pF[0:1, 0:129], REPS)
                    s_akf.inc(ins)
                if LVL < 4:
                    return
                ctx.wait(vector, "dve", s_bqe, bqe0 + hp * 12 + 4)
                ctx.wait(vector, "dve", s_adg, adg0 + hp * 32 + 16)
                for nt in range(NT):
                    ins = vector.tensor_mul(ksq[:, ts(nt, T)], qa[:, ts(nt, T)],
                                            qa[:, ts(nt, T)])
                    s_bev.inc(ins)
                qsq_dve[hp] = s_bev.n
                for tt in range(TT16):
                    d = qdb + tt - bsc0
                    dbank = (pC, pD, pI)[d % 3]
                    ctx.wait(vector, "dve", s_adg, adg0 + d + 1)
                    ctx.wait(vector, "dve", s_bsc, bsc0 + d + 1)
                    ctx.wait(vector, "dve", s_brc, brc0 + d)
                    vector.reduce_max(mxc[:, 1:2], dbank[:, 0:64], axis=AX.X)
                    ins = vector.reduce_max(mxc[:, 2:3], dbank[:, 64:128],
                                            axis=AX.X)
                    oedge(vector, s_odv, ins)
                    vector.tensor_add(mxc[:, 3:4], pG[0:128, 0:1], mxc[:, 1:2])
                    ins = vector.tensor_add(bcols[:, 2:3], pG[0:128, 1:2],
                                            mxc[:, 2:3])
                    oedge(vector, s_odv, ins)
                    ctx.wait(vector, "dve", s_bxp, bxp0 + 2 * d)
                    vector.tensor_scalar(out=bcols[:, 0:1], in0=mxc[:, 3:4],
                                         scalar1=-1.0, scalar2=LNR,
                                         op0=ALU.mult, op1=ALU.add)
                    ins = vector.tensor_scalar(out=bcols[:, 1:2],
                                               in0=bcols[:, 2:3],
                                               scalar1=-1.0, scalar2=LNR,
                                               op0=ALU.mult, op1=ALU.add)
                    s_brc.inc(ins)
                    ctx.wait(vector, "dve", s_bxp, bxp0 + 2 * (d + 1))
                    if tt >= 2:
                        ctx.wait(vector, "dve", s_ptr,
                                 ptr0 + ptr_pa + hp * 16 + tt - 1)
                    ins = vector.tensor_scalar_add(tf32a[:, ts(tt % 2, 128)],
                                                   eftile[:, ts(tt % 2, 128)],
                                                   REPS)
                    s_aep.inc(ins)
                    if tt >= 1:
                        qi = hp * 16 + tt - 1
                        ctx.wait(vector, "dve", s_ptr, ptr0 + ptr_pa + qi + 1)
                        bank = pA if (tt - 1) % 2 == 0 else pB
                        ins = vector.tensor_copy(
                            out=hgrp[:, hp * NTOK + (tt - 1) * 128:
                                     hp * NTOK + tt * 128],
                            in_=bank[0:128, 0:128])
                        s_aqv.inc(ins)
                qi = hp * 16 + 15
                ctx.wait(vector, "dve", s_ptr, ptr0 + ptr_pa + qi + 1)
                ins = vector.tensor_copy(
                    out=hgrp[:, hp * NTOK + 15 * 128: hp * NTOK + 16 * 128],
                    in_=pB[0:128, 0:128])
                s_aqv.inc(ins)
                ctx.wait(vector, "dve", s_brb, brb0 + hp + 1)
                ins = vector.tensor_copy(out=kveps[:, hp * 129:(hp + 1) * 129],
                                         in_=pF[:, 0:129])
                s_akf.inc(ins)
            block.vector(paA_ve)

        COLVECS = [("ab1", b1c, 32), ("ab2", b2c, 8), ("ln1_g", g1c, 8),
                   ("ln1_b", be1c, 8), ("ln2_g", g2c, 8), ("ln2_b", be2c, 8)]
        if ATTNA_STOP in ("A0", "C", "P", "V", "K"):
            lst.close()
            return s_dve.n
        amx_end = s_amx.n

        def cc_pe(tensor):
            ctx.wait(tensor, "pe", s_amx, amx_end)
            ctx.wait(tensor, "pe", s_bxp, bxp0 + 512)
            ctx.wait(tensor, "pe", s_brc, brc0 + 256)
            ctx.wait(tensor, "pe", s_aep, aep0 + 128)
            ins = tensor.transpose(pC[0:1, 0:128], mxc[:, 0:1], id_f32[:, :])
            s_ptr.inc(ins)
        block.tensor(cc_pe)
        ptr_cc = s_ptr.n

        def cc_ve(vector):
            ctx.wait(vector, "dve", s_ptr, ptr_cc)
            ins = vector.reduce_max(mxc[0:1, 3:4], pC[0:1, 0:128], axis=AX.X)
            s_dve.inc(ins)
        block.vector(cc_ve)
        dve_cc = s_dve.n

        def cc_dma1(sync):
            ctx.wait(sync, "sync", s_dve, dve_cc)
            ins = sync.dma_start(out=cc_in, in_=mxc[0:1, 3:4])
            s_ccd.inc(ins, 16)
            gi = 0
            for name, buf, nb in COLVECS:
                for g4 in range(nb // 4):
                    ctx.wait(sync, "sync", s_acv, 4 * gi)
                    ins = sync.dma_start(
                        out=iotile[0:1, 0:512],
                        in_=W[name][ts(g4, 512)].rearrange(
                            "(one m) -> one m", one=1))
                    s_abc.inc(ins, 16)
                    gi += 1
            if ATTNA_STOP != "A":
                wo2 = W["Wo"].rearrange("h k d -> (h k) d")
                for dc in range(DC):
                    stw3.load(sync, "sync", wo2[:, ts(dc, 128)])
        block.sync(cc_dma1)

        def col_pe(tensor):
            gi = 0
            bi = 0
            for name, buf, nb in COLVECS:
                for g4 in range(nb // 4):
                    ctx.wait(tensor, "pe", s_abc, 16 * (gi + 1))
                    ctx.wait(tensor, "pe", s_brc, brc0 + 256)
                    if bi >= 4:
                        ctx.wait(tensor, "pe", s_ace, bi - 3)
                    for c in range(4):
                        ctx.wait(tensor, "pe", s_ace, bi - 3)
                        ins = tensor.transpose(
                            pG[0:128, (bi % 4):(bi % 4) + 1],
                            iotile[0:1, c * 128:(c + 1) * 128],
                            id_f32[0:1, 0:1])
                        s_acv.inc(ins)
                        bi += 1
                    gi += 1
        block.tensor(col_pe)

        def col_ve(vector):
            bi = 0
            for name, buf, nb in COLVECS:
                for g4 in range(nb // 4):
                    for c in range(4):
                        ctx.wait(vector, "dve", s_acv, bi + 1)
                        col = g4 * 4 + c
                        ins = vector.tensor_copy(
                            out=buf[:, col:col + 1],
                            in_=pG[0:128, (bi % 4):(bi % 4) + 1])
                        s_ace.inc(ins)
                        bi += 1
        block.vector(col_ve)
        ace_end = s_ace.n

        def cc_gp(gp):
            ctx.wait(gp, "gp", s_ccd, s_ccd.n)
            ins = gp.collective_compute(
                "AllReduce", ALU.max, replica_groups=[list(range(8))],
                ins=[cc_in], outs=[cc_out])
            s_cc.inc(ins)
            gp.wait_ge(s_cc.h, s_cc.n)
            ins = gp.dma_start(out=mxc[0:1, 3:4], in_=cc_out)
            s_ccd2.inc(ins, 16)
        block.gpsimd(cc_gp)

        def cc_pe2(tensor):
            ctx.wait(tensor, "pe", s_ccd2, s_ccd2.n)
            ins = tensor.matmul(pD[:, 0:1], ones_f32_r[:, :], mxc[0:1, 3:4],
                                start=True, stop=True)
            s_brb.inc(ins)
        block.tensor(cc_pe2)
        brb_cc = s_brb.n

        def cc_sc(scalar):
            ctx.wait(scalar, "act", s_brb, brb_cc)
            ins = scalar.activation(out=emxc[:, :], in_=pD[:, 0:1],
                                    func=AF.Exp, bias=0.0, scale=-1.0)
            s_act.inc(ins)
        block.scalar(cc_sc)
        act_cc = s_act.n

        COLVECS_DONE = True
        if ATTNA_STOP == "A":
            lst.close()
            return s_aqv.n if False else s_dve.n
        for hp in range(8):
            def paB_ve1(vector, hp=hp):
                if hp == 0:
                    ctx.wait(vector, "dve", s_act, act_cc)
                else:
                    ctx.wait(vector, "dve", s_bsc,
                             bsc0 + 256 + hp * NT)
                    ctx.wait(vector, "dve", s_adg,
                             adg0 + 256 + hp * 2 * NT)
                ins = vector.scalar_tensor_tensor(
                    out=kvf[:, :], in0=kvu[:, hp * 129:(hp + 1) * 129],
                    scalar=emxc[:, 0:1], in1=kveps[:, hp * 129:(hp + 1) * 129],
                    op0=ALU.mult, op1=ALU.add)
                s_dve.inc(ins)
                vector.memset(kvblk[:, :], 0.0)
                ins = vector.memset(sblk[:, :], 0.0)
                oedge(vector, s_odv, ins)
                vector.tensor_copy(out=kvblk[0:64, 0:64], in_=kvf[0:64, 0:64])
                vector.tensor_copy(out=kvblk[64:128, 64:128],
                                   in_=kvf[64:128, 64:128])
                vector.tensor_copy(out=sblk[0:64, 0:1], in_=kvf[0:64, 128:129])
                ins = vector.tensor_copy(out=sblk[64:128, 1:2],
                                         in_=kvf[64:128, 128:129])
                s_akf.inc(ins)
            block.vector(paB_ve1)
            akf_b = s_akf.n

            def paB_pe(tensor, hp=hp, akf_b=akf_b):
                ctx.wait(tensor, "pe", s_akf, akf_b)
                for nt in range(NT):
                    i = hp * NT + nt
                    ctx.wait(tensor, "pe", s_brc, brc0 + 256 + 2 * i)
                    ins = tensor.matmul(pF[0:1, :], sblk[:, 0:1],
                                        hgrp[:, hp * NTOK + nt * T:
                                             hp * NTOK + (nt + 1) * T],
                                        start=True, stop=True)
                    s_adg.inc(ins)
                    ins = tensor.matmul(pG[0:1, :], sblk[:, 1:2],
                                        hgrp[:, hp * NTOK + nt * T:
                                             hp * NTOK + (nt + 1) * T],
                                        start=True, stop=True)
                    s_adg.inc(ins)
                    nbank = pC if i % 2 == 0 else pD
                    ctx.wait(tensor, "pe", s_bas, bas0 + i - 1)
                    ins = tensor.matmul(nbank[:, :], kvblk[:, :],
                                        hgrp[:, hp * NTOK + nt * T:
                                             hp * NTOK + (nt + 1) * T],
                                        start=True, stop=True)
                    s_bsc.inc(ins)
                    ctx.wait(tensor, "pe", s_brc, brc0 + 256 + 2 * i + 2)
                    ctx.wait(tensor, "pe", s_brbe, brbe0 + 2 * i - 2 + 2)
                    tensor.matmul(pE[0:64, :], ones_f32_r[0:1, 0:64],
                                  row_b[0:1, 0:T], start=True, stop=True)
                    ins = tensor.matmul(pA[64:128, :], ones_f32_r[0:1, 0:64],
                                        row_c[0:1, 0:T], start=True, stop=True,
                                        tile_position=(0, 64))
                    s_brb.inc(ins)
            block.tensor(paB_pe)

            def paB_sc(scalar, hp=hp):
                for nt in range(NT):
                    i = hp * NT + nt
                    nbank = pC if i % 2 == 0 else pD
                    ctx.wait(scalar, "act", s_bsc, bsc0 + 256 + i + 1)
                    ctx.wait(scalar, "act", s_bat, bat0 + i)
                    ins = scalar.copy(out=aav_s[:, :], in_=nbank[:, :])
                    s_bas.inc(ins)
            block.scalar(paB_sc)

            def paB_ve2(vector, hp=hp):
                for nt in range(NT):
                    i = hp * NT + nt
                    ctx.wait(vector, "dve", s_adg, adg0 + 256 + 2 * i + 2)
                    vector.reciprocal(row_b[0:1, 0:T], pF[0:1, :])
                    ins = vector.reciprocal(row_c[0:1, 0:T], pG[0:1, :])
                    s_brc.inc(ins, 2)
                    ctx.wait(vector, "dve", s_brb, brb0 + 8 + 1 + i + 1)
                    ctx.wait(vector, "dve", s_bat, bat0 + i)
                    vector.tensor_copy(out=arb_s[0:64, :], in_=pE[0:64, :])
                    ins = vector.tensor_copy(out=arb_s[64:128, :],
                                             in_=pA[64:128, :])
                    s_brbe.inc(ins, 2)
                    vector.wait_ge(s_brbe.h, s_brbe.n)
                    ctx.wait(vector, "dve", s_bas, bas0 + i + 1)
                    ins = vector.tensor_mul(
                        hgrp[:, hp * NTOK + nt * T: hp * NTOK + (nt + 1) * T],
                        aav_s[:, :], arb_s[:, :])
                    s_bat.inc(ins)
            block.vector(paB_ve2)

        bat_end = s_bat.n
        bqe_end = s_bqe.n
        omm0 = s_omm.n
        oad0 = s_oad.n

        def wo_casts(gp):
            for dc in range(DC):
                jj = stw3.n_ct - wo_ct0
                pj = jj - stw3.nbf
                fw = None if pj < 0 else (s_omm, omm0 + pj * 4 + 4)
                stw3.cast(gp, "gp", free_wait=fw)
        block.gpsimd(wo_casts)

        def wo_pe(tensor):
            ctx.wait(tensor, "pe", s_bat, bat_end)
            for dc in range(DC):
                j = wo_j0 + dc
                stw3.wait_ready(tensor, "pe", j)
                for nt in range(NT):
                    oidx = s_omm.n - omm0
                    bank = pA if oidx % 2 == 0 else pB
                    ctx.wait(tensor, "pe", s_oad, oad0 + oidx - 1)
                    for c in range(DC):
                        ins = tensor.matmul(
                            bank[:, :], stw3.tile(j, c * 128, 128),
                            hgrp[:, c * NTOK + nt * T: c * NTOK + (nt + 1) * T],
                            start=(c == 0), stop=(c == DC - 1))
                    s_omm.inc(ins)
        block.tensor(wo_pe)

        def wo_ve(vector):
            for dc in range(DC):
                for nt in range(NT):
                    oidx = s_oad.n - oad0
                    bank = pA if oidx % 2 == 0 else pB
                    ctx.wait(vector, "dve", s_omm, omm0 + oidx + 1)
                    xsl = xh[:, dc * NTOK + nt * T: dc * NTOK + (nt + 1) * T]
                    ins = vector.tensor_add(xsl, xsl, bank[:, :])
                    s_oad.inc(ins)
        block.vector(wo_ve)

        def colgate(vector):
            ctx.wait(vector, "dve", s_ace, ace_end)
        block.vector(colgate)
        ln1_end = emit_ln(xh, xh, EPS_LN_IN, s_oad, s_oad.n,
                          g_cols=g1c, b_cols=be1c)

        iw1_j0 = stw12.n_ld
        iw2_j0 = stw3.n_ld
        iw1_ct0 = stw12.n_ct
        iw2_ct0 = stw3.n_ct
        amm_i0 = s_amm.n
        omm_i0 = s_omm.n
        osub_i0 = s_osub.n
        brbe_end_a = s_brbe.n
        gelu_i0 = s_gelu.n
        bev_i0 = s_bev.n
        hm_i0 = s_hm.n
        oad_i0 = s_oad.n

        def if_loads(sync):
            for nt in range(NT):
                for ic in range(32):
                    stw12.load(sync, "sync", W["aW1"][:, ts(ic, 128)])
                for dc in range(DC):
                    for g in range(4):
                        stw3.load(sync, "sync",
                                  W["aW2"][ts(g, 1024), ts(dc, 128)])
        block.sync(if_loads)

        def if_casts(gp):
            for nt in range(NT):
                for ic in range(32):
                    jj = stw12.n_ct - iw1_ct0
                    pj = jj - stw12.nbf
                    if pj < 0:
                        fw = (s_bp, bp0 + 96)
                    else:
                        fw = (s_amm, amm_i0 + pj + 1)
                    stw12.cast(gp, "gp", free_wait=fw)
                for k in range(32):
                    jj = stw3.n_ct - iw2_ct0
                    pj = jj - stw3.nbf
                    if pj < 0:
                        fw = (s_omm, omm_i0)
                    elif pj % 4 == 3:
                        fw = (s_omm, omm_i0 + pj // 4 + 1)
                    else:
                        fw = (s_osub, osub_i0 + (pj // 4) * 3 + pj % 4 + 1)
                    stw3.cast(gp, "gp", free_wait=fw)
        block.gpsimd(if_casts)

        for nt in range(NT):
            def if_pe(tensor, nt=nt):
                if nt == 0:
                    ctx.wait(tensor, "pe", s_dve, ln1_end)
                    ctx.wait(tensor, "pe", s_oad, oad_i0)
                    ctx.wait(tensor, "pe", s_brbe, brbe_end_a)
                for ic in range(32):
                    gi = s_amm.n - amm_i0
                    j = iw1_j0 + nt * 32 + ic
                    stw12.wait_ready(tensor, "pe", j)
                    bank = pA if gi % 2 == 0 else pB
                    ctx.wait(tensor, "pe", s_bev, bev_i0 + gi - 1)
                    for c in range(DC):
                        ins = tensor.matmul(
                            bank[:, :], stw12.tile(j, c * 128, 128),
                            xh[:, c * NTOK + nt * T: c * NTOK + (nt + 1) * T],
                            start=(c == 0), stop=(c == DC - 1))
                    s_amm.inc(ins)
                ctx.wait(tensor, "pe", s_hm, hm_i0 + (nt + 1) * 32)
                for dc in range(DC):
                    oi = s_omm.n - omm_i0
                    bank = pE if oi % 2 == 0 else pF
                    ctx.wait(tensor, "pe", s_oad, oad_i0 + oi - 1)
                    for ic in range(32):
                        j = iw2_j0 + nt * 32 + dc * 4 + ic // 8
                        stw3.wait_ready(tensor, "pe", j)
                        ins = tensor.matmul(
                            bank[:, :], stw3.tile(j, (ic % 8) * 128, 128),
                            hgrp[:, ic * T: (ic + 1) * T],
                            start=(ic == 0), stop=(ic == 31))
                        if ic % 8 == 7 and ic < 31:
                            s_osub.inc(ins)
                    s_omm.inc(ins)
            block.tensor(if_pe)

            def if_ve(vector, nt=nt):
                if nt == 0:
                    ctx.wait(vector, "dve", s_omm, omm_i0)
                for ic in range(32):
                    bi = s_bev.n - bev_i0
                    bank = pA if bi % 2 == 0 else pB
                    ctx.wait(vector, "dve", s_amm, amm_i0 + bi + 1)
                    if nt >= 1 and ic == 0:
                        ctx.wait(vector, "dve", s_omm, omm_i0 + nt * 8)
                    ctx.wait(vector, "dve", s_hm, hm_i0 + bi)
                    vector.tensor_scalar(out=tf32a[:, :], in0=bank[:, :],
                                         scalar1=b1c[:, ic:ic + 1], scalar2=0.0,
                                         op0=ALU.add, op1=ALU.max)
                    ins = vector.tensor_scalar(out=tf32b[:, :],
                                               in0=bank[:, :],
                                               scalar1=b1c[:, ic:ic + 1],
                                               scalar2=0.0,
                                               op0=ALU.add, op1=ALU.min)
                    s_bev.inc(ins)
                    vector.wait_ge(s_bev.h, s_bev.n)
                    hi = s_hm.n - hm_i0
                    ctx.wait(vector, "dve", s_gelu, gelu_i0 + hi + 1)
                    ins = vector.scalar_tensor_tensor(
                        out=hgrp[:, ic * T:(ic + 1) * T],
                        in0=va_bf[:, :].bitcast(F32)[:, 0:512], scalar=-1.0,
                        in1=tf32a[:, :], op0=ALU.add, op1=ALU.add)
                    s_hm.inc(ins)
                for dc in range(DC):
                    oi = s_oad.n - oad_i0
                    bank = pE if oi % 2 == 0 else pF
                    ctx.wait(vector, "dve", s_omm, omm_i0 + oi + 1)
                    xsl = xh[:, dc * NTOK + nt * T: dc * NTOK + (nt + 1) * T]
                    ins = vector.scalar_tensor_tensor(
                        out=xsl, in0=bank[:, :], scalar=b2c[:, dc:dc + 1],
                        in1=xsl, op0=ALU.add, op1=ALU.add)
                    s_oad.inc(ins)
            block.vector(if_ve)

            def if_sc(scalar, nt=nt):
                for ic in range(32):
                    gi = s_gelu.n - gelu_i0
                    ctx.wait(scalar, "act", s_bev, bev_i0 + gi + 1)
                    ctx.wait(scalar, "act", s_hm, hm_i0 + gi - 1 + 1)
                    ins = scalar.activation(
                        out=va_bf[:, :].bitcast(F32)[:, 0:512],
                        in_=tf32b[:, :], func=AF.Exp, bias=0.0, scale=1.0)
                    s_gelu.inc(ins)
            block.scalar(if_sc)

        ln2_end = emit_ln(xh, xh, EPS_LN_IN, s_oad, s_oad.n,
                          g_cols=g2c, b_cols=be2c, add_to=x_sb)
        lst.close()
        return ln2_end

    def emit_attnb(ln_gate_dve, moe_ends):
        stw12 = moe_res["w12"]
        stw3 = moe_res["w3"]
        attn_fm = moe_res["hgrp"]

        lst = ExitStack()
        C2b = ctx.sbuf("C2b", [128, NTOK], BF16, stack=lst)
        S2b = ctx.sbuf("S2b", [128, NTOK], BF16, stack=lst)
        qb = ctx.sbuf("qb", [128, NTOK], BF16, stack=lst)
        kb = ctx.sbuf("kb", [128, NTOK], BF16, stack=lst)
        qsw = ctx.sbuf("qsw", [128, NTOK // 2], BF16, stack=lst)
        vb32 = ctx.sbuf("vb32", [128, NTOK], F32, stack=lst)
        v_tok = ctx.sbuf("v_tok", [128, 16 * 65], BF16, stack=lst)
        probsT = iotile[:, :].bitcast(BF16)
        rb_s = ctx.sbuf("rb_s", [128, T], BF16, stack=lst)
        av_s = ctx.sbuf("av_s", [128, T], BF16, stack=lst)
        bout_c = ctx.sbuf("bout_c", [128, DC], F32, stack=lst)
        colsc = ctx.sbuf("colsc", [128, 4], F32, stack=lst)
        colsi = ctx.sbuf("colsi", [128, 1], mybir.dt.int32, stack=lst)
        tb32 = iotile
        ii32 = vb32[:, :].bitcast(mybir.dt.int32)

        s_bp = ctx.sem("s_bp")
        s_bqe = ctx.sem("s_bqe")
        s_bsc = ctx.sem("s_bsc")
        s_bxp = ctx.sem("s_bxp")
        s_bav = ctx.sem("s_bav")
        s_brb = ctx.sem("s_brb")
        s_bvt = ctx.sem("s_bvt")
        s_bvk = ctx.sem("s_bvk")
        s_brc = ctx.sem("s_brc")
        s_brbe = ctx.sem("s_brbe")
        s_bat = ctx.sem("s_bat")
        s_bas = ctx.sem("s_bas")
        s_rope = ctx.sem("s_rope")
        s_swap = ctx.sem("s_swap")
        s_bo = ctx.sem("s_bo")

        bp0 = s_bp.n
        bqe0 = s_bqe.n
        bsc0 = s_bsc.n
        bxp0 = s_bxp.n
        bav0 = s_bav.n
        brb0 = s_brb.n
        bvt0 = s_bvt.n
        bvk0 = s_bvk.n
        brc0 = s_brc.n
        brbe0 = s_brbe.n
        bat0 = s_bat.n
        bas0 = s_bas.n
        rope0 = s_rope.n
        TWO_PI = 2.0 * math.pi

        INV2PI = 1.0 / (2.0 * math.pi)

        def tbl_gp(gp):
            for q4 in range(4):
                gp.iota(colsi[q4 * 32:(q4 + 1) * 32, :], pattern=[[0, 1]],
                        base=0, channel_multiplier=1)
            ins = gp.iota(vb32[:, :].bitcast(mybir.dt.int32),
                          pattern=[[1, NTOK]], base=0, channel_multiplier=0)
            s_gp.inc(ins)
        block.gpsimd(tbl_gp)
        gp_tbl = s_gp.n

        def tbl_ve1(vector):
            ctx.wait(vector, "dve", s_gp, gp_tbl)
            vector.tensor_copy(out=vb32[:, :], in_=vb32[:, :].bitcast(mybir.dt.int32))
            ins = vector.tensor_copy(out=colsc[:, 0:1], in_=colsi[:, :])
            s_dve.inc(ins)
        block.vector(tbl_ve1)
        dve_tbl1 = s_dve.n

        def tbl_sc1(scalar):
            ctx.wait(scalar, "act", s_dve, dve_tbl1)
            ins = scalar.activation(out=colsc[:, 2:3], in_=colsc[:, 0:1],
                                    func=AF.Exp, bias=0.0,
                                    scale=-math.log(10000.0) / 32.0)
            s_act.inc(ins)
        block.scalar(tbl_sc1)
        act_tbl1 = s_act.n

        HS = NTOK // 2
        qbi = qb[:, :].bitcast(mybir.dt.int32)
        kbf = kb[:, :].bitcast(F32)

        def tbl_ve2(vector):
            ctx.wait(vector, "dve", s_act, act_tbl1)
            ins = vector.tensor_scalar_mul(vb32[:, :], vb32[:, :],
                                           colsc[:, 2:3])
            oedge(vector, s_odv, ins)
            for ch in range(2):
                c0 = ch * HS
                vsl = vb32[:, c0:c0 + HS]
                tsl = tb32[:, c0:c0 + HS]
                ins = vector.tensor_scalar(out=tsl, in0=vsl, scalar1=INV2PI,
                                           scalar2=0.25, op0=ALU.mult,
                                           op1=ALU.add)
                oedge(vector, s_odv, ins)
                ins = vector.tensor_scalar_mul(vsl, vsl, INV2PI)
                oedge(vector, s_odv, ins)
                ins = vector.tensor_copy(out=qbi[:, 0:HS], in_=tsl)
                oedge(vector, s_odv, ins)
                ins = vector.tensor_copy(out=kbf[:, 0:HS], in_=qbi[:, 0:HS])
                oedge(vector, s_odv, ins)
                ins = vector.tensor_sub(tsl, tsl, kbf[:, 0:HS])
                oedge(vector, s_odv, ins)
                ins = vector.tensor_copy(out=qbi[:, 0:HS], in_=vsl)
                oedge(vector, s_odv, ins)
                ins = vector.tensor_scalar_add(tsl, tsl, -0.5)
                oedge(vector, s_odv, ins)
                ins = vector.tensor_copy(out=kbf[:, 0:HS], in_=qbi[:, 0:HS])
                oedge(vector, s_odv, ins)
                ins = vector.tensor_sub(vsl, vsl, kbf[:, 0:HS])
                oedge(vector, s_odv, ins)
                ins = vector.tensor_scalar_add(vsl, vsl, -0.5)
                if ch == 0:
                    oedge(vector, s_odv, ins)
                else:
                    s_dve.inc(ins)
        block.vector(tbl_ve2)
        dve_tbl2 = s_dve.n

        def tbl_sc2(scalar):
            ctx.wait(scalar, "act", s_dve, dve_tbl2)
            scalar.activation(out=C2b[:, :], in_=tb32[:, 0:NTOK], func=AF.Sin,
                              bias=0.0, scale=-2.0 * math.pi)
            ins = scalar.activation(out=S2b[:, :], in_=vb32[:, :], func=AF.Sin,
                                    bias=0.0, scale=-2.0 * math.pi)
            s_act.inc(ins)
        block.scalar(tbl_sc2)
        act_tbl2 = s_act.n

        def tbl_ve3(vector):
            ctx.wait(vector, "dve", s_act, act_tbl2)
            vector.tensor_scalar_mul(S2b[0:32, :], S2b[0:32, :], -1.0)
            ins = vector.tensor_scalar_mul(S2b[64:96, :], S2b[64:96, :], -1.0)
            s_dve.inc(ins)
        block.vector(tbl_ve3)
        dve_tbl3 = s_dve.n

        qkv_j0 = stw12.n_ld
        wo_j0 = stw3.n_ld
        qkv_ct0 = stw12.n_ct
        wo_ct0 = stw3.n_ct

        def b_loads(sync):
            for c in range(DC):
                ins = sync.dma_start(
                    out=bout_c[:, c:c + 1],
                    in_=W["b_out"][ts(c, 128)].rearrange("(p one) -> p one", one=1))
                s_bo.inc(ins, 16)
            for hp in range(8):
                for w in range(3):
                    cols = w * D + hp * 128
                    stw12.load(sync, "sync", W["w_qkv"][:, cols: cols + 128])
            for dc in range(DC):
                stw3.load(sync, "sync", W["w_out"][:, ts(dc, 128)])
        block.sync(b_loads)

        def qkv_cast(gp):
            jj = stw12.n_ct - qkv_ct0
            if jj - stw12.nbf < 0:
                fw = (s_bmm, moe_ends["bmm"])
            else:
                pj = jj - stw12.nbf
                php, pw = divmod(pj, 3)
                fw = (s_bp, bp0 + php * 12 + pw * 4 + 4)
            stw12.cast(gp, "gp", free_wait=fw)

        def b_casts0(gp):
            qkv_cast(gp)
            qkv_cast(gp)
        block.gpsimd(b_casts0)

        for hp in range(8):
            def pe_pair(tensor, hp=hp):
                if hp == 0:
                    ctx.wait(tensor, "pe", s_dve, ln_gate_dve)
                    ctx.wait(tensor, "pe", s_gelu, moe_ends["gelu"])
                    ctx.wait(tensor, "pe", s_bev, moe_ends["bev"])
                    ctx.wait(tensor, "pe", s_oad, moe_ends["oad"])
                    ctx.wait(tensor, "pe", s_mb, moe_ends["mb"])
                for w in range(3):
                    j = qkv_j0 + hp * 3 + w
                    stw12.wait_ready(tensor, "pe", j)
                    for nt in range(NT):
                        pr = s_bp.n - bp0
                        bank = pA if pr % 2 == 0 else pB
                        ctx.wait(tensor, "pe", s_bqe, bqe0 + pr - 1)
                        for c in range(DC):
                            ins = tensor.matmul(
                                bank[:, :], stw12.tile(j, c * 128, 128),
                                xh[:, c * NTOK + nt * T: c * NTOK + (nt + 1) * T],
                                start=(c == 0), stop=(c == DC - 1))
                        s_bp.inc(ins)
                ctx.wait(tensor, "pe", s_rope, rope0 + hp * 4 + 4)
                ctx.wait(tensor, "pe", s_bqe, bqe0 + hp * 12 + 12)
                for h in range(2):
                    r0 = h * 64
                    dr0 = 64 - r0
                    segb = bvk0 + (hp * 2 + h) * 5
                    seg_brc = brc0 + hp * 8 + h * 4
                    ctx.wait(tensor, "pe", s_brbe, brbe0 + hp * 8 + h * 4)
                    ctx.wait(tensor, "pe", s_brc, seg_brc)
                    ctx.wait(tensor, "pe", s_bas, bas0 + hp * 8 + h * 4)
                    for g4 in range(4):
                        bank = pG if g4 % 2 == 0 else pF
                        if g4 >= 2:
                            ctx.wait(tensor, "pe", s_bvk, segb + g4)
                        for c in range(4):
                            ck = g4 * 4 + c
                            ins = tensor.transpose(
                                bank[0:128, ts(c, 128)],
                                vb32[:, ts(ck, 128)], id_f32[:, :])
                        s_bvt.inc(tr_flush(tensor) or ins)
                    ctx.wait(tensor, "pe", s_bvk, segb + 5)
                    for qt in range(NT):
                        hq = s_bav.n - bav0
                        avbank = pE if hq % 2 == 0 else pF
                        ctx.wait(tensor, "pe", s_brc, brc0 + hq - 1)
                        ctx.wait(tensor, "pe", s_bas, bas0 + hq - 1)
                        ctx.wait(tensor, "pe", s_brbe, brbe0 + hq)
                        sc_base = s_bsc.n - bsc0

                        def av_mm(tensor, ck, first, last):
                            tensor.matmul(
                                pG[0:1, :],
                                v_tok[:, ck * 65 + 64: ck * 65 + 65],
                                probsT[:, ts((sc_base + ck) % 3, T)],
                                start=first, stop=last)
                            return tensor.matmul(
                                avbank[r0:r0 + 64, :],
                                v_tok[:, ck * 65: ck * 65 + 64],
                                probsT[:, ts((sc_base + ck) % 3, T)],
                                start=first, stop=last,
                                tile_position=(0, r0))

                        for ck in range(16):
                            sbank = (pC, pD, pI)[(sc_base + ck) % 3]
                            ctx.wait(tensor, "pe", s_bxp,
                                     bxp0 + sc_base + ck - 2)
                            ins = tensor.matmul(
                                sbank[:, :],
                                kb[r0:r0 + 64, ts(ck, 128)],
                                qb[r0:r0 + 64, ts(qt, T)],
                                start=True, stop=True)
                            s_bsc.inc(ins)
                            if ck >= 2:
                                ctx.wait(tensor, "pe", s_bxp,
                                         bxp0 + sc_base + ck - 1)
                                av_mm(tensor, ck - 2, ck - 2 == 0, False)
                        ctx.wait(tensor, "pe", s_bxp, bxp0 + sc_base + 15)
                        av_mm(tensor, 14, False, False)
                        ctx.wait(tensor, "pe", s_bxp, bxp0 + sc_base + 16)
                        ins = av_mm(tensor, 15, False, True)
                        s_bav.inc(ins)
                        ctx.wait(tensor, "pe", s_brc, brc0 + hq + 1)
                        ctx.wait(tensor, "pe", s_brbe, brbe0 + hq)
                        if h == 0:
                            ins = tensor.matmul(
                                pG[0:64, :], ones_f32_r[0:1, 0:64],
                                row_c[0:1, :], start=True, stop=True)
                        else:
                            ins = tensor.matmul(
                                pG[64:128, :], ones_f32_r[0:1, 0:64],
                                row_c[0:1, :], start=True, stop=True,
                                tile_position=(0, 64))
                        s_brb.inc(ins)
            block.tensor(pe_pair)

            def sc_pair(scalar, hp=hp):
                for w in range(3):
                    for nt in range(NT):
                        pr = s_bqe.n - bqe0
                        bank = pA if pr % 2 == 0 else pB
                        ctx.wait(scalar, "act", s_bp, bp0 + pr + 1)
                        if w == 0:
                            if hp == 0:
                                ctx.wait(scalar, "act", s_dve, dve_tbl2)
                            ctx.wait(scalar, "act", s_bsc, bsc0 + hp * 128)
                            ctx.wait(scalar, "act", s_rope, rope0 + hp * 4)
                            ins = scalar.activation(
                                out=qb[:, ts(nt, T)], in_=bank[:, :],
                                func=AF.Copy, bias=0.0, scale=SCALE_B)
                        elif w == 1:
                            ins = scalar.copy(out=kb[:, ts(nt, T)],
                                              in_=bank[:, :])
                        else:
                            ctx.wait(scalar, "act", s_bvt, bvt0 + hp * 8)
                            ins = scalar.copy(out=vb32[:, ts(nt, T)],
                                              in_=bank[:, :])
                        s_bqe.inc(ins)
                for h in range(2):
                    r0 = h * 64
                    for qt in range(NT):
                        hq = s_bas.n - bas0
                        for ck in range(16):
                            xp = s_bxp.n - bxp0
                            sbank = (pC, pD, pI)[xp % 3]
                            ctx.wait(scalar, "act", s_bsc, bsc0 + xp + 1)
                            ins = scalar.activation(
                                out=probsT[:, ts(xp % 3, T)], in_=sbank[:, :],
                                func=AF.Exp, bias=0.0, scale=1.0)
                            s_bxp.inc(ins)
                        avbank = pE if hq % 2 == 0 else pF
                        ctx.wait(scalar, "act", s_bav, bav0 + hq + 1)
                        ctx.wait(scalar, "act", s_bat, bat0 + hq)
                        ins = scalar.copy(out=av_s[r0:r0 + 64, :],
                                          in_=avbank[r0:r0 + 64, :])
                        s_bas.inc(ins)
            block.scalar(sc_pair)

            def ve_pair(vector, hp=hp):
                if hp == 0:
                    ctx.wait(vector, "dve", s_omm, moe_ends["omm"])
                for h in range(2):
                    r0 = h * 64
                    dr0 = 64 - r0
                    ctx.wait(vector, "dve", s_bav, bav0 + (hp * 2 + h) * 4)
                    ins = vector.memset(v_tok[:, :], 1.0)
                    s_bvk.inc(ins)
                    vector.wait_ge(s_bvk.h, s_bvk.n)
                    for g4 in range(4):
                        bank = pG if g4 % 2 == 0 else pF
                        ctx.wait(vector, "dve", s_bvt,
                                 bvt0 + (hp * 2 + h) * 4 + g4 + 1)
                        for c in range(4):
                            ck = g4 * 4 + c
                            ins = vector.tensor_copy(
                                out=v_tok[:, ck * 65: ck * 65 + 64],
                                in_=bank[0:128, c * 128 + r0:
                                         c * 128 + r0 + 64])
                        s_bvk.inc(ins)
                    for qt in range(NT):
                        hq = s_brc.n - brc0
                        avbank = pE if hq % 2 == 0 else pF
                        ctx.wait(vector, "dve", s_bav, bav0 + hq + 1)
                        ins = vector.reciprocal(row_c[0:1, :],
                                                pG[0:1, :])
                        s_brc.inc(ins)
                        vector.tensor_copy(out=junk[:, 0:4], in_=id_f32[:, 0:4])
                        ctx.wait(vector, "dve", s_brb, brb0 + hq + 1)
                        ins = vector.tensor_copy(out=rb_s[r0:r0 + 64, :],
                                                 in_=pG[r0:r0 + 64, :])
                        s_brbe.inc(ins)
                        vector.wait_ge(s_brbe.h, s_brbe.n)
                        ctx.wait(vector, "dve", s_bas, bas0 + hq + 1)
                        ins = vector.tensor_mul(
                            attn_fm[r0:r0 + 64, hp * NTOK + qt * T:
                                    hp * NTOK + (qt + 1) * T],
                            av_s[r0:r0 + 64, :], rb_s[r0:r0 + 64, :])
                        s_bat.inc(ins)
            block.vector(ve_pair)

            def gp_pair(gp, hp=hp):
                if hp == 0:
                    ctx.wait(gp, "gp", s_dve, dve_tbl3)
                    ctx.wait(gp, "gp", s_act, act_tbl2)
                for which, buf in (("q", qb), ("k", kb)):
                    ctx.wait(gp, "gp", s_bqe,
                             bqe0 + hp * 12 + (4 if which == "q" else 8))
                    for ch in range(2):
                        c0 = ch * HS
                        ri = s_rope.n - rope0
                        if ri >= 1:
                            ctx.wait(gp, "gp", s_rope, rope0 + ri)
                        for (dst0, src0) in ((0, 32), (32, 0), (64, 96), (96, 64)):
                            ins = gp.dma_start(
                                out=qsw[dst0:dst0 + 32, :],
                                in_=buf[src0:src0 + 32, c0:c0 + HS])
                            s_swap.inc(ins, 16)
                        gp.wait_ge(s_swap.h, s_swap.n)
                        for half in range(2):
                            r = half * 64
                            gp.tensor_mul(qsw[r:r + 64, :], qsw[r:r + 64, :],
                                          S2b[r:r + 64, c0:c0 + HS])
                        for half in range(2):
                            r = half * 64
                            ins = gp.tensor_mul(buf[r:r + 64, c0:c0 + HS],
                                                buf[r:r + 64, c0:c0 + HS],
                                                C2b[r:r + 64, c0:c0 + HS])
                        oedge(gp, s_ogp, ins)
                        for half in range(2):
                            r = half * 64
                            ins = gp.tensor_add(buf[r:r + 64, c0:c0 + HS],
                                                buf[r:r + 64, c0:c0 + HS],
                                                qsw[r:r + 64, :])
                        s_rope.inc(ins)
                qkv_cast(gp)
                if hp < 7:
                    qkv_cast(gp)
                    qkv_cast(gp)
                if hp == 7:
                    for dc in range(DC):
                        jj = stw3.n_ct - wo_ct0
                        if jj - stw3.nbf < 0:
                            fw = (s_omm, moe_ends["omm"])
                        else:
                            fw = (s_omm,
                                  moe_ends["omm"] + (jj - stw3.nbf) * 4 + 4)
                        stw3.cast(gp, "gp", free_wait=fw)
            block.gpsimd(gp_pair)

        omm_p4_0 = s_omm.n
        bat_end = s_bat.n
        oad_p4_0 = s_oad.n
        bqe_end = s_bqe.n

        def p4_pe(tensor):
            ctx.wait(tensor, "pe", s_bat, bat_end)
            ctx.wait(tensor, "pe", s_bqe, bqe_end)
            for dc in range(DC):
                j = wo_j0 + dc
                stw3.wait_ready(tensor, "pe", j)
                for nt in range(NT):
                    oidx = s_omm.n - omm_p4_0
                    bank = pA if oidx % 2 == 0 else pB
                    ctx.wait(tensor, "pe", s_oad, oad_p4_0 + oidx - 1)
                    for c in range(DC):
                        ins = tensor.matmul(
                            bank[:, :], stw3.tile(j, c * 128, 128),
                            attn_fm[:, c * NTOK + nt * T:
                                    c * NTOK + (nt + 1) * T],
                            start=(c == 0), stop=(c == DC - 1))
                    s_omm.inc(ins)
        block.tensor(p4_pe)

        def p4_ve(vector):
            ctx.wait(vector, "dve", s_bo, s_bo.n)
            for dc in range(DC):
                for nt in range(NT):
                    oidx = s_oad.n - oad_p4_0
                    bank = pA if oidx % 2 == 0 else pB
                    ctx.wait(vector, "dve", s_omm, omm_p4_0 + oidx + 1)
                    xsl = x_sb[:, dc * NTOK + nt * T: dc * NTOK + (nt + 1) * T]
                    ins = vector.scalar_tensor_tensor(
                        out=xsl, in0=bank[:, :], scalar=bout_c[:, dc: dc + 1],
                        in1=xsl, op0=ALU.add, op1=ALU.add)
                    s_oad.inc(ins)
        block.vector(p4_ve)

        lst.close()
        return s_oad.n

    def emit_store(gate_sem, gate_val):
        pe_st0 = s_pe.n
        dve_st0 = s_dve.n

        def out_tr(tensor):
            ctx.wait(tensor, "pe", gate_sem, gate_val)
            for t in range(TT16):
                for c4 in range(2):
                    g = t * 2 + c4
                    bank = pG if g % 2 == 0 else pF
                    ctx.wait(tensor, "pe", s_dve, dve_st0 + g - 1)
                    for c in range(c4 * 4, c4 * 4 + 4):
                        ins = tensor.transpose(
                            bank[0:128, ts(c % 4, 128)],
                            x_sb[:, c * NTOK + t * 128: c * NTOK + (t + 1) * 128],
                            id_f32[:, :])
                    s_pe.inc(tr_flush(tensor) or ins)
        block.tensor(out_tr)

        def out_ev(vector):
            for t in range(TT16):
                for c4 in range(2):
                    g = t * 2 + c4
                    bank = pG if g % 2 == 0 else pF
                    ctx.wait(vector, "dve", s_pe, pe_st0 + g + 1)
                    if t >= 2 and c4 == 0:
                        ctx.wait(vector, "dve", s_st0 if t % 2 == 0 else s_st1,
                                 16 * ((t - 2) // 2 + 1))
                    for c in range(c4 * 4, c4 * 4 + 4):
                        ins = vector.tensor_copy(
                            out=iotile[:, (t % 2) * D + c * 128: (t % 2) * D + (c + 1) * 128],
                            in_=bank[0:128, ts(c % 4, 128)])
                    s_dve.inc(ins)
        block.vector(out_ev)

        def out_dma(sync):
            for t in range(TT16):
                ctx.wait(sync, "sync", s_dve, dve_st0 + (t + 1) * 2)
                ins = sync.dma_start(out=y_out[ts(t, 128), :],
                                     in_=iotile[:, ts(t % 2, D)])
                (s_st0 if t % 2 == 0 else s_st1).inc(ins, 16)
            sync.wait_ge(s_st0.h, s_st0.n)
            sync.wait_ge(s_st1.h, s_st1.n)
        block.sync(out_dma)

    def emit_store_raw(gate_sem, gate_val):
        def raw_dma(sync):
            ctx.wait(sync, "sync", gate_sem, gate_val)
            ins = sync.dma_start(
                out=y_out.rearrange("(a p) d -> p a d", p=128),
                in_=x_sb[:, :].rearrange("p (a d) -> p a d", d=1024))
            s_out.inc(ins, 16)
            sync.wait_ge(s_out.h, s_out.n)
        block.sync(raw_dma)

    if upto == "copy":
        emit_store(s_dve, X_READY_DVE)
        return nc, stack

    if upto == "ln":
        ln_end = emit_ln(x_sb, xh, EPS_LN_OUT, s_dve, X_READY_DVE)

        def lncp(vector):
            ctx.wait(vector, "dve", s_dve, ln_end)
            for c in range(DC):
                ins = vector.tensor_copy(out=x_sb[:, ts(c, NTOK)],
                                         in_=xh[:, ts(c, NTOK)])
                s_dve.inc(ins)
        block.vector(lncp)
        emit_store(s_dve, s_dve.n)
        return nc, stack

    if upto == "attnb":
        ln_end = emit_ln(x_sb, xh, EPS_LN_OUT, s_dve, X_READY_DVE)
        oad_end = emit_moe(ln_end, elist=MOE_ELIST)
        moe_ends = {"gelu": s_gelu.n, "bev": s_bev.n, "oad": s_oad.n,
                    "mb": s_mb.n, "omm": s_omm.n, "bmm": s_bmm.n,
                    "amm": s_amm.n}
        ln2_end = emit_ln(x_sb, xh, EPS_LN_OUT, s_oad, oad_end)
        oad_end2 = emit_attnb(ln2_end, moe_ends)
        emit_store(s_oad, oad_end2)
        return nc, stack

    if upto == "attna_only":
        if NOLN:
            def rawln(vector):
                ctx.wait(vector, "dve", s_dve, X_READY_DVE)
                for c in range(DC):
                    ins = vector.tensor_copy(out=xh[:, ts(c, NTOK)],
                                             in_=x_sb[:, ts(c, NTOK)])
                    s_dve.inc(ins)
            block.vector(rawln)
            ln_end = s_dve.n
        else:
            ln_end = emit_ln(x_sb, xh, EPS_LN_OUT, s_dve, X_READY_DVE)
        a_end = emit_attna(ln_end)
        if RAWSTORE:
            emit_store_raw(s_dve, a_end)
        else:
            emit_store(s_dve, a_end)
        return nc, stack

    if upto == "full":
        ln_end = emit_ln(x_sb, xh, EPS_LN_OUT, s_dve, X_READY_DVE)
        a_end = emit_attna(ln_end)
        pw12 = (s_amm, s_amm.n)
        pw3 = (s_omm, s_omm.n)
        ln2_end = emit_ln(x_sb, xh, EPS_LN_OUT, s_dve, a_end)
        oad1 = emit_moe(ln2_end, elist=MOE_ELIST, prev_w12=pw12, prev_w3=pw3)
        moe_ends = {"gelu": s_gelu.n, "bev": s_bev.n, "oad": s_oad.n,
                    "mb": s_mb.n, "omm": s_omm.n, "bmm": s_bmm.n,
                    "amm": s_amm.n}
        ln3_end = emit_ln(x_sb, xh, EPS_LN_OUT, s_oad, oad1)
        oad2 = emit_attnb(ln3_end, moe_ends)
        s_bp_h = ctx.sem("s_bp")
        pw12b = (s_bp_h, s_bp_h.n)
        pw3b = (s_omm, s_omm.n)
        ln4_end = emit_ln(x_sb, xh, EPS_LN_OUT, s_oad, oad2)
        oad3 = emit_moe(ln4_end, elist=MOE_ELIST, prev_w12=pw12b, prev_w3=pw3b)
        emit_store(s_oad, oad3)
        return nc, stack

    if upto == "ffn1":
        ln_end = emit_ln(x_sb, xh, EPS_LN_OUT, s_dve, X_READY_DVE)
        oad_end = emit_moe(ln_end, elist=MOE_ELIST)
        if DBG:
            dbgx = nc.dram_tensor("dbgx", [128, DC * NTOK], F32,
                                  kind="ExternalOutput").ap()
            dbgh = nc.dram_tensor("dbgh", [128, DC * NTOK], BF16,
                                  kind="ExternalOutput").ap()
            dbgm = nc.dram_tensor("dbgm", [128, TT16 * NEXP], F32,
                                  kind="ExternalOutput").ap()

            s_dbg = ctx.sem("s_dbg")

            def dbg_dma(sync):
                ctx.wait(sync, "sync", s_oad, oad_end)
                sync.dma_start(out=dbgx, in_=x_sb[:, :]).then_inc(s_dbg.h, 16)
                sync.dma_start(out=dbgh, in_=xh[:, :]).then_inc(s_dbg.h, 16)
                ins = sync.dma_start(out=dbgm, in_=xh[:, 0:TT16 * NEXP])
                s_dbg.inc(ins, 32)
                s_dbg.n += 16
                sync.wait_ge(s_dbg.h, 48)
            block.sync(dbg_dma)
        emit_store(s_oad, oad_end)
        return nc, stack

    raise NotImplementedError(upto)


def kernel(**inputs):
    x = np.ascontiguousarray(inputs["L0_attn_a_x"])
    nc, stack = build_nc("full")
    weights = {k: np.ascontiguousarray(inputs[k])
               for k in INPUTS_FOR["full"]}
    in_maps = [dict(weights, x=x[b]) for b in range(8)]
    res = run_bass_kernel_spmd(nc, in_maps, core_ids=list(range(8)))
    stack.close()
    return np.stack([res.results[b]["y"] for b in range(8)], axis=0)
